# version 06
# baseline (speedup 1.0000x reference)
import sys
sys.path.insert(0, "/opt/trn_rl_repo")
import numpy as np
import concourse.bass as bass
import concourse.mybir as mybir
from concourse.bass_utils import run_bass_kernel_spmd
from concourse.aot_env import allow_ici_all_to_all
from contextlib import ExitStack

F32 = mybir.dt.float32
BF16 = mybir.dt.bfloat16
AF = mybir.ActivationFunctionType
ALU = mybir.AluOpType
AXX = mybir.AxisListType.X

B = 2
N = 8192
T = 16384
D = 1024
NCORE = 8
WIN = 2048
OT = 512
NOT = T // OT
NI = N // OT
NJ = N // 128
NT3 = WIN // 128
EPS = 1e-5
STAGE = 3
P3SUB = 5
P3OPS = 63
LOOSE = False
VB_MODE = 0
SYB_MODE = 0


class Marks(dict):
    strict = False

    def __missing__(self, key):
        if Marks.strict and not LOOSE:
            raise KeyError(key)
        return 0


class _DummyInst:
    def then_inc(self, *a, **k):
        return self


class _DummyEng:
    def __getattr__(self, name):
        return lambda *a, **k: _DummyInst()


_PASSTHRU = {"wait_ge", "dma_start", "collective_compute", "nop", "drain",
             "indirect_dma_start", "sem_inc"}


class EngProxy:

    def __init__(self, real, ctr, em):
        self.raw = real
        self._ctr = ctr
        self._em = em

    def __getattr__(self, name):
        if name in _PASSTHRU:
            return getattr(self.raw, name)
        real_fn = getattr(self.raw, name)
        def call(*a, **k):
            c = self._ctr
            if self._em and c.n - (c.lag - 1) * c.step > 0:
                self.raw.wait_ge(c.sem, c.n - (c.lag - 1) * c.step)
            inst = real_fn(*a, **k)
            c.n += c.step
            if self._em and c.sem is not None:
                inst.then_inc(c.sem, c.step)
            return inst
        return call


class Ctr:
    def __init__(self, name, step):
        self.name = name
        self.step = step
        self.n = 0
        self.sem = None
        self.lag = 1

    def inc(self, inst):
        self.n += self.step
        inst.then_inc(self.sem, self.step) if self.sem is not None else None
        return inst


class Env:

    def __init__(self, nc, marks):
        self.nc = nc
        self.M = marks
        self.em = False
        self.t = self.sy = _DummyEng()
        self.v = self.s = self.g = None
        self.tt = Ctr("tt", 1)
        self.vv = Ctr("vv", 1)
        self.ss = Ctr("ss", 1)
        self.gc = Ctr("gc", 1)
        self.cc = Ctr("cc", 1)
        self.syn = Ctr("syn", 16)
        self.gd = Ctr("gd", 16)
        self.st = Ctr("st", 16)
        slot_names = ["gw", "gx0", "gx1", "gt0", "gt1", "gw3", "ghg",
                      "sg0", "sg1", "skk", "sqt0", "sqt1", "sos0", "sos1",
                      "sot0", "sot1", "sxw0", "sxw1", "shgi", "sy0", "sy1"]
        self.slots = {n: Ctr(n, 16) for n in slot_names}
        self.ctrs = [self.tt, self.vv, self.ss, self.gc, self.cc, self.syn, self.gd, self.st] + \
            [self.slots[n] for n in slot_names]

    def use_engines(self, t=None, v=None, s=None, g=None, sy=None):
        if t is not None:
            self.t = t
        if sy is not None:
            self.sy = sy
        if v is not None:
            self.v = EngProxy(v, self.vv, self.em)
        if s is not None:
            self.s = EngProxy(s, self.ss, self.em)
        if g is not None:
            self.g = EngProxy(g, self.gc, self.em)

    def reset(self):
        for c in self.ctrs:
            c.n = 0

    def sems(self, nc, stack):
        for c in self.ctrs:
            c.sem = stack.enter_context(nc.semaphore("sem_" + c.name))

    def w(self, eng, ctr, val):
        if self.em and val > 0:
            eng.wait_ge(ctr.sem, val)


def act_raw(E, eng, out_ap, in_ap, func, bias_ap, scale=1.0):
    c = eng._ctr
    if not E.em:
        c.n += c.step
        return _DummyInst()
    nc = E.nc
    with nc.discard():
        proto = eng.raw.activation(out_ap, in_ap, AF.Exp, bias=bias_ap, scale=scale).ins
    new = mybir.InstActivation(
        name=nc.get_next_instruction_name(), engine=proto.engine,
        ins=list(proto.ins), outs=list(proto.outs), func=func, debug=proto.debug)
    if c.n - (c.lag - 1) * c.step > 0:
        eng.raw.wait_ge(c.sem, c.n - (c.lag - 1) * c.step)
    inst = eng.raw.add_instruction(new)
    c.n += c.step
    inst.then_inc(c.sem, c.step)
    return inst



def prog_sync_a(E, R):
    sy, M = E.sy, E.M
    for ot in range(NOT):
        gt = ot * OT
        for h in range(2):
            ft = ot * 2 + h
            sp = ft % 2
            sg = E.slots["sg%d" % sp]
            E.w(sy, E.vv, M[("p1stg", ot, h)])
            sg.inc(sy.dma_start(
                out=R.QD.ap()[:, h * T + gt: h * T + gt + 512]
                    .rearrange("(g p) t -> p g t", g=2),
                in_=R.stg[sp][0:64, :].rearrange("p (g t) -> p g t", g=2)))
            sg.inc(sy.dma_start(
                out=R.KK[0:64, h * T + gt: h * T + gt + 512],
                in_=R.stg[sp][64:128, 0:512]))
            sg.inc(sy.dma_start(
                out=R.KK[64:128, h * T + gt: h * T + gt + 512],
                in_=R.stg[sp][64:128, 512:1024]))
            M[("p1syn_stg", ot, h)] = (sg, sg.n)
    M["p1sg0_end"] = E.slots["sg0"].n
    M["p1sg1_end"] = E.slots["sg1"].n
    if STAGE < 2:
        M["p2syn_end"] = E.syn.n
        return
    def store_O(bi):
        b, I = bi // NI, bi % NI
        gtok = b * N + I * OT
        w_, c0 = gtok // WIN, gtok % WIN
        so = E.slots["sos%d" % (bi % 2)]
        so.inc(sy.dma_start(out=R.A2I.ap()[128 * w_:128 * w_ + 128, c0:c0 + 512],
                            in_=R.Ost[bi % 2][:, :]))
        M[("p2ost", bi)] = (so, so.n)

    for bi in range(B * NI):
        b, I = bi // NI, bi % NI
        buf = bi % 2
        if bi >= 2:
            E.w(sy, E.tt, M[("p2tte", bi - 2)])
        gt = b * N + I * OT
        sq = E.slots["sqt%d" % buf]
        sq.inc(sy.dma_start(
            out=R.Qt[buf][:, :].rearrange("p (h t) -> p h t", h=2),
            in_=R.QD.ap().rearrange("p (h n) -> p h n", h=2)[:, :, gt:gt + 512]))
        M[("p2qt", bi)] = (sq, sq.n)
        if bi >= 1:
            E.w(sy, E.vv, M[("p2evac", bi - 1)])
            store_O(bi - 1)
    E.w(sy, E.vv, M[("p2evac", B * NI - 1)])
    store_O(B * NI - 1)
    M["p2sos0_end"] = E.slots["sos0"].n
    M["p2sos1_end"] = E.slots["sos1"].n


def prog_gpsimd_a(E, R):
    g, M = E.g, E.M
    g.memset(R.zb[:, :], 0.0)
    g.memset(R.epsb[:, :], EPS)
    g.memset(R.ones[:, :], 1.0)
    M["p1memset"] = E.gc.n
    gw = E.slots["gw"]
    for src, dst, c in ((R.dwqk, R.wqk, 256), (R.dwv, R.wv, 128)):
        gw.inc(g.dma_start(out=dst[:, :].rearrange("p (c f) -> p c f", c=8),
                           in_=src.ap().rearrange("(c p) f -> p c f", c=8)))
    gw.inc(g.dma_start(out=R.bqk[:, :], in_=R.dbqk.ap()))
    gw.inc(g.dma_start(out=R.bv[:, :], in_=R.dbv.ap()))
    gw.inc(g.dma_start(out=R.permb[:, :], in_=R.dperm.ap()))
    gw.inc(g.dma_start(out=R.idnb[:, :], in_=R.didn.ap()))
    M["p1w"] = gw.n
    for ot in range(NOT):
        bsel = ot % 2
        gt = ot * OT
        pos = gt % N
        if ot >= 2:
            E.w(g, E.ss, M[("p1norm", ot - 2)])
            E.w(g, E.vv, M[("p1stat", ot - 2)])
            E.w(g, E.vv, M[("p1stg", ot - 2, 1)])
        gx = E.slots["gx%d" % bsel]
        gt_s = E.slots["gt%d" % bsel]
        gx.inc(g.dma_start(
            out=R.xts[bsel][:, :].rearrange("p (s c) -> p s c", s=4),
            in_=R.dxf.ap()[gt:gt + 512, :].rearrange("(s p) c -> p s c", s=4)))
        gt_s.inc(g.dma_start(out=R.tcs[bsel][:, :], in_=R.dtblc.ap()[:, pos:pos + 512]))
        gt_s.inc(g.dma_start(out=R.tss[bsel][:, :], in_=R.dtbls.ap()[:, pos:pos + 512]))
        M[("p1x", ot)] = (gx, gx.n)
        M[("p1tb", ot)] = (gt_s, gt_s.n)
    if STAGE < 2:
        M["a2a"] = E.cc.n
        return
    E.w(g, E.slots["sos0"], M["p2sos0_end"])
    E.w(g, E.slots["sos1"], M["p2sos1_end"])
    if E.em:
        with allow_ici_all_to_all():
            E.cc.inc(g.collective_compute(
                "AllToAll", ALU.bypass, replica_groups=[list(range(8))],
                ins=[R.A2I.ap().opt()], outs=[R.A2O.ap().opt()]))
    else:
        E.cc.inc(_DummyInst())
    M["a2a"] = E.cc.n


def prog_tensor_a(E, R):
    t, M = E.t, E.M
    E.w(t, E.slots["gw"], M["p1w"])
    E.w(t, E.gc, M["p1memset"])
    QKb = [R.S1p[0][:, :], R.S1p[1][:, :]]
    SWp = R.S2p[1][:, :]
    Vpp = R.S2p[0][:, :]
    for ot in range(NOT):
        bsel = ot % 2
        E.w(t, E.ss, M[("p1norm", ot)])
        if ot >= 1:
            pass
        for ccn in range(8):
            tp = R.TPq[ccn % 2][:, 0:512].bitcast(BF16) if False else R.TPq[ccn % 2]
            gcc = ot * 8 + ccn
            if gcc >= 2:
                ev = M[("p1tpev", gcc - 2)]
                if ev:
                    E.w(t, ev[0], ev[1])
            for sb in range(4):
                inst = t.transpose(tp[:, sb * 128:(sb + 1) * 128],
                                   R.xnb[bsel][:, sb * D + ccn * 128: sb * D + ccn * 128 + 128],
                                   R.idnb[:, :])
            E.tt.inc(inst)
            M[("p1tp", ot, ccn)] = E.tt.n
        E.w(t, E.vv, M[("p1trev_v", ot)])
        E.w(t, E.ss, M[("p1trev_s", ot)])
        for h in range(2):
            ft = ot * 2 + h
            if ft >= 2:
                E.w(t, E.vv, M[("p1stg", (ft - 2) // 2, (ft - 2) % 2)])
            for ccn in range(8):
                t.matmul(QKb[h], R.wqk[:, ccn * 256 + h * 128: ccn * 256 + h * 128 + 128],
                         R.xnT[:, ccn * OT: ccn * OT + 512], start=(ccn == 0), stop=False)
            E.tt.inc(t.matmul(QKb[h], R.bqk[:, h * 128:(h + 1) * 128], R.ones[:, 0:512],
                              start=False, stop=True))
            M[("p1qk", ot, h)] = E.tt.n
        for h in range(2):
            ft = ot * 2 + h
            sp = ft % 2
            E.w(t, E.vv, M[("p1E", ot, h)])
            if h == 1:
                E.w(t, E.vv, M[("p1ropeA", ot, 0)])
            E.tt.inc(t.matmul(SWp, R.permb[:, :], R.stg[sp][:, 512:1024],
                              start=True, stop=True))
            M[("p1sw", ot, h)] = E.tt.n
        if ot >= 1:
            E.w(t, E.vv, M[("p1vev", ot - 1)])
        for sb in range(4):
            for ccn in range(8):
                t.matmul(Vpp[:, sb * 128:(sb + 1) * 128],
                         R.xnT[:, ccn * OT + sb * 128: ccn * OT + sb * 128 + 128],
                         R.wv[:, ccn * 128:(ccn + 1) * 128], start=(ccn == 0), stop=False)
            inst = t.matmul(Vpp[:, sb * 128:(sb + 1) * 128], R.ones[:, 0:128], R.bv[:, :],
                            start=False, stop=True)
        E.tt.inc(inst)
        M[("p1v", ot)] = E.tt.n
    if STAGE < 2:
        return
    E.w(t, E.slots["sg0"], M["p1sg0_end"])
    E.w(t, E.slots["sg1"], M["p1sg1_end"])
    E.w(t, E.vv, M["p1vv_end"])
    E.w(t, E.ss, M["p1ss_end"])
    ACCap = [R.TPq[0][:, :].bitcast(F32), R.TPq[1][:, :].bitcast(F32)]
    u = 0
    for bi in range(B * NI):
        b, I = bi // NI, bi % NI
        buf = bi % 2
        qtm = M[("p2qt", bi)]
        if qtm:
            E.w(t, qtm[0], qtm[1])
        for h in range(2):
            hb = h * T + b * N
            for J in range(NJ):
                p = u % 3
                if u >= 3:
                    E.w(t, E.vv, M[("p2w", u - 3)])
                t.matmul(R.S1p[p][:, :], R.KK[0:64, hb + J * 128: hb + J * 128 + 128],
                         R.Qt[buf][0:64, h * 512:(h + 1) * 512],
                         start=True, stop=True, tile_position=(0, 0))
                E.tt.inc(t.matmul(R.S2p[p][:, :], R.KK[64:128, hb + J * 128: hb + J * 128 + 128],
                                  R.Qt[buf][64:128, h * 512:(h + 1) * 512],
                                  start=True, stop=True, tile_position=(64, 0)))
                M[("p2s", u)] = E.tt.n
                if J >= 1:
                    _mm3(E, R, ACCap, b, h, J - 1, u - 1, bi)
                u += 1
            _mm3(E, R, ACCap, b, h, NJ - 1, u - 1, bi)
        M[("p2tte", bi)] = E.tt.n
    M["p2tt_end"] = E.tt.n


def _mm3(E, R, ACCap, b, h, J, u, bi):
    t, M = E.t, E.M
    if J == 0 and bi >= 1:
        E.w(t, E.vv, M[("p2evac", bi - 1)])
    E.w(t, E.vv, M[("p2w", u)])
    vcol = (b * 64 + J) * 128 + h * 64
    if h == 0:
        inst = t.matmul(ACCap[0][0:64, :], R.Vt[:, vcol:vcol + 64], R.WT[u % 3][:, :],
                        start=(J == 0), stop=(J == NJ - 1))
    else:
        inst = t.matmul(ACCap[1][64:128, :], R.Vt[:, vcol:vcol + 64], R.WT[u % 3][:, :],
                        start=(J == 0), stop=(J == NJ - 1), tile_position=(0, 64))
    if J == NJ - 1:
        E.tt.inc(inst)


def prog_vector_a(E, R):
    v, M = E.v, E.M
    QKb = [R.S1p[0][:, :], R.S1p[1][:, :]]
    SWp = R.S2p[1][:, :]
    Vpp = R.S2p[0][:, :]
    for ot in range(NOT):
        bsel = ot % 2
        xm = M[("p1x", ot)]
        if xm:
            E.w(v, xm[0], xm[1])
        for sb in range(4):
            inst = v.reduce_sum(R.sums[:, ot * 4 + sb: ot * 4 + sb + 1],
                                R.xts[bsel][:, sb * D:(sb + 1) * D], axis=AXX)
        M[("p1stat", ot)] = E.vv.n
        E.w(v, E.ss, M[("p1sq", ot)])
        c4 = slice(ot * 4, ot * 4 + 4)
        v.tensor_scalar(R.lnt[:, 0:4], R.sums[:, c4], 1.0 / D, None, ALU.mult)
        v.tensor_scalar(R.lnt[:, 4:8], R.sumsq[:, c4], 1.0 / D, None, ALU.mult)
        v.tensor_tensor(R.lnt[:, 8:12], R.lnt[:, 0:4], R.lnt[:, 0:4], ALU.mult)
        v.tensor_tensor(R.lnt[:, 12:16], R.lnt[:, 4:8], R.lnt[:, 8:12], ALU.subtract)
        M[("p1var", ot)] = E.vv.n
        E.w(v, E.ss, M[("p1rstd", ot)])
        v.tensor_scalar(R.lnt[:, 20:24], R.lnt[:, 0:4], -1.0, None, ALU.mult)
        v.tensor_tensor(R.lnt[:, 20:24], R.lnt[:, 20:24], R.lnt[:, 16:20], ALU.mult)
        M[("p1negmr", ot)] = E.vv.n
        for ccn in range(0, 8, 2):
            E.w(v, E.tt, M[("p1tp", ot, ccn)])
            if ot >= 1 and ccn == 0:
                E.w(v, E.tt, M[("p1v", ot - 1)])
            v.tensor_copy(R.xnT[:, ccn * OT: ccn * OT + 512], R.TPq[ccn % 2][:, 0:512])
            M[("p1tpev", ot * 8 + ccn)] = (E.vv, E.vv.n)
        M[("p1trev_v", ot)] = E.vv.n
        for h in range(2):
            ft = ot * 2 + h
            sp = ft % 2
            E.w(v, E.tt, M[("p1qk", ot, h)])
            if ft >= 1:
                E.w(v, E.ss, M[("p1exp", *divmod(ft - 1, 2))])
            v.tensor_scalar_min(R.es[:, :], QKb[h], 0.0)
            M[("p1min", ot, h)] = E.vv.n
            E.w(v, E.ss, M[("p1exp", ot, h)])
            if ft >= 2:
                sgm = M[("p1syn_stg", (ft - 2) // 2, (ft - 2) % 2)]
                if sgm:
                    E.w(v, sgm[0], sgm[1])
            v.scalar_tensor_tensor(R.stg[sp][:, 512:1024], QKb[h], 0.0,
                                            R.expm[:, :], ALU.max, ALU.add)
            M[("p1E", ot, h)] = E.vv.n
            E.w(v, E.tt, M[("p1sw", ot, h)])
            tbm = M[("p1tb", ot)]
            if tbm:
                E.w(v, tbm[0], tbm[1])
            v.tensor_tensor(R.asb[:, :], SWp, R.tss[bsel][:, :], ALU.mult)
            M[("p1ropeA", ot, h)] = E.vv.n
            v.tensor_tensor(R.bsb[:, :], R.stg[sp][:, 512:1024], R.tcs[bsel][:, :], ALU.mult)
            v.tensor_tensor(R.stg[sp][:, 0:512], R.asb[:, :], R.bsb[:, :], ALU.add)
            M[("p1stg", ot, h)] = E.vv.n
        E.w(v, E.tt, M[("p1v", ot)])
        v.tensor_copy(R.Vt[:, ot * 512:(ot + 1) * 512], Vpp)
        M[("p1vev", ot)] = E.vv.n
    M["p1vv_end"] = E.vv.n
    if STAGE < 2:
        return
    ACCap = [R.TPq[0][:, :].bitcast(F32), R.TPq[1][:, :].bitcast(F32)]
    u = 0
    E.vv.lag = 3
    for bi in range(B * NI):
        for h in range(2):
            for J in range(NJ):
                p = u % 3
                E.w(v, E.ss, M[("p2r", u)])
                v.tensor_tensor(R.WT[p][:, :], R.S1p[p][:, :], R.rp[p][:, :], ALU.mult)
                M[("p2w", u)] = E.vv.n
                u += 1
        E.w(v, E.tt, M[("p2tte", bi)])
        if bi >= 2:
            osm = M[("p2ost", bi - 2)]
            if osm:
                E.w(v, osm[0], osm[1])
        v.tensor_copy(R.Ost[bi % 2][0:64, :], ACCap[0][0:64, 0:512])
        v.tensor_copy(R.Ost[bi % 2][64:128, :], ACCap[1][64:128, 0:512])
        M[("p2evac", bi)] = E.vv.n
    E.vv.lag = 1


def prog_scalar_a(E, R):
    s, M = E.s, E.M
    for ot in range(NOT):
        bsel = ot % 2
        xm = M[("p1x", ot)]
        if xm:
            E.w(s, xm[0], xm[1])
        E.w(s, E.gc, M["p1memset"])
        for sb in range(4):
            inst = s.activation(R.sqsc[:, :], R.xts[bsel][:, sb * D:(sb + 1) * D],
                                AF.Square, bias=R.zb[:, :],
                                accum_out=R.sumsq[:, ot * 4 + sb: ot * 4 + sb + 1])
        M[("p1sq", ot)] = E.ss.n
        E.w(s, E.vv, M[("p1var", ot)])
        s.activation(R.lnt[:, 16:20], R.lnt[:, 12:16], AF.Sqrt, bias=R.epsb[:, :])
        act_raw(E, s, R.lnt[:, 16:20], R.lnt[:, 16:20], AF.Reciprocal, R.zb[:, :])
        M[("p1rstd", ot)] = E.ss.n
        E.w(s, E.vv, M[("p1negmr", ot)])
        if ot >= 1:
            E.w(s, E.tt, M[("p1tp", ot - 1, 7)])
        for sb in range(4):
            inst = s.activation(R.xnb[bsel][:, sb * D:(sb + 1) * D],
                                R.xts[bsel][:, sb * D:(sb + 1) * D], AF.Identity,
                                bias=R.lnt[:, 20 + sb:21 + sb], scale=R.lnt[:, 16 + sb:17 + sb])
        M[("p1norm", ot)] = E.ss.n
        for ccn in range(1, 8, 2):
            E.w(s, E.tt, M[("p1tp", ot, ccn)])
            if ot >= 1 and ccn == 1:
                E.w(s, E.tt, M[("p1v", ot - 1)])
            s.copy(R.xnT[:, ccn * OT: ccn * OT + 512], R.TPq[ccn % 2][:, 0:512])
            M[("p1tpev", ot * 8 + ccn)] = (E.ss, E.ss.n)
        M[("p1trev_s", ot)] = E.ss.n
        for h in range(2):
            E.w(s, E.vv, M[("p1min", ot, h)])
            if ot * 2 + h >= 1:
                E.w(s, E.vv, M[("p1E", *divmod(ot * 2 + h - 1, 2))])
            s.activation(R.expm[:, :], R.es[:, :], AF.Exp, bias=R.zb[:, :])
            M[("p1exp", ot, h)] = E.ss.n
    M["p1ss_end"] = E.ss.n
    if STAGE < 2:
        return
    u = 0
    E.ss.lag = 3
    for bi in range(B * NI):
        for h in range(2):
            for J in range(NJ):
                p = u % 3
                E.w(s, E.tt, M[("p2s", u)])
                if u >= 3:
                    E.w(s, E.vv, M[("p2w", u - 3)])
                act_raw(E, s, R.rp[p][:, :], R.S2p[p][:, :], AF.Reciprocal, R.zb[:, :])
                M[("p2r", u)] = E.ss.n
                u += 1
    E.ss.lag = 1



def prog_gpsimd_b(E, R):
    g, M = E.g, E.M
    E.w(g, E.cc, M["a2a"])
    g.memset(R.zb3[:, :], 0.0)
    g.memset(R.epsb3[:, :], EPS)
    g.memset(R.ones3[:, :], 1.0)
    M["p3memset"] = E.gc.n
    gw3 = E.slots["gw3"]
    for src, dst in ((R.dwout, R.wouts), (R.dwr, R.wrs), (R.dwk, R.wks), (R.dwv2, R.wv2s)):
        gw3.inc(g.dma_start(out=dst[:, :].rearrange("p (c f) -> p c f", c=8),
                            in_=src.ap().rearrange("(c p) f -> p c f", c=8)))
    gw3.inc(g.dma_start(out=R.bouts[:, :], in_=R.dbout.ap()))
    gw3.inc(g.dma_start(out=R.idn3[:, :], in_=R.didn.ap()))
    gw3.inc(g.dma_start(out=R.hmm[:, :], in_=R.dhm.ap()))
    gw3.inc(g.dma_start(out=R.murs[:, :], in_=R.dmur.ap()))
    gw3.inc(g.dma_start(out=R.muks[:, :], in_=R.dmuk.ap()))
    M["p3w"] = gw3.n
    if P3SUB < 2:
        return
    E.w(g, E.slots["shgi"], M["p3hgi"])
    if E.em:
        E.cc.inc(g.collective_compute(
            "AllGather", ALU.bypass, replica_groups=[list(range(8))],
            ins=[R.HGI.ap().opt()], outs=[R.HGO.ap().opt()]))
    else:
        E.cc.inc(_DummyInst())
    M["hag"] = E.cc.n
    E.w(g, E.cc, M["hag"])
    E.slots["ghg"].inc(g.dma_start(out=R.hgos[:, :], in_=R.HGO.ap()))
    M["p3hgo"] = E.slots["ghg"].n


def prog_sync_b(E, R):
    sy, M = E.sy, E.M
    E.w(sy, E.cc, M["a2a"])
    for it in range(NT3):
        buf = it % 2
        if it >= 2:
            E.w(sy, E.tt, M[("p3op", it - 2)])
        sot = E.slots["sot%d" % buf]
        sot.inc(sy.dma_start(
            out=R.OTt[buf][:, :].rearrange("p (c t) -> p c t", c=8),
            in_=R.A2O.ap().rearrange("(c p) t -> p c t", c=8)[:, :, it * 128:(it + 1) * 128]))
        M[("p3ot", it)] = (sot, sot.n)
        if it >= 2:
            E.w(sy, E.vv, M[("p3xr1", it - 2)])
        sxw = E.slots["sxw%d" % buf]
        sxw.inc(sy.dma_start(out=R.xw[buf][:, :], in_=R.dxw.ap()[it * 128:(it + 1) * 128, :]))
        M[("p3xw", it)] = (sxw, sxw.n)
    if P3SUB < 2:
        E.w(sy, E.vv, M[("p3xn3", NT3 - 1)])
        for half in range(2):
            ev = M[("p3t3ev", (NT3 - 1) * 2 + half)]
            if ev:
                E.w(sy, ev[0], ev[1])
        return
    E.w(sy, E.vv, M[("p3xn3", NT3 - 1)])
    E.slots["shgi"].inc(sy.dma_start(out=R.HGI.ap(), in_=R.xn3b[127:128, :]))
    M["p3hgi"] = E.slots["shgi"].n
    if P3SUB < 5:
        return
    for it in range(NT3):
        buf = it % 2
        if it < 2:
            E.w(sy, E.vv, M[("p3xr1", NT3 - 2 + it)])
        else:
            E.w(sy, E.vv, M[("p3y", it - 2)])
        sxw = E.slots["sxw%d" % buf]
        sxw.inc(sy.dma_start(out=R.xw[buf][:, :], in_=R.dxw.ap()[it * 128:(it + 1) * 128, :]))
        M[("p3xw2", it)] = (sxw, sxw.n)
        E.w(sy, E.vv, M[("p3y", it)])
        syt = E.slots["sy%d" % (it % 2)]
        syt.inc(sy.dma_start(out=R.dy.ap()[it * 128:(it + 1) * 128, :], in_=R.ysb[it % 2][:, :]))
        M[("p3yst", it)] = (syt, syt.n)
    if E.em:
        sy.wait_ge(E.slots["sy0"].sem, E.slots["sy0"].n)
        sy.wait_ge(E.slots["sy1"].sem, E.slots["sy1"].n)


def prog_tensor_b(E, R):
    t, M = E.t, E.M
    E.w(t, E.slots["gw3"], M["p3w"])
    E.w(t, E.gc, M["p3memset"])
    for it in range(NT3):
        buf = it % 2
        if not (P3OPS & 1):
            break
        otm = M[("p3ot", it)]
        if otm:
            E.w(t, otm[0], otm[1])
        if it >= 1:
            E.w(t, E.ss, M[("p3pev", it - 1)])
            E.w(t, E.vv, M[("p3ps", it - 1)])
        for fh in range(2):
            for ccn in range(8):
                t.matmul(R.Pp[:, fh * 512:(fh + 1) * 512],
                         R.OTt[buf][:, ccn * 128:(ccn + 1) * 128],
                         R.wouts[:, ccn * D + fh * 512: ccn * D + fh * 512 + 512],
                         start=(ccn == 0), stop=False)
            inst = t.matmul(R.Pp[:, fh * 512:(fh + 1) * 512], R.ones3[:, 0:128],
                            R.bouts[:, fh * 512:(fh + 1) * 512], start=False, stop=True)
        E.tt.inc(inst)
        M[("p3op", it)] = E.tt.n
        if not (P3OPS & 32):
            continue
        E.w(t, E.vv, M[("p3xn3", it)])
        for half in range(2):
            tp = R.TP3[half]
            ghalf = it * 2 + half
            if ghalf >= 2:
                ev = M[("p3t3ev", ghalf - 2)]
                if ev:
                    E.w(t, ev[0], ev[1])
            for q in range(4):
                ccn = half * 4 + q
                inst = t.transpose(tp[:, q * 128:(q + 1) * 128],
                                   R.xn3b[:, ccn * 128:(ccn + 1) * 128], R.idn3[:, :])
            E.tt.inc(inst)
            M[("p3t3", it, half)] = E.tt.n
    if P3SUB < 2:
        return
    E.w(t, E.slots["ghg"], M["p3hgo"])
    for fh in range(2):
        E.tt.inc(t.matmul(R.HP[0:1, :], R.hmm[:, :], R.hgos[:, fh * 512:(fh + 1) * 512],
                          start=True, stop=True))
        M[("p3hp", fh)] = E.tt.n
        E.w(t, E.ss, M[("p3hev", fh)])
    E.w(t, E.ss, M[("p3hev", 1)])
    for ccn in range(8):
        inst = t.transpose(R.HS[:, 2 * ccn:2 * ccn + 1], R.halo[0:1, ccn * 128:(ccn + 1) * 128],
                           R.idn3[0:1, 0:1])
    E.tt.inc(inst)
    M["p3hst"] = E.tt.n
    if P3SUB < 4:
        return
    for ts4 in range(4):
        for dt_ in range(8):
            u = ts4 * 8 + dt_
            p = u % 2
            if u >= 2:
                E.w(t, E.ss, M[("p3kact", u - 2)])
            E.w(t, E.vv, M[("p3mixk", ts4)])
            for ccn in range(8):
                inst = t.matmul(R.Kp[p][:, :],
                                R.wks[:, ccn * D + dt_ * 128: ccn * D + dt_ * 128 + 128],
                                R.xmkT[:, ccn * 512:(ccn + 1) * 512],
                                start=(ccn == 0), stop=(ccn == 7))
            E.tt.inc(inst)
            M[("p3k", u)] = E.tt.n
        if P3SUB < 5:
            continue
        for q in range(4):
            it = ts4 * 4 + q
            E.w(t, E.vv, M[("p3mixr", ts4)])
            if it >= 1:
                E.w(t, E.ss, M[("p3sig", it - 1)])
                E.w(t, E.vv, M[("p3ff", it - 1)])
            for fh in range(2):
                for ccn in range(8):
                    inst = t.matmul(R.Pp[:, fh * 512:(fh + 1) * 512],
                                    R.xmrT[:, ccn * 512 + q * 128: ccn * 512 + q * 128 + 128],
                                    R.wrs[:, ccn * D + fh * 512: ccn * D + fh * 512 + 512],
                                    start=(ccn == 0), stop=(ccn == 7))
            E.tt.inc(inst)
            M[("p3r", it)] = E.tt.n
            E.w(t, E.ss, M[("p3sig", it)])
            E.w(t, E.ss, M[("p3kact", ts4 * 8 + 7)])
            for fh in range(2):
                for ccn in range(8):
                    inst = t.matmul(R.Pp[:, fh * 512:(fh + 1) * 512],
                                    R.kactS[:, ccn * 512 + q * 128: ccn * 512 + q * 128 + 128],
                                    R.wv2s[:, ccn * D + fh * 512: ccn * D + fh * 512 + 512],
                                    start=(ccn == 0), stop=(ccn == 7))
            E.tt.inc(inst)
            M[("p3vo", it)] = E.tt.n


def prog_vector_b(E, R):
    v, M = E.v, E.M
    E.w(v, E.slots["gw3"], M["p3w"])
    v.tensor_scalar(R.omurs[:, :], R.murs[:, :], -1.0, 1.0, ALU.mult, ALU.add)
    v.tensor_scalar(R.omuks[:, :], R.muks[:, :], -1.0, 1.0, ALU.mult, ALU.add)
    for it in range(NT3):
        if not (P3OPS & 2):
            break
        E.w(v, E.tt, M[("p3op", it)])
        if VB_MODE == 0:
            v.reduce_sum(R.l3[:, 0:1], R.Pp[:, :], axis=AXX)
        elif VB_MODE == 1:
            v.reduce_sum(R.l3[:, 0:1], R.Pp[:, 0:512], axis=AXX)
        elif VB_MODE == 2:
            v.tensor_copy(R.l3[:, 0:8], R.Pp[:, 0:8])
        else:
            v.tensor_copy(R.l3[:, 0:8], R.xw[0][:, 0:8])
        M[("p3ps", it)] = E.vv.n
        if not (P3OPS & 8):
            continue
        E.w(v, E.ss, M[("p3pev", it)])
        v.tensor_scalar(R.l3[:, 2:3], R.l3[:, 0:1], 1.0 / D, None, ALU.mult)
        v.tensor_scalar(R.l3[:, 3:4], R.l3[:, 1:2], 1.0 / D, None, ALU.mult)
        v.tensor_tensor(R.l3[:, 4:5], R.l3[:, 2:3], R.l3[:, 2:3], ALU.mult)
        v.tensor_tensor(R.l3[:, 5:6], R.l3[:, 3:4], R.l3[:, 4:5], ALU.subtract)
        M[("p3var2", it)] = E.vv.n
        E.w(v, E.ss, M[("p3rs2", it)])
        v.tensor_scalar(R.ln2o[:, it * D:(it + 1) * D], R.Psb[:, :],
                                 R.l3[:, 2:3], R.l3[:, 6:7], ALU.subtract, ALU.mult)
        M[("p3ln2", it)] = E.vv.n
        if not (P3OPS & 16):
            continue
        xwm = M[("p3xw", it)]
        if xwm:
            E.w(v, xwm[0], xwm[1])
        v.tensor_tensor(R.xr1[:, :], R.xw[it % 2][:, :],
                                 R.ln2o[:, it * D:(it + 1) * D], ALU.add)
        M[("p3xr1", it)] = E.vv.n
        v.reduce_sum(R.l3[:, 8:9], R.xr1[:, :], axis=AXX)
        E.w(v, E.ss, M[("p3sq3", it)])
        v.tensor_scalar(R.l3[:, 10:11], R.l3[:, 8:9], 1.0 / D, None, ALU.mult)
        v.tensor_scalar(R.l3[:, 11:12], R.l3[:, 9:10], 1.0 / D, None, ALU.mult)
        v.tensor_tensor(R.l3[:, 12:13], R.l3[:, 10:11], R.l3[:, 10:11], ALU.mult)
        v.tensor_tensor(R.l3[:, 13:14], R.l3[:, 11:12], R.l3[:, 12:13], ALU.subtract)
        M[("p3var3", it)] = E.vv.n
        E.w(v, E.ss, M[("p3rs3", it)])
        if it >= 1:
            E.w(v, E.tt, M[("p3t3", it - 1, 1)])
        v.tensor_scalar(R.xn3b[:, :], R.xr1[:, :],
                                 R.l3[:, 10:11], R.l3[:, 14:15], ALU.subtract, ALU.mult)
        M[("p3xn3", it)] = E.vv.n
        for half in range(2):
            E.w(v, E.tt, M[("p3t3", it, half)])
            dst = R.xn3T[:, :].rearrange("p (c w) -> p c w", c=8)[:, half * 4:half * 4 + 4,
                                                                 it * 128:it * 128 + 128]
            v.tensor_copy(dst, R.TP3[half][:, :].rearrange("p (c w) -> p c w", c=4))
            M[("p3t3ev", it * 2 + half)] = (E.vv, E.vv.n)
    if P3SUB < 2:
        return
    E.w(v, E.tt, M["p3hst"])
    v.tensor_copy(R.haloT[:, :], R.HS[:, 0:16:2])
    M["p3halot"] = E.vv.n
    if P3SUB < 3:
        return
    for ts4 in range(4):
        t0 = ts4 * 512
        for mu, omu, dst, mk in ((R.murs, R.omurs, R.xmrT, "p3mixr"), (R.muks, R.omuks, R.xmkT, "p3mixk")):
            if ts4 == 0:
                E.w(v, E.vv, M["p3halot"])
            if ts4 >= 1:
                E.w(v, E.tt, M[("p3r", ts4 * 4 - 1)] if mk == "p3mixr" else M[("p3k", (ts4 - 1) * 8 + 7)])
            for ccn in range(8):
                cbase = ccn * WIN
                if t0 == 0:
                    v.tensor_scalar(R.mixs[:, 1:512], R.xn3T[:, cbase: cbase + 511],
                                    omu[:, ccn:ccn + 1], None, ALU.mult)
                    v.tensor_scalar(R.mixs[:, 0:1], R.haloT[:, ccn:ccn + 1],
                                    omu[:, ccn:ccn + 1], None, ALU.mult)
                else:
                    v.tensor_scalar(R.mixs[:, 0:512], R.xn3T[:, cbase + t0 - 1: cbase + t0 + 511],
                                    omu[:, ccn:ccn + 1], None, ALU.mult)
                inst = v.scalar_tensor_tensor(dst[:, ccn * 512:(ccn + 1) * 512],
                                              R.xn3T[:, cbase + t0: cbase + t0 + 512],
                                              mu[:, ccn:ccn + 1], R.mixs[:, 0:512],
                                              ALU.mult, ALU.add)
            M[(mk, ts4)] = E.vv.n
        if P3SUB < 4:
            continue
        for dt_ in range(8):
            u = ts4 * 8 + dt_
            p = u % 2
            E.w(v, E.tt, M[("p3k", u)])
            if u >= 1:
                E.w(v, E.ss, M[("p3kact", u - 1)])
            v.tensor_scalar_max(R.kpre[:, :], R.Kp[p][:, :], 0.0)
            M[("p3relu", u)] = E.vv.n
        if P3SUB < 5:
            continue
        for q in range(4):
            it = ts4 * 4 + q
            E.w(v, E.tt, M[("p3vo", it)])
            if it >= 1:
                E.w(v, E.ss, M[("p3sq4", it - 1)])
            v.tensor_tensor(R.ffo[:, :], R.Pp[:, :], R.rsig[:, :], ALU.mult)
            M[("p3ff", it)] = E.vv.n
            v.reduce_sum(R.l3[:, 16:17], R.ffo[:, :], axis=AXX)
            E.w(v, E.ss, M[("p3sq4", it)])
            v.tensor_scalar(R.l3[:, 18:19], R.l3[:, 16:17], 1.0 / D, None, ALU.mult)
            v.tensor_scalar(R.l3[:, 19:20], R.l3[:, 17:18], 1.0 / D, None, ALU.mult)
            v.tensor_tensor(R.l3[:, 20:21], R.l3[:, 18:19], R.l3[:, 18:19], ALU.mult)
            v.tensor_tensor(R.l3[:, 21:22], R.l3[:, 19:20], R.l3[:, 20:21], ALU.subtract)
            M[("p3var4", it)] = E.vv.n
            E.w(v, E.ss, M[("p3rs4", it)])
            v.tensor_scalar(R.ln4o[:, :], R.ffo[:, :], R.l3[:, 18:19], R.l3[:, 22:23],
                            ALU.subtract, ALU.mult)
            xwm = M[("p3xw2", it)]
            if xwm:
                E.w(v, xwm[0], xwm[1])
            if it >= 2:
                ym = M[("p3yst", it - 2)]
                if ym:
                    E.w(v, ym[0], ym[1])
            v.tensor_tensor(R.ysb[it % 2][:, :], R.xw[it % 2][:, :],
                            R.ln2o[:, it * D:(it + 1) * D], ALU.add)
            v.tensor_tensor(R.ysb[it % 2][:, :], R.ysb[it % 2][:, :], R.ln4o[:, :], ALU.add)
            M[("p3y", it)] = E.vv.n


def prog_scalar_b(E, R):
    s, M = E.s, E.M
    for it in range(NT3):
        if not (P3OPS & 4):
            break
        E.w(s, E.tt, M[("p3op", it)])
        E.w(s, E.vv, M[("p3ps", it)])
        if it >= 1:
            E.w(s, E.vv, M[("p3xn3", it - 1)])
        s.copy(R.Psb[:, :], R.Pp[:, :])
        s.activation(R.sq3[:, :], R.Pp[:, :], AF.Square, bias=R.zb3[:, :],
                              accum_out=R.l3[:, 1:2])
        M[("p3pev", it)] = E.ss.n
        if not (P3OPS & 8):
            continue
        E.w(s, E.vv, M[("p3var2", it)])
        s.activation(R.l3[:, 6:7], R.l3[:, 5:6], AF.Sqrt, bias=R.epsb3[:, :])
        act_raw(E, s, R.l3[:, 6:7], R.l3[:, 6:7], AF.Reciprocal, R.zb3[:, :])
        M[("p3rs2", it)] = E.ss.n
        E.w(s, E.vv, M[("p3xr1", it)])
        s.activation(R.sq3[:, :], R.xr1[:, :], AF.Square, bias=R.zb3[:, :],
                              accum_out=R.l3[:, 9:10])
        M[("p3sq3", it)] = E.ss.n
        E.w(s, E.vv, M[("p3var3", it)])
        s.activation(R.l3[:, 14:15], R.l3[:, 13:14], AF.Sqrt, bias=R.epsb3[:, :])
        act_raw(E, s, R.l3[:, 14:15], R.l3[:, 14:15], AF.Reciprocal, R.zb3[:, :])
        M[("p3rs3", it)] = E.ss.n

    if P3SUB < 2:
        return
    for fh in range(2):
        E.w(s, E.tt, M[("p3hp", fh)])
        s.copy(R.halo[0:1, fh * 512:(fh + 1) * 512], R.HP[0:1, :])
        M[("p3hev", fh)] = E.ss.n
    if P3SUB < 4:
        return
    for ts4 in range(4):
        for dt_ in range(8):
            u = ts4 * 8 + dt_
            E.w(s, E.vv, M[("p3relu", u)])
            if dt_ == 0 and ts4 >= 1:
                E.w(s, E.tt, M[("p3vo", ts4 * 4 - 1)])
            s.activation(R.kactS[:, dt_ * 512:(dt_ + 1) * 512], R.kpre[:, :],
                                  AF.Square, bias=R.zb3[:, :])
            M[("p3kact", u)] = E.ss.n
        if P3SUB < 5:
            continue
        for q in range(4):
            it = ts4 * 4 + q
            E.w(s, E.tt, M[("p3r", it)])
            if it >= 1:
                E.w(s, E.vv, M[("p3ff", it - 1)])
            s.activation(R.rsig[:, :], R.Pp[:, :], AF.Sigmoid, bias=R.zb3[:, :])
            M[("p3sig", it)] = E.ss.n
            E.w(s, E.vv, M[("p3ff", it)])
            s.activation(R.sq3[:, :], R.ffo[:, :], AF.Square, bias=R.zb3[:, :],
                                  accum_out=R.l3[:, 17:18])
            M[("p3sq4", it)] = E.ss.n
            E.w(s, E.vv, M[("p3var4", it)])
            s.activation(R.l3[:, 22:23], R.l3[:, 21:22], AF.Sqrt, bias=R.epsb3[:, :])
            act_raw(E, s, R.l3[:, 22:23], R.l3[:, 22:23], AF.Reciprocal, R.zb3[:, :])
            M[("p3rs4", it)] = E.ss.n



class _R:
    pass


def build_nc():
    nc = bass.Bass("TRN2", target_bir_lowering=False)
    R = _R()
    R.dxf = nc.dram_tensor("x_full", [T, D], F32, kind="ExternalInput")
    R.dxw = nc.dram_tensor("x_win", [WIN, D], F32, kind="ExternalInput")
    R.dwqk = nc.dram_tensor("wqk", [D, 256], F32, kind="ExternalInput")
    R.dbqk = nc.dram_tensor("bqk", [1, 256], F32, kind="ExternalInput")
    R.dwv = nc.dram_tensor("wv", [D, 128], F32, kind="ExternalInput")
    R.dbv = nc.dram_tensor("bv", [1, 128], F32, kind="ExternalInput")
    R.dwout = nc.dram_tensor("wout", [D, D], F32, kind="ExternalInput")
    R.dbout = nc.dram_tensor("bout", [1, D], F32, kind="ExternalInput")
    R.dwr = nc.dram_tensor("wr_t", [D, D], F32, kind="ExternalInput")
    R.dwk = nc.dram_tensor("wk_t", [D, D], F32, kind="ExternalInput")
    R.dwv2 = nc.dram_tensor("wv2_t", [D, D], F32, kind="ExternalInput")
    R.dmur = nc.dram_tensor("mu_r", [128, 8], F32, kind="ExternalInput")
    R.dmuk = nc.dram_tensor("mu_k", [128, 8], F32, kind="ExternalInput")
    R.dtblc = nc.dram_tensor("tblc", [128, N], F32, kind="ExternalInput")
    R.dtbls = nc.dram_tensor("tbls", [128, N], F32, kind="ExternalInput")
    R.dperm = nc.dram_tensor("permm", [128, 128], F32, kind="ExternalInput")
    R.didn = nc.dram_tensor("idn", [128, 128], F32, kind="ExternalInput")
    R.dhm = nc.dram_tensor("hmask", [8, 1], F32, kind="ExternalInput")
    R.dy = nc.dram_tensor("y", [WIN, D], F32, kind="ExternalOutput")
    R.QD = nc.dram_tensor("QD", [128, 2 * T], BF16)
    R.A2I = nc.dram_tensor("a2i", [1024, WIN], BF16)
    R.A2O = nc.dram_tensor("a2o", [1024, WIN], BF16)
    R.HGI = nc.dram_tensor("hgi", [1, D], BF16)
    R.HGO = nc.dram_tensor("hgo", [8, D], BF16)

    marks = Marks()
    E = Env(nc, marks)

    with ExitStack() as top:
        E.sems(nc, top)
        with ExitStack() as pa:
            e = pa.enter_context
            R.KK = e(nc.sbuf_tensor("KK", [128, 2 * T], BF16))
            R.Vt = e(nc.sbuf_tensor("Vt", [128, T], BF16))
            R.wqk = e(nc.sbuf_tensor("wqk_s", [128, 8 * 256], BF16))
            R.wv = e(nc.sbuf_tensor("wv_s", [128, 8 * 128], BF16))
            R.bqk = e(nc.sbuf_tensor("bqk_s", [1, 256], BF16))
            R.bv = e(nc.sbuf_tensor("bv_s", [1, 128], BF16))
            R.ones = e(nc.sbuf_tensor("ones_s", [1, 512], BF16))
            R.permb = e(nc.sbuf_tensor("permb", [128, 128], BF16))
            R.idnb = e(nc.sbuf_tensor("idnb", [128, 128], BF16))
            R.zb = e(nc.sbuf_tensor("zb", [128, 1], F32))
            R.epsb = e(nc.sbuf_tensor("epsb", [128, 1], F32))
            R.xts = [e(nc.sbuf_tensor(f"xts{i}", [128, 4 * D], BF16)) for i in range(2)]
            R.sqsc = e(nc.sbuf_tensor("sqsc", [128, D], F32))
            R.xnb = [e(nc.sbuf_tensor(f"xnb{i}", [128, 4 * D], BF16)) for i in range(2)]
            R.xnT = e(nc.sbuf_tensor("xnT", [128, 8 * OT], BF16))
            R.sums = e(nc.sbuf_tensor("sums", [128, 4 * NOT], F32))
            R.sumsq = e(nc.sbuf_tensor("sumsq", [128, 4 * NOT], F32))
            R.lnt = e(nc.sbuf_tensor("lnt", [128, 24], F32))
            R.es = e(nc.sbuf_tensor("es", [128, 512], F32))
            R.expm = e(nc.sbuf_tensor("expm", [128, 512], F32))
            R.stg = [e(nc.sbuf_tensor(f"stg{i}", [128, 1024], BF16)) for i in range(2)]
            R.asb = e(nc.sbuf_tensor("asb", [128, 512], BF16))
            R.bsb = e(nc.sbuf_tensor("bsb", [128, 512], BF16))
            R.tcs = [e(nc.sbuf_tensor(f"tcs{i}", [128, 512], BF16)) for i in range(2)]
            R.tss = [e(nc.sbuf_tensor(f"tss{i}", [128, 512], BF16)) for i in range(2)]
            R.Qt = [e(nc.sbuf_tensor(f"Qt{i}", [128, 1024], BF16)) for i in range(2)]
            R.rp = [e(nc.sbuf_tensor(f"rp{i}", [128, 512], F32)) for i in range(3)]
            R.WT = [e(nc.sbuf_tensor(f"WT{i}", [128, 512], BF16)) for i in range(3)]
            R.Ost = [e(nc.sbuf_tensor(f"Ost{i}", [128, 512], BF16)) for i in range(2)]
            R.S1p = [e(nc.psum_tensor(f"S1p{i}", [128, 512], F32)) for i in range(3)]
            R.S2p = [e(nc.psum_tensor(f"S2p{i}", [128, 512], F32)) for i in range(3)]
            R.TPq = [e(nc.psum_tensor(f"TPq{i}", [128, 1024], BF16)) for i in range(2)]

            E.em = False
            E.reset()
            E.use_engines(t=_DummyEng(), v=_DummyEng(), s=_DummyEng(), g=_DummyEng(), sy=_DummyEng())
            for fn in (prog_sync_a, prog_gpsimd_a, prog_tensor_a, prog_vector_a, prog_scalar_a):
                fn(E, R)
            snapA = [c.n for c in E.ctrs]

            E._snapA = snapA

            Marks.strict = True
            E.em = True
            E.reset()
            with nc.Block() as blk:
                @blk.sync
                def _(sy):
                    E.use_engines(sy=sy)
                    prog_sync_a(E, R)

                @blk.gpsimd
                def _(g):
                    E.use_engines(g=g)
                    prog_gpsimd_a(E, R)

                @blk.tensor
                def _(t):
                    E.use_engines(t=t)
                    prog_tensor_a(E, R)

                @blk.vector
                def _(v):
                    E.use_engines(v=v)
                    prog_vector_a(E, R)

                @blk.scalar
                def _(s):
                    E.use_engines(s=s)
                    prog_scalar_a(E, R)
            assert [c.n for c in E.ctrs] == snapA, "pass A count mismatch"

        if STAGE < 3:
            return nc
        with ExitStack() as pb:
            e = pb.enter_context
            R.OTt = [e(nc.sbuf_tensor(f"OTt{i}", [128, 8 * 128], BF16)) for i in range(2)]
            R.ln2o = e(nc.sbuf_tensor("ln2o", [128, NT3 * D], BF16))
            R.xn3T = e(nc.sbuf_tensor("xn3T", [128, 8 * WIN], BF16))
            R.wouts = e(nc.sbuf_tensor("wouts", [128, 8 * D], BF16))
            R.bouts = e(nc.sbuf_tensor("bouts", [1, D], BF16))
            R.wrs = e(nc.sbuf_tensor("wrs", [128, 8 * D], BF16))
            R.wks = e(nc.sbuf_tensor("wks", [128, 8 * D], BF16))
            R.wv2s = e(nc.sbuf_tensor("wv2s", [128, 8 * D], BF16))
            R.murs = e(nc.sbuf_tensor("murs", [128, 8], F32))
            R.muks = e(nc.sbuf_tensor("muks", [128, 8], F32))
            R.omurs = e(nc.sbuf_tensor("omurs", [128, 8], F32))
            R.omuks = e(nc.sbuf_tensor("omuks", [128, 8], F32))
            R.zb3 = e(nc.sbuf_tensor("zb3", [128, 1], F32))
            R.epsb3 = e(nc.sbuf_tensor("epsb3", [128, 1], F32))
            R.ones3 = e(nc.sbuf_tensor("ones3", [1, 512], BF16))
            R.idn3 = e(nc.sbuf_tensor("idn3", [128, 128], BF16))
            R.hmm = e(nc.sbuf_tensor("hmm", [8, 1], BF16))
            R.hgos = e(nc.sbuf_tensor("hgos", [8, D], BF16))
            R.halo = e(nc.sbuf_tensor("halo", [1, D], BF16))
            R.haloT = e(nc.sbuf_tensor("haloT", [128, 8], BF16))
            R.xw = [e(nc.sbuf_tensor(f"xw{i}", [128, D], F32)) for i in range(2)]
            R.Psb = e(nc.sbuf_tensor("Psb", [128, D], F32))
            R.xr1 = e(nc.sbuf_tensor("xr1", [128, D], F32))
            R.xn3b = e(nc.sbuf_tensor("xn3b", [128, D], BF16))
            R.sq3 = e(nc.sbuf_tensor("sq3", [128, D], F32))
            R.ln4o = e(nc.sbuf_tensor("ln4o", [128, D], F32))
            R.l3 = e(nc.sbuf_tensor("l3", [128, 24], F32))
            R.mixs = e(nc.sbuf_tensor("mixs", [128, 512], BF16))
            R.xmrT = e(nc.sbuf_tensor("xmrT", [128, 8 * 512], BF16))
            R.xmkT = e(nc.sbuf_tensor("xmkT", [128, 8 * 512], BF16))
            R.kactS = e(nc.sbuf_tensor("kactS", [128, 8 * 512], BF16))
            R.kpre = e(nc.sbuf_tensor("kpre", [128, 512], BF16))
            R.rsig = e(nc.sbuf_tensor("rsig", [128, D], BF16))
            R.ffo = e(nc.sbuf_tensor("ffo", [128, D], F32))
            R.ysb = [e(nc.sbuf_tensor(f"ysb{i}", [128, D], F32)) for i in range(2)]
            R.Pp = e(nc.psum_tensor("Pp", [128, 1024], F32))
            R.Kp = [e(nc.psum_tensor(f"Kp{i}", [128, 512], F32)) for i in range(2)]
            R.TP3 = [e(nc.psum_tensor(f"TP3{i}", [128, 512], BF16)) for i in range(2)]
            R.HP = e(nc.psum_tensor("HP", [1, 512], F32))
            R.HS = e(nc.psum_tensor("HS", [128, 16], BF16))

            Marks.strict = False
            E.em = False
            for c, n0 in zip(E.ctrs, E._snapA):
                c.n = n0
            E.use_engines(t=_DummyEng(), v=_DummyEng(), s=_DummyEng(), g=_DummyEng(), sy=_DummyEng())
            for fn in (prog_gpsimd_b, prog_sync_b, prog_tensor_b, prog_vector_b, prog_scalar_b):
                fn(E, R)
            snapB = [c.n for c in E.ctrs]

            Marks.strict = True
            E.em = True
            for c, n0 in zip(E.ctrs, E._snapA):
                c.n = n0
            with nc.Block() as blk3:
                @blk3.gpsimd
                def _(g):
                    E.use_engines(g=g)
                    prog_gpsimd_b(E, R)

                @blk3.sync
                def _(sy):
                    E.use_engines(sy=sy)
                    prog_sync_b(E, R)

                @blk3.tensor
                def _(t):
                    E.use_engines(t=t)
                    prog_tensor_b(E, R)

                @blk3.vector
                def _(v):
                    E.use_engines(v=v)
                    prog_vector_b(E, R)

                @blk3.scalar
                def _(s):
                    E.use_engines(s=s)
                    prog_scalar_b(E, R)
            assert [c.n for c in E.ctrs] == snapB, "pass B count mismatch"

    return nc


def _host_prep(inputs):
    x = np.ascontiguousarray(np.asarray(inputs["L0_attn_x"], dtype=np.float32).reshape(T, D))
    wqkv = np.asarray(inputs["L0_attn_w_qkv"], dtype=np.float32)
    bqkv = np.asarray(inputs["L0_attn_b_qkv"], dtype=np.float32)
    wout = np.ascontiguousarray(np.asarray(inputs["L0_attn_w_out"], dtype=np.float32))
    bout = np.asarray(inputs["L0_attn_b_out"], dtype=np.float32).reshape(1, D)
    Wr = np.asarray(inputs["L0_ffn_W_r"], dtype=np.float32)
    Wk = np.asarray(inputs["L0_ffn_W_k"], dtype=np.float32)
    Wv2 = np.asarray(inputs["L0_ffn_W_v"], dtype=np.float32)
    mur = np.ascontiguousarray(np.asarray(inputs["L0_ffn_mu_r"], dtype=np.float32).reshape(8, 128).T)
    muk = np.ascontiguousarray(np.asarray(inputs["L0_ffn_mu_k"], dtype=np.float32).reshape(8, 128).T)

    perm = np.concatenate([np.arange(0, 64, 2), np.arange(1, 64, 2)])
    p32 = np.arange(32, dtype=np.float64)
    theta = 1.0 / (10000.0 ** (p32 / 32.0))
    pos = np.arange(N, dtype=np.float64)
    ang = pos[:, None] * theta[None, :]
    cosv = np.cos(ang).T
    sinv = np.sin(ang).T
    tblc = np.empty((128, N), dtype=np.float32)
    tbls = np.empty((128, N), dtype=np.float32)
    for g in range(4):
        r0 = g * 32
        tblc[r0:r0 + 32] = cosv
        if g % 2 == 0:
            tbls[r0:r0 + 32] = -sinv
        else:
            tbls[r0:r0 + 32] = sinv
    permm = np.zeros((128, 128), dtype=np.float32)
    for i in range(128):
        j = i + 32 if (i // 32) % 2 == 0 else i - 32
        permm[j, i] = 1.0
    idn = np.eye(128, dtype=np.float32)

    wr_t = np.ascontiguousarray(Wr.T)
    wk_t = np.ascontiguousarray(Wk.T)
    wv2_t = np.ascontiguousarray(Wv2.T)

    in_maps = []
    for c in range(NCORE):
        h0 = 2 * c
        qcols = []
        for h in (h0, h0 + 1):
            qcols.append(np.concatenate([
                wqkv[:, h * 64 + perm],
                wqkv[:, D + h * 64 + perm],
            ], axis=1))
        wqk = np.ascontiguousarray(np.concatenate(qcols, axis=1))
        bqk = np.concatenate([
            np.concatenate([bqkv[h * 64 + perm], bqkv[D + h * 64 + perm]])
            for h in (h0, h0 + 1)]).reshape(1, 256).astype(np.float32)
        wv = np.ascontiguousarray(wqkv[:, 2 * D + h0 * 64: 2 * D + h0 * 64 + 128])
        bv = bqkv[2 * D + h0 * 64: 2 * D + h0 * 64 + 128].reshape(1, 128).astype(np.float32)
        hm = np.zeros((8, 1), dtype=np.float32)
        if c not in (0, 4):
            hm[c - 1, 0] = 1.0
        in_maps.append({
            "x_full": x,
            "x_win": np.ascontiguousarray(x[c * WIN:(c + 1) * WIN]),
            "wqk": wqk, "bqk": np.ascontiguousarray(bqk),
            "wv": wv, "bv": np.ascontiguousarray(bv),
            "wout": wout, "bout": np.ascontiguousarray(bout),
            "wr_t": wr_t, "wk_t": wk_t, "wv2_t": wv2_t,
            "mu_r": mur, "mu_k": muk,
            "tblc": tblc, "tbls": tbls, "permm": permm, "idn": idn,
            "hmask": hm,
        })
    return in_maps


def kernel(**inputs):
    in_maps = _host_prep(inputs)
    nc = build_nc()
    res = run_bass_kernel_spmd(nc, in_maps, core_ids=list(range(NCORE)))
    y = np.concatenate([np.asarray(res.results[c]["y"], dtype=np.float32)
                        for c in range(NCORE)], axis=0)
    return y.reshape(B, N, D)


# version 07
# speedup vs baseline: 1.6957x; 1.6957x over previous
import sys
sys.path.insert(0, "/opt/trn_rl_repo")
import numpy as np
import concourse.bass as bass
import concourse.mybir as mybir
from concourse.bass_utils import run_bass_kernel_spmd
from concourse.aot_env import allow_ici_all_to_all
from contextlib import ExitStack

F32 = mybir.dt.float32
BF16 = mybir.dt.bfloat16
AF = mybir.ActivationFunctionType
ALU = mybir.AluOpType
AXX = mybir.AxisListType.X

B = 2
N = 8192
T = 16384
D = 1024
NCORE = 8
WIN = 2048
OT = 512
NOT = T // OT
NI = N // OT
NJ = N // 128
NT3 = WIN // 128
EPS = 1e-5
STAGE = 3
P3SUB = 5
P3OPS = 63
LOOSE = False
VB_MODE = 0
SYB_MODE = 0


class Marks(dict):
    strict = False

    def __missing__(self, key):
        if Marks.strict and not LOOSE:
            raise KeyError(key)
        return 0


class _DummyInst:
    def then_inc(self, *a, **k):
        return self


class _DummyEng:
    def __getattr__(self, name):
        return lambda *a, **k: _DummyInst()


_PASSTHRU = {"wait_ge", "dma_start", "collective_compute", "nop", "drain",
             "indirect_dma_start", "sem_inc"}


class EngProxy:

    def __init__(self, real, ctr, em):
        self.raw = real
        self._ctr = ctr
        self._em = em

    def __getattr__(self, name):
        if name in _PASSTHRU:
            return getattr(self.raw, name)
        real_fn = getattr(self.raw, name)
        def call(*a, **k):
            c = self._ctr
            if self._em and c.n - (c.lag - 1) * c.step > 0:
                self.raw.wait_ge(c.sem, c.n - (c.lag - 1) * c.step)
            inst = real_fn(*a, **k)
            c.n += c.step
            if self._em and c.sem is not None:
                inst.then_inc(c.sem, c.step)
            return inst
        return call


class Ctr:
    def __init__(self, name, step):
        self.name = name
        self.step = step
        self.n = 0
        self.sem = None
        self.lag = 1

    def inc(self, inst):
        self.n += self.step
        inst.then_inc(self.sem, self.step) if self.sem is not None else None
        return inst


class Env:

    def __init__(self, nc, marks):
        self.nc = nc
        self.M = marks
        self.em = False
        self.t = self.sy = _DummyEng()
        self.v = self.s = self.g = None
        self.tt = Ctr("tt", 1)
        self.vv = Ctr("vv", 1)
        self.ss = Ctr("ss", 1)
        self.gc = Ctr("gc", 1)
        self.cc = Ctr("cc", 1)
        self.syn = Ctr("syn", 16)
        self.gd = Ctr("gd", 16)
        self.st = Ctr("st", 16)
        slot_names = ["gw", "gx0", "gx1", "gt0", "gt1", "gw3", "ghg",
                      "sg0", "sg1", "skk", "sqt0", "sqt1", "sos0", "sos1",
                      "sot0", "sot1", "sxw0", "sxw1", "shgi", "sy0", "sy1"]
        self.slots = {n: Ctr(n, 16) for n in slot_names}
        self.ctrs = [self.tt, self.vv, self.ss, self.gc, self.cc, self.syn, self.gd, self.st] + \
            [self.slots[n] for n in slot_names]

    def use_engines(self, t=None, v=None, s=None, g=None, sy=None):
        if t is not None:
            self.t = t
        if sy is not None:
            self.sy = sy
        if v is not None:
            self.v = EngProxy(v, self.vv, self.em)
        if s is not None:
            self.s = EngProxy(s, self.ss, self.em)
        if g is not None:
            self.g = EngProxy(g, self.gc, self.em)

    def reset(self):
        for c in self.ctrs:
            c.n = 0

    def sems(self, nc, stack):
        for c in self.ctrs:
            c.sem = stack.enter_context(nc.semaphore("sem_" + c.name))

    def w(self, eng, ctr, val):
        if self.em and val > 0:
            eng.wait_ge(ctr.sem, val)


def act_raw(E, eng, out_ap, in_ap, func, bias_ap, scale=1.0):
    c = eng._ctr
    if not E.em:
        c.n += c.step
        return _DummyInst()
    nc = E.nc
    with nc.discard():
        proto = eng.raw.activation(out_ap, in_ap, AF.Exp, bias=bias_ap, scale=scale).ins
    new = mybir.InstActivation(
        name=nc.get_next_instruction_name(), engine=proto.engine,
        ins=list(proto.ins), outs=list(proto.outs), func=func, debug=proto.debug)
    if c.n - (c.lag - 1) * c.step > 0:
        eng.raw.wait_ge(c.sem, c.n - (c.lag - 1) * c.step)
    inst = eng.raw.add_instruction(new)
    c.n += c.step
    inst.then_inc(c.sem, c.step)
    return inst



def prog_sync_a(E, R):
    sy, M = E.sy, E.M
    for ot in range(NOT):
        gt = ot * OT
        for h in range(2):
            ft = ot * 2 + h
            sp = ft % 2
            sg = E.slots["sg%d" % sp]
            E.w(sy, E.vv, M[("p1stg", ot, h)])
            sg.inc(sy.dma_start(
                out=R.QD.ap()[:, h * T + gt: h * T + gt + 512]
                    .rearrange("(g p) t -> p g t", g=2),
                in_=R.stg[sp][0:64, :].rearrange("p (g t) -> p g t", g=2)))
            sg.inc(sy.dma_start(
                out=R.KK[0:64, h * T + gt: h * T + gt + 512],
                in_=R.stg[sp][64:128, 0:512]))
            sg.inc(sy.dma_start(
                out=R.KK[64:128, h * T + gt: h * T + gt + 512],
                in_=R.stg[sp][64:128, 512:1024]))
            M[("p1syn_stg", ot, h)] = (sg, sg.n)
    M["p1sg0_end"] = E.slots["sg0"].n
    M["p1sg1_end"] = E.slots["sg1"].n
    if STAGE < 2:
        M["p2syn_end"] = E.syn.n
        return
    def store_O(bi):
        b, I = bi // NI, bi % NI
        gtok = b * N + I * OT
        w_, c0 = gtok // WIN, gtok % WIN
        so = E.slots["sos%d" % (bi % 2)]
        so.inc(sy.dma_start(out=R.A2I.ap()[128 * w_:128 * w_ + 128, c0:c0 + 512],
                            in_=R.Ost[bi % 2][:, :]))
        M[("p2ost", bi)] = (so, so.n)

    for bi in range(B * NI):
        b, I = bi // NI, bi % NI
        buf = bi % 2
        if bi >= 2:
            E.w(sy, E.tt, M[("p2tte", bi - 2)])
        gt = b * N + I * OT
        sq = E.slots["sqt%d" % buf]
        sq.inc(sy.dma_start(
            out=R.Qt[buf][:, :].rearrange("p (h t) -> p h t", h=2),
            in_=R.QD.ap().rearrange("p (h n) -> p h n", h=2)[:, :, gt:gt + 512]))
        M[("p2qt", bi)] = (sq, sq.n)
        if bi >= 1:
            E.w(sy, E.vv, M[("p2evac", bi - 1)])
            store_O(bi - 1)
    E.w(sy, E.vv, M[("p2evac", B * NI - 1)])
    store_O(B * NI - 1)
    M["p2sos0_end"] = E.slots["sos0"].n
    M["p2sos1_end"] = E.slots["sos1"].n


def prog_gpsimd_a(E, R):
    g, M = E.g, E.M
    g.memset(R.zb[:, :], 0.0)
    g.memset(R.epsb[:, :], EPS)
    g.memset(R.ones[:, :], 1.0)
    M["p1memset"] = E.gc.n
    gw = E.slots["gw"]
    for src, dst, c in ((R.dwqk, R.wqk, 256), (R.dwv, R.wv, 128)):
        gw.inc(g.dma_start(out=dst[:, :].rearrange("p (c f) -> p c f", c=8),
                           in_=src.ap().rearrange("(c p) f -> p c f", c=8)))
    gw.inc(g.dma_start(out=R.bqk[:, :], in_=R.dbqk.ap()))
    gw.inc(g.dma_start(out=R.bv[:, :], in_=R.dbv.ap()))
    gw.inc(g.dma_start(out=R.permb[:, :], in_=R.dperm.ap()))
    gw.inc(g.dma_start(out=R.idnb[:, :], in_=R.didn.ap()))
    M["p1w"] = gw.n
    for ot in range(NOT):
        bsel = ot % 2
        gt = ot * OT
        pos = gt % N
        if ot >= 2:
            E.w(g, E.ss, M[("p1norm", ot - 2)])
            E.w(g, E.vv, M[("p1stat", ot - 2)])
            E.w(g, E.vv, M[("p1stg", ot - 2, 1)])
        gx = E.slots["gx%d" % bsel]
        gt_s = E.slots["gt%d" % bsel]
        gx.inc(g.dma_start(
            out=R.xts[bsel][:, :].rearrange("p (s c) -> p s c", s=4),
            in_=R.dxf.ap()[gt:gt + 512, :].rearrange("(s p) c -> p s c", s=4)))
        gt_s.inc(g.dma_start(out=R.tcs[bsel][:, :], in_=R.dtblc.ap()[:, pos:pos + 512]))
        gt_s.inc(g.dma_start(out=R.tss[bsel][:, :], in_=R.dtbls.ap()[:, pos:pos + 512]))
        M[("p1x", ot)] = (gx, gx.n)
        M[("p1tb", ot)] = (gt_s, gt_s.n)
    if STAGE < 2:
        M["a2a"] = E.cc.n
        return
    E.w(g, E.slots["sos0"], M["p2sos0_end"])
    E.w(g, E.slots["sos1"], M["p2sos1_end"])
    if E.em:
        with allow_ici_all_to_all():
            E.cc.inc(g.collective_compute(
                "AllToAll", ALU.bypass, replica_groups=[list(range(8))],
                ins=[R.A2I.ap().opt()], outs=[R.A2O.ap().opt()]))
    else:
        E.cc.inc(_DummyInst())
    M["a2a"] = E.cc.n


def prog_tensor_a(E, R):
    t, M = E.t, E.M
    E.w(t, E.slots["gw"], M["p1w"])
    E.w(t, E.gc, M["p1memset"])
    QKb = [R.S1p[0][:, :], R.S1p[1][:, :]]
    SWp = R.S2p[1][:, :]
    Vpp = R.S2p[0][:, :]
    for ot in range(NOT):
        bsel = ot % 2
        E.w(t, E.ss, M[("p1norm", ot)])
        if ot >= 1:
            pass
        for ccn in range(8):
            tp = R.TPq[ccn % 2][:, 0:512].bitcast(BF16) if False else R.TPq[ccn % 2]
            gcc = ot * 8 + ccn
            if gcc >= 2:
                ev = M[("p1tpev", gcc - 2)]
                if ev:
                    E.w(t, ev[0], ev[1])
            for sb in range(4):
                inst = t.transpose(tp[:, sb * 128:(sb + 1) * 128],
                                   R.xnb[bsel][:, sb * D + ccn * 128: sb * D + ccn * 128 + 128],
                                   R.idnb[:, :])
            E.tt.inc(inst)
            M[("p1tp", ot, ccn)] = E.tt.n
        E.w(t, E.vv, M[("p1trev_v", ot)])
        E.w(t, E.ss, M[("p1trev_s", ot)])
        for h in range(2):
            ft = ot * 2 + h
            if ft >= 2:
                E.w(t, E.vv, M[("p1stg", (ft - 2) // 2, (ft - 2) % 2)])
            for ccn in range(8):
                t.matmul(QKb[h], R.wqk[:, ccn * 256 + h * 128: ccn * 256 + h * 128 + 128],
                         R.xnT[:, ccn * OT: ccn * OT + 512], start=(ccn == 0), stop=False)
            E.tt.inc(t.matmul(QKb[h], R.bqk[:, h * 128:(h + 1) * 128], R.ones[:, 0:512],
                              start=False, stop=True))
            M[("p1qk", ot, h)] = E.tt.n
        for h in range(2):
            ft = ot * 2 + h
            sp = ft % 2
            E.w(t, E.vv, M[("p1E", ot, h)])
            if h == 1:
                E.w(t, E.vv, M[("p1ropeA", ot, 0)])
            E.tt.inc(t.matmul(SWp, R.permb[:, :], R.stg[sp][:, 512:1024],
                              start=True, stop=True))
            M[("p1sw", ot, h)] = E.tt.n
        if ot >= 1:
            E.w(t, E.vv, M[("p1vev", ot - 1)])
        for sb in range(4):
            for ccn in range(8):
                t.matmul(Vpp[:, sb * 128:(sb + 1) * 128],
                         R.xnT[:, ccn * OT + sb * 128: ccn * OT + sb * 128 + 128],
                         R.wv[:, ccn * 128:(ccn + 1) * 128], start=(ccn == 0), stop=False)
            inst = t.matmul(Vpp[:, sb * 128:(sb + 1) * 128], R.ones[:, 0:128], R.bv[:, :],
                            start=False, stop=True)
        E.tt.inc(inst)
        M[("p1v", ot)] = E.tt.n
    if STAGE < 2:
        return
    E.w(t, E.slots["sg0"], M["p1sg0_end"])
    E.w(t, E.slots["sg1"], M["p1sg1_end"])
    E.w(t, E.vv, M["p1vv_end"])
    E.w(t, E.ss, M["p1ss_end"])
    ACCap = [R.TPq[0][:, :].bitcast(F32), R.TPq[1][:, :].bitcast(F32)]
    u = 0
    for bi in range(B * NI):
        b, I = bi // NI, bi % NI
        buf = bi % 2
        qtm = M[("p2qt", bi)]
        if qtm:
            E.w(t, qtm[0], qtm[1])
        for h in range(2):
            hb = h * T + b * N
            for J in range(NJ):
                p = u % 3
                if u >= 3:
                    E.w(t, E.vv, M[("p2w", u - 3)])
                t.matmul(R.S1p[p][:, :], R.KK[0:64, hb + J * 128: hb + J * 128 + 128],
                         R.Qt[buf][0:64, h * 512:(h + 1) * 512],
                         start=True, stop=True, tile_position=(0, 0))
                E.tt.inc(t.matmul(R.S2p[p][:, :], R.KK[64:128, hb + J * 128: hb + J * 128 + 128],
                                  R.Qt[buf][64:128, h * 512:(h + 1) * 512],
                                  start=True, stop=True, tile_position=(64, 0)))
                M[("p2s", u)] = E.tt.n
                if J >= 2:
                    _mm3(E, R, ACCap, b, h, J - 2, u - 2, bi)
                u += 1
            _mm3(E, R, ACCap, b, h, NJ - 2, u - 2, bi)
            _mm3(E, R, ACCap, b, h, NJ - 1, u - 1, bi)
        M[("p2tte", bi)] = E.tt.n
    M["p2tt_end"] = E.tt.n


def _mm3(E, R, ACCap, b, h, J, u, bi):
    t, M = E.t, E.M
    if J == 0 and bi >= 1:
        E.w(t, E.vv, M[("p2evac", bi - 1)])
    E.w(t, E.vv, M[("p2w", u)])
    vcol = (b * 64 + J) * 128 + h * 64
    if h == 0:
        inst = t.matmul(ACCap[0][0:64, :], R.Vt[:, vcol:vcol + 64], R.WT[u % 3][:, :],
                        start=(J == 0), stop=(J == NJ - 1))
    else:
        inst = t.matmul(ACCap[1][64:128, :], R.Vt[:, vcol:vcol + 64], R.WT[u % 3][:, :],
                        start=(J == 0), stop=(J == NJ - 1), tile_position=(0, 64))
    if J == NJ - 1:
        E.tt.inc(inst)


def prog_vector_a(E, R):
    v, M = E.v, E.M
    QKb = [R.S1p[0][:, :], R.S1p[1][:, :]]
    SWp = R.S2p[1][:, :]
    Vpp = R.S2p[0][:, :]
    for ot in range(NOT):
        bsel = ot % 2
        xm = M[("p1x", ot)]
        if xm:
            E.w(v, xm[0], xm[1])
        for sb in range(4):
            inst = v.reduce_sum(R.sums[:, ot * 4 + sb: ot * 4 + sb + 1],
                                R.xts[bsel][:, sb * D:(sb + 1) * D], axis=AXX)
        M[("p1stat", ot)] = E.vv.n
        E.w(v, E.ss, M[("p1sq", ot)])
        c4 = slice(ot * 4, ot * 4 + 4)
        v.tensor_scalar(R.lnt[:, 0:4], R.sums[:, c4], 1.0 / D, None, ALU.mult)
        v.tensor_scalar(R.lnt[:, 4:8], R.sumsq[:, c4], 1.0 / D, None, ALU.mult)
        v.tensor_tensor(R.lnt[:, 8:12], R.lnt[:, 0:4], R.lnt[:, 0:4], ALU.mult)
        v.tensor_tensor(R.lnt[:, 12:16], R.lnt[:, 4:8], R.lnt[:, 8:12], ALU.subtract)
        M[("p1var", ot)] = E.vv.n
        E.w(v, E.ss, M[("p1rstd", ot)])
        v.tensor_scalar(R.lnt[:, 20:24], R.lnt[:, 0:4], -1.0, None, ALU.mult)
        v.tensor_tensor(R.lnt[:, 20:24], R.lnt[:, 20:24], R.lnt[:, 16:20], ALU.mult)
        M[("p1negmr", ot)] = E.vv.n
        for ccn in range(0, 8, 2):
            E.w(v, E.tt, M[("p1tp", ot, ccn)])
            if ot >= 1 and ccn == 0:
                E.w(v, E.tt, M[("p1v", ot - 1)])
            v.tensor_copy(R.xnT[:, ccn * OT: ccn * OT + 512], R.TPq[ccn % 2][:, 0:512])
            M[("p1tpev", ot * 8 + ccn)] = (E.vv, E.vv.n)
        M[("p1trev_v", ot)] = E.vv.n
        for h in range(2):
            ft = ot * 2 + h
            sp = ft % 2
            E.w(v, E.tt, M[("p1qk", ot, h)])
            if ft >= 1:
                E.w(v, E.ss, M[("p1exp", *divmod(ft - 1, 2))])
            v.tensor_scalar_min(R.es[:, :], QKb[h], 0.0)
            M[("p1min", ot, h)] = E.vv.n
            E.w(v, E.ss, M[("p1exp", ot, h)])
            if ft >= 2:
                sgm = M[("p1syn_stg", (ft - 2) // 2, (ft - 2) % 2)]
                if sgm:
                    E.w(v, sgm[0], sgm[1])
            v.scalar_tensor_tensor(R.stg[sp][:, 512:1024], QKb[h], 0.0,
                                            R.expm[:, :], ALU.max, ALU.add)
            M[("p1E", ot, h)] = E.vv.n
            E.w(v, E.tt, M[("p1sw", ot, h)])
            tbm = M[("p1tb", ot)]
            if tbm:
                E.w(v, tbm[0], tbm[1])
            v.tensor_tensor(R.asb[:, :], SWp, R.tss[bsel][:, :], ALU.mult)
            M[("p1ropeA", ot, h)] = E.vv.n
            v.tensor_tensor(R.bsb[:, :], R.stg[sp][:, 512:1024], R.tcs[bsel][:, :], ALU.mult)
            v.tensor_tensor(R.stg[sp][:, 0:512], R.asb[:, :], R.bsb[:, :], ALU.add)
            M[("p1stg", ot, h)] = E.vv.n
        E.w(v, E.tt, M[("p1v", ot)])
        v.tensor_copy(R.Vt[:, ot * 512:(ot + 1) * 512], Vpp)
        M[("p1vev", ot)] = E.vv.n
    M["p1vv_end"] = E.vv.n
    if STAGE < 2:
        return
    ACCap = [R.TPq[0][:, :].bitcast(F32), R.TPq[1][:, :].bitcast(F32)]
    u = 0
    E.vv.lag = 3
    for bi in range(B * NI):
        for h in range(2):
            for J in range(NJ):
                p = u % 3
                E.w(v, E.ss, M[("p2r", u)])
                v.tensor_tensor(R.WT[p][:, :], R.S1p[p][:, :], R.rp[p][:, :], ALU.mult)
                M[("p2w", u)] = E.vv.n
                u += 1
        E.w(v, E.tt, M[("p2tte", bi)])
        if bi >= 2:
            osm = M[("p2ost", bi - 2)]
            if osm:
                E.w(v, osm[0], osm[1])
        v.tensor_copy(R.Ost[bi % 2][0:64, :], ACCap[0][0:64, 0:512])
        v.tensor_copy(R.Ost[bi % 2][64:128, :], ACCap[1][64:128, 0:512])
        M[("p2evac", bi)] = E.vv.n
    E.vv.lag = 1


def prog_scalar_a(E, R):
    s, M = E.s, E.M
    for ot in range(NOT):
        bsel = ot % 2
        xm = M[("p1x", ot)]
        if xm:
            E.w(s, xm[0], xm[1])
        E.w(s, E.gc, M["p1memset"])
        for sb in range(4):
            inst = s.activation(R.sqsc[:, :], R.xts[bsel][:, sb * D:(sb + 1) * D],
                                AF.Square, bias=R.zb[:, :],
                                accum_out=R.sumsq[:, ot * 4 + sb: ot * 4 + sb + 1])
        M[("p1sq", ot)] = E.ss.n
        E.w(s, E.vv, M[("p1var", ot)])
        s.activation(R.lnt[:, 16:20], R.lnt[:, 12:16], AF.Sqrt, bias=R.epsb[:, :])
        act_raw(E, s, R.lnt[:, 16:20], R.lnt[:, 16:20], AF.Reciprocal, R.zb[:, :])
        M[("p1rstd", ot)] = E.ss.n
        E.w(s, E.vv, M[("p1negmr", ot)])
        if ot >= 1:
            E.w(s, E.tt, M[("p1tp", ot - 1, 7)])
        for sb in range(4):
            inst = s.activation(R.xnb[bsel][:, sb * D:(sb + 1) * D],
                                R.xts[bsel][:, sb * D:(sb + 1) * D], AF.Identity,
                                bias=R.lnt[:, 20 + sb:21 + sb], scale=R.lnt[:, 16 + sb:17 + sb])
        M[("p1norm", ot)] = E.ss.n
        for ccn in range(1, 8, 2):
            E.w(s, E.tt, M[("p1tp", ot, ccn)])
            if ot >= 1 and ccn == 1:
                E.w(s, E.tt, M[("p1v", ot - 1)])
            s.copy(R.xnT[:, ccn * OT: ccn * OT + 512], R.TPq[ccn % 2][:, 0:512])
            M[("p1tpev", ot * 8 + ccn)] = (E.ss, E.ss.n)
        M[("p1trev_s", ot)] = E.ss.n
        for h in range(2):
            E.w(s, E.vv, M[("p1min", ot, h)])
            if ot * 2 + h >= 1:
                E.w(s, E.vv, M[("p1E", *divmod(ot * 2 + h - 1, 2))])
            s.activation(R.expm[:, :], R.es[:, :], AF.Exp, bias=R.zb[:, :])
            M[("p1exp", ot, h)] = E.ss.n
    M["p1ss_end"] = E.ss.n
    if STAGE < 2:
        return
    u = 0
    E.ss.lag = 3
    for bi in range(B * NI):
        for h in range(2):
            for J in range(NJ):
                p = u % 3
                E.w(s, E.tt, M[("p2s", u)])
                if u >= 3:
                    E.w(s, E.vv, M[("p2w", u - 3)])
                act_raw(E, s, R.rp[p][:, :], R.S2p[p][:, :], AF.Reciprocal, R.zb[:, :])
                M[("p2r", u)] = E.ss.n
                u += 1
    E.ss.lag = 1



def prog_gpsimd_b(E, R):
    g, M = E.g, E.M
    E.w(g, E.cc, M["a2a"])
    g.memset(R.zb3[:, :], 0.0)
    g.memset(R.epsb3[:, :], EPS)
    g.memset(R.ones3[:, :], 1.0)
    M["p3memset"] = E.gc.n
    gw3 = E.slots["gw3"]
    for src, dst in ((R.dwout, R.wouts), (R.dwr, R.wrs), (R.dwk, R.wks), (R.dwv2, R.wv2s)):
        gw3.inc(g.dma_start(out=dst[:, :].rearrange("p (c f) -> p c f", c=8),
                            in_=src.ap().rearrange("(c p) f -> p c f", c=8)))
    gw3.inc(g.dma_start(out=R.bouts[:, :], in_=R.dbout.ap()))
    gw3.inc(g.dma_start(out=R.idn3[:, :], in_=R.didn.ap()))
    gw3.inc(g.dma_start(out=R.hmm[:, :], in_=R.dhm.ap()))
    gw3.inc(g.dma_start(out=R.murs[:, :], in_=R.dmur.ap()))
    gw3.inc(g.dma_start(out=R.muks[:, :], in_=R.dmuk.ap()))
    M["p3w"] = gw3.n
    if P3SUB < 2:
        return
    E.w(g, E.slots["shgi"], M["p3hgi"])
    if E.em:
        E.cc.inc(g.collective_compute(
            "AllGather", ALU.bypass, replica_groups=[list(range(8))],
            ins=[R.HGI.ap().opt()], outs=[R.HGO.ap().opt()]))
    else:
        E.cc.inc(_DummyInst())
    M["hag"] = E.cc.n
    E.w(g, E.cc, M["hag"])
    E.slots["ghg"].inc(g.dma_start(out=R.hgos[:, :], in_=R.HGO.ap()))
    M["p3hgo"] = E.slots["ghg"].n


def prog_sync_b(E, R):
    sy, M = E.sy, E.M
    E.w(sy, E.cc, M["a2a"])
    for it in range(NT3):
        buf = it % 2
        if it >= 2:
            E.w(sy, E.tt, M[("p3op", it - 2)])
        sot = E.slots["sot%d" % buf]
        sot.inc(sy.dma_start(
            out=R.OTt[buf][:, :].rearrange("p (c t) -> p c t", c=8),
            in_=R.A2O.ap().rearrange("(c p) t -> p c t", c=8)[:, :, it * 128:(it + 1) * 128]))
        M[("p3ot", it)] = (sot, sot.n)
        if it >= 2:
            E.w(sy, E.vv, M[("p3xr1", it - 2)])
        sxw = E.slots["sxw%d" % buf]
        sxw.inc(sy.dma_start(out=R.xw[buf][:, :], in_=R.dxw.ap()[it * 128:(it + 1) * 128, :]))
        M[("p3xw", it)] = (sxw, sxw.n)
    if P3SUB < 2:
        E.w(sy, E.vv, M[("p3xn3", NT3 - 1)])
        for half in range(2):
            ev = M[("p3t3ev", (NT3 - 1) * 2 + half)]
            if ev:
                E.w(sy, ev[0], ev[1])
        return
    E.w(sy, E.vv, M[("p3xn3", NT3 - 1)])
    E.slots["shgi"].inc(sy.dma_start(out=R.HGI.ap(), in_=R.xn3b[127:128, :]))
    M["p3hgi"] = E.slots["shgi"].n
    if P3SUB < 5:
        return
    for it in range(NT3):
        buf = it % 2
        if it < 2:
            E.w(sy, E.vv, M[("p3xr1", NT3 - 2 + it)])
        else:
            E.w(sy, E.vv, M[("p3y", it - 2)])
        sxw = E.slots["sxw%d" % buf]
        sxw.inc(sy.dma_start(out=R.xw[buf][:, :], in_=R.dxw.ap()[it * 128:(it + 1) * 128, :]))
        M[("p3xw2", it)] = (sxw, sxw.n)
        E.w(sy, E.vv, M[("p3y", it)])
        syt = E.slots["sy%d" % (it % 2)]
        syt.inc(sy.dma_start(out=R.dy.ap()[it * 128:(it + 1) * 128, :], in_=R.ysb[it % 2][:, :]))
        M[("p3yst", it)] = (syt, syt.n)
    if E.em:
        sy.wait_ge(E.slots["sy0"].sem, E.slots["sy0"].n)
        sy.wait_ge(E.slots["sy1"].sem, E.slots["sy1"].n)


def prog_tensor_b(E, R):
    t, M = E.t, E.M
    E.w(t, E.slots["gw3"], M["p3w"])
    E.w(t, E.gc, M["p3memset"])
    for it in range(NT3):
        buf = it % 2
        if not (P3OPS & 1):
            break
        otm = M[("p3ot", it)]
        if otm:
            E.w(t, otm[0], otm[1])
        if it >= 1:
            E.w(t, E.ss, M[("p3pev", it - 1)])
            E.w(t, E.vv, M[("p3ps", it - 1)])
        for fh in range(2):
            for ccn in range(8):
                t.matmul(R.Pp[:, fh * 512:(fh + 1) * 512],
                         R.OTt[buf][:, ccn * 128:(ccn + 1) * 128],
                         R.wouts[:, ccn * D + fh * 512: ccn * D + fh * 512 + 512],
                         start=(ccn == 0), stop=False)
            inst = t.matmul(R.Pp[:, fh * 512:(fh + 1) * 512], R.ones3[:, 0:128],
                            R.bouts[:, fh * 512:(fh + 1) * 512], start=False, stop=True)
        E.tt.inc(inst)
        M[("p3op", it)] = E.tt.n
        if not (P3OPS & 32):
            continue
        E.w(t, E.vv, M[("p3xn3", it)])
        for half in range(2):
            tp = R.TP3[half]
            ghalf = it * 2 + half
            if ghalf >= 2:
                ev = M[("p3t3ev", ghalf - 2)]
                if ev:
                    E.w(t, ev[0], ev[1])
            for q in range(4):
                ccn = half * 4 + q
                inst = t.transpose(tp[:, q * 128:(q + 1) * 128],
                                   R.xn3b[:, ccn * 128:(ccn + 1) * 128], R.idn3[:, :])
            E.tt.inc(inst)
            M[("p3t3", it, half)] = E.tt.n
    if P3SUB < 2:
        return
    E.w(t, E.slots["ghg"], M["p3hgo"])
    for fh in range(2):
        E.tt.inc(t.matmul(R.HP[0:1, :], R.hmm[:, :], R.hgos[:, fh * 512:(fh + 1) * 512],
                          start=True, stop=True))
        M[("p3hp", fh)] = E.tt.n
        E.w(t, E.ss, M[("p3hev", fh)])
    E.w(t, E.ss, M[("p3hev", 1)])
    for ccn in range(8):
        inst = t.transpose(R.HS[:, 2 * ccn:2 * ccn + 1], R.halo[0:1, ccn * 128:(ccn + 1) * 128],
                           R.idn3[0:1, 0:1])
    E.tt.inc(inst)
    M["p3hst"] = E.tt.n
    if P3SUB < 4:
        return
    for ts4 in range(4):
        for dt_ in range(8):
            u = ts4 * 8 + dt_
            p = u % 2
            if u >= 2:
                E.w(t, E.ss, M[("p3kact", u - 2)])
            E.w(t, E.vv, M[("p3mixk", ts4)])
            for ccn in range(8):
                inst = t.matmul(R.Kp[p][:, :],
                                R.wks[:, ccn * D + dt_ * 128: ccn * D + dt_ * 128 + 128],
                                R.xmkT[:, ccn * 512:(ccn + 1) * 512],
                                start=(ccn == 0), stop=(ccn == 7))
            E.tt.inc(inst)
            M[("p3k", u)] = E.tt.n
        if P3SUB < 5:
            continue
        for q in range(4):
            it = ts4 * 4 + q
            E.w(t, E.vv, M[("p3mixr", ts4)])
            if it >= 1:
                E.w(t, E.ss, M[("p3sig", it - 1)])
                E.w(t, E.vv, M[("p3ff", it - 1)])
            for fh in range(2):
                for ccn in range(8):
                    inst = t.matmul(R.Pp[:, fh * 512:(fh + 1) * 512],
                                    R.xmrT[:, ccn * 512 + q * 128: ccn * 512 + q * 128 + 128],
                                    R.wrs[:, ccn * D + fh * 512: ccn * D + fh * 512 + 512],
                                    start=(ccn == 0), stop=(ccn == 7))
            E.tt.inc(inst)
            M[("p3r", it)] = E.tt.n
            E.w(t, E.ss, M[("p3sig", it)])
            E.w(t, E.ss, M[("p3kact", ts4 * 8 + 7)])
            for fh in range(2):
                for ccn in range(8):
                    inst = t.matmul(R.Pp[:, fh * 512:(fh + 1) * 512],
                                    R.kactS[:, ccn * 512 + q * 128: ccn * 512 + q * 128 + 128],
                                    R.wv2s[:, ccn * D + fh * 512: ccn * D + fh * 512 + 512],
                                    start=(ccn == 0), stop=(ccn == 7))
            E.tt.inc(inst)
            M[("p3vo", it)] = E.tt.n


def prog_vector_b(E, R):
    v, M = E.v, E.M
    E.w(v, E.slots["gw3"], M["p3w"])
    v.tensor_scalar(R.omurs[:, :], R.murs[:, :], -1.0, 1.0, ALU.mult, ALU.add)
    v.tensor_scalar(R.omuks[:, :], R.muks[:, :], -1.0, 1.0, ALU.mult, ALU.add)
    for it in range(NT3):
        if not (P3OPS & 2):
            break
        E.w(v, E.tt, M[("p3op", it)])
        if VB_MODE == 0:
            v.reduce_sum(R.l3[:, 0:1], R.Pp[:, :], axis=AXX)
        elif VB_MODE == 1:
            v.reduce_sum(R.l3[:, 0:1], R.Pp[:, 0:512], axis=AXX)
        elif VB_MODE == 2:
            v.tensor_copy(R.l3[:, 0:8], R.Pp[:, 0:8])
        else:
            v.tensor_copy(R.l3[:, 0:8], R.xw[0][:, 0:8])
        M[("p3ps", it)] = E.vv.n
        if not (P3OPS & 8):
            continue
        E.w(v, E.ss, M[("p3pev", it)])
        v.tensor_scalar(R.l3[:, 2:3], R.l3[:, 0:1], 1.0 / D, None, ALU.mult)
        v.tensor_scalar(R.l3[:, 3:4], R.l3[:, 1:2], 1.0 / D, None, ALU.mult)
        v.tensor_tensor(R.l3[:, 4:5], R.l3[:, 2:3], R.l3[:, 2:3], ALU.mult)
        v.tensor_tensor(R.l3[:, 5:6], R.l3[:, 3:4], R.l3[:, 4:5], ALU.subtract)
        M[("p3var2", it)] = E.vv.n
        E.w(v, E.ss, M[("p3rs2", it)])
        v.tensor_scalar(R.ln2o[:, it * D:(it + 1) * D], R.Psb[:, :],
                                 R.l3[:, 2:3], R.l3[:, 6:7], ALU.subtract, ALU.mult)
        M[("p3ln2", it)] = E.vv.n
        if not (P3OPS & 16):
            continue
        xwm = M[("p3xw", it)]
        if xwm:
            E.w(v, xwm[0], xwm[1])
        v.tensor_tensor(R.xr1[:, :], R.xw[it % 2][:, :],
                                 R.ln2o[:, it * D:(it + 1) * D], ALU.add)
        M[("p3xr1", it)] = E.vv.n
        v.reduce_sum(R.l3[:, 8:9], R.xr1[:, :], axis=AXX)
        E.w(v, E.ss, M[("p3sq3", it)])
        v.tensor_scalar(R.l3[:, 10:11], R.l3[:, 8:9], 1.0 / D, None, ALU.mult)
        v.tensor_scalar(R.l3[:, 11:12], R.l3[:, 9:10], 1.0 / D, None, ALU.mult)
        v.tensor_tensor(R.l3[:, 12:13], R.l3[:, 10:11], R.l3[:, 10:11], ALU.mult)
        v.tensor_tensor(R.l3[:, 13:14], R.l3[:, 11:12], R.l3[:, 12:13], ALU.subtract)
        M[("p3var3", it)] = E.vv.n
        E.w(v, E.ss, M[("p3rs3", it)])
        if it >= 1:
            E.w(v, E.tt, M[("p3t3", it - 1, 1)])
        v.tensor_scalar(R.xn3b[:, :], R.xr1[:, :],
                                 R.l3[:, 10:11], R.l3[:, 14:15], ALU.subtract, ALU.mult)
        M[("p3xn3", it)] = E.vv.n
        for half in range(2):
            E.w(v, E.tt, M[("p3t3", it, half)])
            dst = R.xn3T[:, :].rearrange("p (c w) -> p c w", c=8)[:, half * 4:half * 4 + 4,
                                                                 it * 128:it * 128 + 128]
            v.tensor_copy(dst, R.TP3[half][:, :].rearrange("p (c w) -> p c w", c=4))
            M[("p3t3ev", it * 2 + half)] = (E.vv, E.vv.n)
    if P3SUB < 2:
        return
    E.w(v, E.tt, M["p3hst"])
    v.tensor_copy(R.haloT[:, :], R.HS[:, 0:16:2])
    M["p3halot"] = E.vv.n
    if P3SUB < 3:
        return
    for ts4 in range(4):
        t0 = ts4 * 512
        for mu, omu, dst, mk in ((R.murs, R.omurs, R.xmrT, "p3mixr"), (R.muks, R.omuks, R.xmkT, "p3mixk")):
            if ts4 == 0:
                E.w(v, E.vv, M["p3halot"])
            if ts4 >= 1:
                E.w(v, E.tt, M[("p3r", ts4 * 4 - 1)] if mk == "p3mixr" else M[("p3k", (ts4 - 1) * 8 + 7)])
            for ccn in range(8):
                cbase = ccn * WIN
                if t0 == 0:
                    v.tensor_scalar(R.mixs[:, 1:512], R.xn3T[:, cbase: cbase + 511],
                                    omu[:, ccn:ccn + 1], None, ALU.mult)
                    v.tensor_scalar(R.mixs[:, 0:1], R.haloT[:, ccn:ccn + 1],
                                    omu[:, ccn:ccn + 1], None, ALU.mult)
                else:
                    v.tensor_scalar(R.mixs[:, 0:512], R.xn3T[:, cbase + t0 - 1: cbase + t0 + 511],
                                    omu[:, ccn:ccn + 1], None, ALU.mult)
                inst = v.scalar_tensor_tensor(dst[:, ccn * 512:(ccn + 1) * 512],
                                              R.xn3T[:, cbase + t0: cbase + t0 + 512],
                                              mu[:, ccn:ccn + 1], R.mixs[:, 0:512],
                                              ALU.mult, ALU.add)
            M[(mk, ts4)] = E.vv.n
        if P3SUB < 4:
            continue
        for dt_ in range(8):
            u = ts4 * 8 + dt_
            p = u % 2
            E.w(v, E.tt, M[("p3k", u)])
            if u >= 1:
                E.w(v, E.ss, M[("p3kact", u - 1)])
            v.tensor_scalar_max(R.kpre[:, :], R.Kp[p][:, :], 0.0)
            M[("p3relu", u)] = E.vv.n
        if P3SUB < 5:
            continue
        for q in range(4):
            it = ts4 * 4 + q
            E.w(v, E.tt, M[("p3vo", it)])
            if it >= 1:
                E.w(v, E.ss, M[("p3sq4", it - 1)])
            v.tensor_tensor(R.ffo[:, :], R.Pp[:, :], R.rsig[:, :], ALU.mult)
            M[("p3ff", it)] = E.vv.n
            v.reduce_sum(R.l3[:, 16:17], R.ffo[:, :], axis=AXX)
            E.w(v, E.ss, M[("p3sq4", it)])
            v.tensor_scalar(R.l3[:, 18:19], R.l3[:, 16:17], 1.0 / D, None, ALU.mult)
            v.tensor_scalar(R.l3[:, 19:20], R.l3[:, 17:18], 1.0 / D, None, ALU.mult)
            v.tensor_tensor(R.l3[:, 20:21], R.l3[:, 18:19], R.l3[:, 18:19], ALU.mult)
            v.tensor_tensor(R.l3[:, 21:22], R.l3[:, 19:20], R.l3[:, 20:21], ALU.subtract)
            M[("p3var4", it)] = E.vv.n
            E.w(v, E.ss, M[("p3rs4", it)])
            v.tensor_scalar(R.ln4o[:, :], R.ffo[:, :], R.l3[:, 18:19], R.l3[:, 22:23],
                            ALU.subtract, ALU.mult)
            xwm = M[("p3xw2", it)]
            if xwm:
                E.w(v, xwm[0], xwm[1])
            if it >= 2:
                ym = M[("p3yst", it - 2)]
                if ym:
                    E.w(v, ym[0], ym[1])
            v.tensor_tensor(R.ysb[it % 2][:, :], R.xw[it % 2][:, :],
                            R.ln2o[:, it * D:(it + 1) * D], ALU.add)
            v.tensor_tensor(R.ysb[it % 2][:, :], R.ysb[it % 2][:, :], R.ln4o[:, :], ALU.add)
            M[("p3y", it)] = E.vv.n


def prog_scalar_b(E, R):
    s, M = E.s, E.M
    for it in range(NT3):
        if not (P3OPS & 4):
            break
        E.w(s, E.tt, M[("p3op", it)])
        E.w(s, E.vv, M[("p3ps", it)])
        if it >= 1:
            E.w(s, E.vv, M[("p3xn3", it - 1)])
        s.copy(R.Psb[:, :], R.Pp[:, :])
        s.activation(R.sq3[:, :], R.Pp[:, :], AF.Square, bias=R.zb3[:, :],
                              accum_out=R.l3[:, 1:2])
        M[("p3pev", it)] = E.ss.n
        if not (P3OPS & 8):
            continue
        E.w(s, E.vv, M[("p3var2", it)])
        s.activation(R.l3[:, 6:7], R.l3[:, 5:6], AF.Sqrt, bias=R.epsb3[:, :])
        act_raw(E, s, R.l3[:, 6:7], R.l3[:, 6:7], AF.Reciprocal, R.zb3[:, :])
        M[("p3rs2", it)] = E.ss.n
        E.w(s, E.vv, M[("p3xr1", it)])
        s.activation(R.sq3[:, :], R.xr1[:, :], AF.Square, bias=R.zb3[:, :],
                              accum_out=R.l3[:, 9:10])
        M[("p3sq3", it)] = E.ss.n
        E.w(s, E.vv, M[("p3var3", it)])
        s.activation(R.l3[:, 14:15], R.l3[:, 13:14], AF.Sqrt, bias=R.epsb3[:, :])
        act_raw(E, s, R.l3[:, 14:15], R.l3[:, 14:15], AF.Reciprocal, R.zb3[:, :])
        M[("p3rs3", it)] = E.ss.n

    if P3SUB < 2:
        return
    for fh in range(2):
        E.w(s, E.tt, M[("p3hp", fh)])
        s.copy(R.halo[0:1, fh * 512:(fh + 1) * 512], R.HP[0:1, :])
        M[("p3hev", fh)] = E.ss.n
    if P3SUB < 4:
        return
    for ts4 in range(4):
        for dt_ in range(8):
            u = ts4 * 8 + dt_
            E.w(s, E.vv, M[("p3relu", u)])
            if dt_ == 0 and ts4 >= 1:
                E.w(s, E.tt, M[("p3vo", ts4 * 4 - 1)])
            s.activation(R.kactS[:, dt_ * 512:(dt_ + 1) * 512], R.kpre[:, :],
                                  AF.Square, bias=R.zb3[:, :])
            M[("p3kact", u)] = E.ss.n
        if P3SUB < 5:
            continue
        for q in range(4):
            it = ts4 * 4 + q
            E.w(s, E.tt, M[("p3r", it)])
            if it >= 1:
                E.w(s, E.vv, M[("p3ff", it - 1)])
            s.activation(R.rsig[:, :], R.Pp[:, :], AF.Sigmoid, bias=R.zb3[:, :])
            M[("p3sig", it)] = E.ss.n
            E.w(s, E.vv, M[("p3ff", it)])
            s.activation(R.sq3[:, :], R.ffo[:, :], AF.Square, bias=R.zb3[:, :],
                                  accum_out=R.l3[:, 17:18])
            M[("p3sq4", it)] = E.ss.n
            E.w(s, E.vv, M[("p3var4", it)])
            s.activation(R.l3[:, 22:23], R.l3[:, 21:22], AF.Sqrt, bias=R.epsb3[:, :])
            act_raw(E, s, R.l3[:, 22:23], R.l3[:, 22:23], AF.Reciprocal, R.zb3[:, :])
            M[("p3rs4", it)] = E.ss.n



class _R:
    pass


def build_nc():
    nc = bass.Bass("TRN2", target_bir_lowering=False)
    R = _R()
    R.dxf = nc.dram_tensor("x_full", [T, D], F32, kind="ExternalInput")
    R.dxw = nc.dram_tensor("x_win", [WIN, D], F32, kind="ExternalInput")
    R.dwqk = nc.dram_tensor("wqk", [D, 256], F32, kind="ExternalInput")
    R.dbqk = nc.dram_tensor("bqk", [1, 256], F32, kind="ExternalInput")
    R.dwv = nc.dram_tensor("wv", [D, 128], F32, kind="ExternalInput")
    R.dbv = nc.dram_tensor("bv", [1, 128], F32, kind="ExternalInput")
    R.dwout = nc.dram_tensor("wout", [D, D], F32, kind="ExternalInput")
    R.dbout = nc.dram_tensor("bout", [1, D], F32, kind="ExternalInput")
    R.dwr = nc.dram_tensor("wr_t", [D, D], F32, kind="ExternalInput")
    R.dwk = nc.dram_tensor("wk_t", [D, D], F32, kind="ExternalInput")
    R.dwv2 = nc.dram_tensor("wv2_t", [D, D], F32, kind="ExternalInput")
    R.dmur = nc.dram_tensor("mu_r", [128, 8], F32, kind="ExternalInput")
    R.dmuk = nc.dram_tensor("mu_k", [128, 8], F32, kind="ExternalInput")
    R.dtblc = nc.dram_tensor("tblc", [128, N], F32, kind="ExternalInput")
    R.dtbls = nc.dram_tensor("tbls", [128, N], F32, kind="ExternalInput")
    R.dperm = nc.dram_tensor("permm", [128, 128], F32, kind="ExternalInput")
    R.didn = nc.dram_tensor("idn", [128, 128], F32, kind="ExternalInput")
    R.dhm = nc.dram_tensor("hmask", [8, 1], F32, kind="ExternalInput")
    R.dy = nc.dram_tensor("y", [WIN, D], F32, kind="ExternalOutput")
    R.QD = nc.dram_tensor("QD", [128, 2 * T], BF16)
    R.A2I = nc.dram_tensor("a2i", [1024, WIN], BF16)
    R.A2O = nc.dram_tensor("a2o", [1024, WIN], BF16)
    R.HGI = nc.dram_tensor("hgi", [1, D], BF16)
    R.HGO = nc.dram_tensor("hgo", [8, D], BF16)

    marks = Marks()
    E = Env(nc, marks)

    with ExitStack() as top:
        E.sems(nc, top)
        with ExitStack() as pa:
            e = pa.enter_context
            R.KK = e(nc.sbuf_tensor("KK", [128, 2 * T], BF16))
            R.Vt = e(nc.sbuf_tensor("Vt", [128, T], BF16))
            R.wqk = e(nc.sbuf_tensor("wqk_s", [128, 8 * 256], BF16))
            R.wv = e(nc.sbuf_tensor("wv_s", [128, 8 * 128], BF16))
            R.bqk = e(nc.sbuf_tensor("bqk_s", [1, 256], BF16))
            R.bv = e(nc.sbuf_tensor("bv_s", [1, 128], BF16))
            R.ones = e(nc.sbuf_tensor("ones_s", [1, 512], BF16))
            R.permb = e(nc.sbuf_tensor("permb", [128, 128], BF16))
            R.idnb = e(nc.sbuf_tensor("idnb", [128, 128], BF16))
            R.zb = e(nc.sbuf_tensor("zb", [128, 1], F32))
            R.epsb = e(nc.sbuf_tensor("epsb", [128, 1], F32))
            R.xts = [e(nc.sbuf_tensor(f"xts{i}", [128, 4 * D], BF16)) for i in range(2)]
            R.sqsc = e(nc.sbuf_tensor("sqsc", [128, D], F32))
            R.xnb = [e(nc.sbuf_tensor(f"xnb{i}", [128, 4 * D], BF16)) for i in range(2)]
            R.xnT = e(nc.sbuf_tensor("xnT", [128, 8 * OT], BF16))
            R.sums = e(nc.sbuf_tensor("sums", [128, 4 * NOT], F32))
            R.sumsq = e(nc.sbuf_tensor("sumsq", [128, 4 * NOT], F32))
            R.lnt = e(nc.sbuf_tensor("lnt", [128, 24], F32))
            R.es = e(nc.sbuf_tensor("es", [128, 512], F32))
            R.expm = e(nc.sbuf_tensor("expm", [128, 512], F32))
            R.stg = [e(nc.sbuf_tensor(f"stg{i}", [128, 1024], BF16)) for i in range(2)]
            R.asb = e(nc.sbuf_tensor("asb", [128, 512], BF16))
            R.bsb = e(nc.sbuf_tensor("bsb", [128, 512], BF16))
            R.tcs = [e(nc.sbuf_tensor(f"tcs{i}", [128, 512], BF16)) for i in range(2)]
            R.tss = [e(nc.sbuf_tensor(f"tss{i}", [128, 512], BF16)) for i in range(2)]
            R.Qt = [e(nc.sbuf_tensor(f"Qt{i}", [128, 1024], BF16)) for i in range(2)]
            R.rp = [e(nc.sbuf_tensor(f"rp{i}", [128, 512], F32)) for i in range(3)]
            R.WT = [e(nc.sbuf_tensor(f"WT{i}", [128, 512], BF16)) for i in range(3)]
            R.Ost = [e(nc.sbuf_tensor(f"Ost{i}", [128, 512], BF16)) for i in range(2)]
            R.S1p = [e(nc.psum_tensor(f"S1p{i}", [128, 512], F32)) for i in range(3)]
            R.S2p = [e(nc.psum_tensor(f"S2p{i}", [128, 512], F32)) for i in range(3)]
            R.TPq = [e(nc.psum_tensor(f"TPq{i}", [128, 1024], BF16)) for i in range(2)]

            E.em = False
            E.reset()
            E.use_engines(t=_DummyEng(), v=_DummyEng(), s=_DummyEng(), g=_DummyEng(), sy=_DummyEng())
            for fn in (prog_sync_a, prog_gpsimd_a, prog_tensor_a, prog_vector_a, prog_scalar_a):
                fn(E, R)
            snapA = [c.n for c in E.ctrs]

            E._snapA = snapA

            Marks.strict = True
            E.em = True
            E.reset()
            with nc.Block() as blk:
                @blk.sync
                def _(sy):
                    E.use_engines(sy=sy)
                    prog_sync_a(E, R)

                @blk.gpsimd
                def _(g):
                    E.use_engines(g=g)
                    prog_gpsimd_a(E, R)

                @blk.tensor
                def _(t):
                    E.use_engines(t=t)
                    prog_tensor_a(E, R)

                @blk.vector
                def _(v):
                    E.use_engines(v=v)
                    prog_vector_a(E, R)

                @blk.scalar
                def _(s):
                    E.use_engines(s=s)
                    prog_scalar_a(E, R)
            assert [c.n for c in E.ctrs] == snapA, "pass A count mismatch"

        if STAGE < 3:
            return nc
        with ExitStack() as pb:
            e = pb.enter_context
            R.OTt = [e(nc.sbuf_tensor(f"OTt{i}", [128, 8 * 128], BF16)) for i in range(2)]
            R.ln2o = e(nc.sbuf_tensor("ln2o", [128, NT3 * D], BF16))
            R.xn3T = e(nc.sbuf_tensor("xn3T", [128, 8 * WIN], BF16))
            R.wouts = e(nc.sbuf_tensor("wouts", [128, 8 * D], BF16))
            R.bouts = e(nc.sbuf_tensor("bouts", [1, D], BF16))
            R.wrs = e(nc.sbuf_tensor("wrs", [128, 8 * D], BF16))
            R.wks = e(nc.sbuf_tensor("wks", [128, 8 * D], BF16))
            R.wv2s = e(nc.sbuf_tensor("wv2s", [128, 8 * D], BF16))
            R.murs = e(nc.sbuf_tensor("murs", [128, 8], F32))
            R.muks = e(nc.sbuf_tensor("muks", [128, 8], F32))
            R.omurs = e(nc.sbuf_tensor("omurs", [128, 8], F32))
            R.omuks = e(nc.sbuf_tensor("omuks", [128, 8], F32))
            R.zb3 = e(nc.sbuf_tensor("zb3", [128, 1], F32))
            R.epsb3 = e(nc.sbuf_tensor("epsb3", [128, 1], F32))
            R.ones3 = e(nc.sbuf_tensor("ones3", [1, 512], BF16))
            R.idn3 = e(nc.sbuf_tensor("idn3", [128, 128], BF16))
            R.hmm = e(nc.sbuf_tensor("hmm", [8, 1], BF16))
            R.hgos = e(nc.sbuf_tensor("hgos", [8, D], BF16))
            R.halo = e(nc.sbuf_tensor("halo", [1, D], BF16))
            R.haloT = e(nc.sbuf_tensor("haloT", [128, 8], BF16))
            R.xw = [e(nc.sbuf_tensor(f"xw{i}", [128, D], F32)) for i in range(2)]
            R.Psb = e(nc.sbuf_tensor("Psb", [128, D], F32))
            R.xr1 = e(nc.sbuf_tensor("xr1", [128, D], F32))
            R.xn3b = e(nc.sbuf_tensor("xn3b", [128, D], BF16))
            R.sq3 = e(nc.sbuf_tensor("sq3", [128, D], F32))
            R.ln4o = e(nc.sbuf_tensor("ln4o", [128, D], F32))
            R.l3 = e(nc.sbuf_tensor("l3", [128, 24], F32))
            R.mixs = e(nc.sbuf_tensor("mixs", [128, 512], BF16))
            R.xmrT = e(nc.sbuf_tensor("xmrT", [128, 8 * 512], BF16))
            R.xmkT = e(nc.sbuf_tensor("xmkT", [128, 8 * 512], BF16))
            R.kactS = e(nc.sbuf_tensor("kactS", [128, 8 * 512], BF16))
            R.kpre = e(nc.sbuf_tensor("kpre", [128, 512], BF16))
            R.rsig = e(nc.sbuf_tensor("rsig", [128, D], BF16))
            R.ffo = e(nc.sbuf_tensor("ffo", [128, D], F32))
            R.ysb = [e(nc.sbuf_tensor(f"ysb{i}", [128, D], F32)) for i in range(2)]
            R.Pp = e(nc.psum_tensor("Pp", [128, 1024], F32))
            R.Kp = [e(nc.psum_tensor(f"Kp{i}", [128, 512], F32)) for i in range(2)]
            R.TP3 = [e(nc.psum_tensor(f"TP3{i}", [128, 512], BF16)) for i in range(2)]
            R.HP = e(nc.psum_tensor("HP", [1, 512], F32))
            R.HS = e(nc.psum_tensor("HS", [128, 16], BF16))

            Marks.strict = False
            E.em = False
            for c, n0 in zip(E.ctrs, E._snapA):
                c.n = n0
            E.use_engines(t=_DummyEng(), v=_DummyEng(), s=_DummyEng(), g=_DummyEng(), sy=_DummyEng())
            for fn in (prog_gpsimd_b, prog_sync_b, prog_tensor_b, prog_vector_b, prog_scalar_b):
                fn(E, R)
            snapB = [c.n for c in E.ctrs]

            Marks.strict = True
            E.em = True
            for c, n0 in zip(E.ctrs, E._snapA):
                c.n = n0
            with nc.Block() as blk3:
                @blk3.gpsimd
                def _(g):
                    E.use_engines(g=g)
                    prog_gpsimd_b(E, R)

                @blk3.sync
                def _(sy):
                    E.use_engines(sy=sy)
                    prog_sync_b(E, R)

                @blk3.tensor
                def _(t):
                    E.use_engines(t=t)
                    prog_tensor_b(E, R)

                @blk3.vector
                def _(v):
                    E.use_engines(v=v)
                    prog_vector_b(E, R)

                @blk3.scalar
                def _(s):
                    E.use_engines(s=s)
                    prog_scalar_b(E, R)
            assert [c.n for c in E.ctrs] == snapB, "pass B count mismatch"

    return nc


def _host_prep(inputs):
    x = np.ascontiguousarray(np.asarray(inputs["L0_attn_x"], dtype=np.float32).reshape(T, D))
    wqkv = np.asarray(inputs["L0_attn_w_qkv"], dtype=np.float32)
    bqkv = np.asarray(inputs["L0_attn_b_qkv"], dtype=np.float32)
    wout = np.ascontiguousarray(np.asarray(inputs["L0_attn_w_out"], dtype=np.float32))
    bout = np.asarray(inputs["L0_attn_b_out"], dtype=np.float32).reshape(1, D)
    Wr = np.asarray(inputs["L0_ffn_W_r"], dtype=np.float32)
    Wk = np.asarray(inputs["L0_ffn_W_k"], dtype=np.float32)
    Wv2 = np.asarray(inputs["L0_ffn_W_v"], dtype=np.float32)
    mur = np.ascontiguousarray(np.asarray(inputs["L0_ffn_mu_r"], dtype=np.float32).reshape(8, 128).T)
    muk = np.ascontiguousarray(np.asarray(inputs["L0_ffn_mu_k"], dtype=np.float32).reshape(8, 128).T)

    perm = np.concatenate([np.arange(0, 64, 2), np.arange(1, 64, 2)])
    p32 = np.arange(32, dtype=np.float64)
    theta = 1.0 / (10000.0 ** (p32 / 32.0))
    pos = np.arange(N, dtype=np.float64)
    ang = pos[:, None] * theta[None, :]
    cosv = np.cos(ang).T
    sinv = np.sin(ang).T
    tblc = np.empty((128, N), dtype=np.float32)
    tbls = np.empty((128, N), dtype=np.float32)
    for g in range(4):
        r0 = g * 32
        tblc[r0:r0 + 32] = cosv
        if g % 2 == 0:
            tbls[r0:r0 + 32] = -sinv
        else:
            tbls[r0:r0 + 32] = sinv
    permm = np.zeros((128, 128), dtype=np.float32)
    for i in range(128):
        j = i + 32 if (i // 32) % 2 == 0 else i - 32
        permm[j, i] = 1.0
    idn = np.eye(128, dtype=np.float32)

    wr_t = np.ascontiguousarray(Wr.T)
    wk_t = np.ascontiguousarray(Wk.T)
    wv2_t = np.ascontiguousarray(Wv2.T)

    in_maps = []
    for c in range(NCORE):
        h0 = 2 * c
        qcols = []
        for h in (h0, h0 + 1):
            qcols.append(np.concatenate([
                wqkv[:, h * 64 + perm],
                wqkv[:, D + h * 64 + perm],
            ], axis=1))
        wqk = np.ascontiguousarray(np.concatenate(qcols, axis=1))
        bqk = np.concatenate([
            np.concatenate([bqkv[h * 64 + perm], bqkv[D + h * 64 + perm]])
            for h in (h0, h0 + 1)]).reshape(1, 256).astype(np.float32)
        wv = np.ascontiguousarray(wqkv[:, 2 * D + h0 * 64: 2 * D + h0 * 64 + 128])
        bv = bqkv[2 * D + h0 * 64: 2 * D + h0 * 64 + 128].reshape(1, 128).astype(np.float32)
        hm = np.zeros((8, 1), dtype=np.float32)
        if c not in (0, 4):
            hm[c - 1, 0] = 1.0
        in_maps.append({
            "x_full": x,
            "x_win": np.ascontiguousarray(x[c * WIN:(c + 1) * WIN]),
            "wqk": wqk, "bqk": np.ascontiguousarray(bqk),
            "wv": wv, "bv": np.ascontiguousarray(bv),
            "wout": wout, "bout": np.ascontiguousarray(bout),
            "wr_t": wr_t, "wk_t": wk_t, "wv2_t": wv2_t,
            "mu_r": mur, "mu_k": muk,
            "tblc": tblc, "tbls": tbls, "permm": permm, "idn": idn,
            "hmask": hm,
        })
    return in_maps


def kernel(**inputs):
    in_maps = _host_prep(inputs)
    nc = build_nc()
    res = run_bass_kernel_spmd(nc, in_maps, core_ids=list(range(NCORE)))
    y = np.concatenate([np.asarray(res.results[c]["y"], dtype=np.float32)
                        for c in range(NCORE)], axis=0)
    return y.reshape(B, N, D)


# version 10
# speedup vs baseline: 1.8132x; 1.0693x over previous
import sys
sys.path.insert(0, "/opt/trn_rl_repo")
import numpy as np
import concourse.bass as bass
import concourse.mybir as mybir
from concourse.bass_utils import run_bass_kernel_spmd
from concourse.aot_env import allow_ici_all_to_all
from contextlib import ExitStack

F32 = mybir.dt.float32
BF16 = mybir.dt.bfloat16
AF = mybir.ActivationFunctionType
ALU = mybir.AluOpType
AXX = mybir.AxisListType.X

B = 2
N = 8192
T = 16384
D = 1024
NCORE = 8
WIN = 2048
OT = 512
NOT = T // OT
NI = N // OT
NJ = N // 128
NT3 = WIN // 128
EPS = 1e-5
STAGE = 3
P3SUB = 5
P3OPS = 63
LOOSE = False
VB_MODE = 0
SYB_MODE = 0


class Marks(dict):
    strict = False

    def __missing__(self, key):
        if Marks.strict and not LOOSE:
            raise KeyError(key)
        return 0


class _DummyInst:
    def then_inc(self, *a, **k):
        return self


class _DummyEng:
    def __getattr__(self, name):
        return lambda *a, **k: _DummyInst()


_PASSTHRU = {"wait_ge", "dma_start", "collective_compute", "nop", "drain",
             "indirect_dma_start", "sem_inc"}


class EngProxy:

    def __init__(self, real, ctr, em):
        self.raw = real
        self._ctr = ctr
        self._em = em

    def __getattr__(self, name):
        if name in _PASSTHRU:
            return getattr(self.raw, name)
        real_fn = getattr(self.raw, name)
        def call(*a, **k):
            c = self._ctr
            if self._em and c.n - (c.lag - 1) * c.step > 0:
                self.raw.wait_ge(c.sem, c.n - (c.lag - 1) * c.step)
            inst = real_fn(*a, **k)
            c.n += c.step
            if self._em and c.sem is not None:
                inst.then_inc(c.sem, c.step)
            return inst
        return call


class Ctr:
    def __init__(self, name, step):
        self.name = name
        self.step = step
        self.n = 0
        self.sem = None
        self.lag = 1

    def inc(self, inst):
        self.n += self.step
        inst.then_inc(self.sem, self.step) if self.sem is not None else None
        return inst


class Env:

    def __init__(self, nc, marks):
        self.nc = nc
        self.M = marks
        self.em = False
        self.t = self.sy = _DummyEng()
        self.v = self.s = self.g = None
        self.tt = Ctr("tt", 1)
        self.vv = Ctr("vv", 1)
        self.ss = Ctr("ss", 1)
        self.gc = Ctr("gc", 1)
        self.cc = Ctr("cc", 1)
        self.syn = Ctr("syn", 16)
        self.gd = Ctr("gd", 16)
        self.st = Ctr("st", 16)
        slot_names = ["gw", "gx0", "gx1", "gt0", "gt1", "gw3", "ghg",
                      "sg0", "sg1", "skk", "sqt0", "sqt1", "sos0", "sos1",
                      "sot0", "sot1", "sxw0", "sxw1", "shgi", "sy0", "sy1"]
        self.slots = {n: Ctr(n, 16) for n in slot_names}
        self.ctrs = [self.tt, self.vv, self.ss, self.gc, self.cc, self.syn, self.gd, self.st] + \
            [self.slots[n] for n in slot_names]

    def use_engines(self, t=None, v=None, s=None, g=None, sy=None):
        if t is not None:
            self.t = t
        if sy is not None:
            self.sy = sy
        if v is not None:
            self.v = EngProxy(v, self.vv, self.em)
        if s is not None:
            self.s = EngProxy(s, self.ss, self.em)
        if g is not None:
            self.g = EngProxy(g, self.gc, self.em)

    def reset(self):
        for c in self.ctrs:
            c.n = 0

    def sems(self, nc, stack):
        for c in self.ctrs:
            c.sem = stack.enter_context(nc.semaphore("sem_" + c.name))

    def w(self, eng, ctr, val):
        if self.em and val > 0:
            eng.wait_ge(ctr.sem, val)


def act_raw(E, eng, out_ap, in_ap, func, bias_ap, scale=1.0):
    c = eng._ctr
    if not E.em:
        c.n += c.step
        return _DummyInst()
    nc = E.nc
    with nc.discard():
        proto = eng.raw.activation(out_ap, in_ap, AF.Exp, bias=bias_ap, scale=scale).ins
    new = mybir.InstActivation(
        name=nc.get_next_instruction_name(), engine=proto.engine,
        ins=list(proto.ins), outs=list(proto.outs), func=func, debug=proto.debug)
    if c.n - (c.lag - 1) * c.step > 0:
        eng.raw.wait_ge(c.sem, c.n - (c.lag - 1) * c.step)
    inst = eng.raw.add_instruction(new)
    c.n += c.step
    inst.then_inc(c.sem, c.step)
    return inst



def prog_sync_a(E, R):
    sy, M = E.sy, E.M
    for ot in range(NOT):
        gt = ot * OT
        for h in range(2):
            ft = ot * 2 + h
            sp = ft % 2
            sg = E.slots["sg%d" % sp]
            E.w(sy, E.vv, M[("p1stg", ot, h)])
            sg.inc(sy.dma_start(
                out=R.QD.ap()[:, h * T + gt: h * T + gt + 512]
                    .rearrange("(g p) t -> p g t", g=2),
                in_=R.stg[sp][0:64, :].rearrange("p (g t) -> p g t", g=2)))
            sg.inc(sy.dma_start(
                out=R.KK[0:64, h * T + gt: h * T + gt + 512],
                in_=R.stg[sp][64:128, 0:512]))
            sg.inc(sy.dma_start(
                out=R.KK[64:128, h * T + gt: h * T + gt + 512],
                in_=R.stg[sp][64:128, 512:1024]))
            M[("p1syn_stg", ot, h)] = (sg, sg.n)
    M["p1sg0_end"] = E.slots["sg0"].n
    M["p1sg1_end"] = E.slots["sg1"].n
    if STAGE < 2:
        M["p2syn_end"] = E.syn.n
        return
    def store_O(bi):
        b, I = bi // NI, bi % NI
        gtok = b * N + I * OT
        w_, c0 = gtok // WIN, gtok % WIN
        so = E.slots["sos%d" % (bi % 2)]
        so.inc(sy.dma_start(out=R.A2I.ap()[128 * w_:128 * w_ + 128, c0:c0 + 512],
                            in_=R.Ost[bi % 2][:, :]))
        M[("p2ost", bi)] = (so, so.n)

    for bi in range(B * NI):
        b, I = bi // NI, bi % NI
        buf = bi % 2
        if bi >= 2:
            E.w(sy, E.tt, M[("p2tte", bi - 2)])
        gt = b * N + I * OT
        sq = E.slots["sqt%d" % buf]
        sq.inc(sy.dma_start(
            out=R.Qt[buf][:, :].rearrange("p (h t) -> p h t", h=2),
            in_=R.QD.ap().rearrange("p (h n) -> p h n", h=2)[:, :, gt:gt + 512]))
        M[("p2qt", bi)] = (sq, sq.n)
        if bi >= 1:
            E.w(sy, E.vv, M[("p2evac", bi - 1)])
            store_O(bi - 1)
    E.w(sy, E.vv, M[("p2evac", B * NI - 1)])
    store_O(B * NI - 1)
    M["p2sos0_end"] = E.slots["sos0"].n
    M["p2sos1_end"] = E.slots["sos1"].n


def prog_gpsimd_a(E, R):
    g, M = E.g, E.M
    g.memset(R.zb[:, :], 0.0)
    g.memset(R.epsb[:, :], EPS)
    g.memset(R.ones[:, :], 1.0)
    M["p1memset"] = E.gc.n
    gw = E.slots["gw"]
    for src, dst, c in ((R.dwqk, R.wqk, 256), (R.dwv, R.wv, 128)):
        gw.inc(g.dma_start(out=dst[:, :].rearrange("p (c f) -> p c f", c=8),
                           in_=src.ap().rearrange("(c p) f -> p c f", c=8)))
    gw.inc(g.dma_start(out=R.bqk[:, :], in_=R.dbqk.ap()))
    gw.inc(g.dma_start(out=R.bv[:, :], in_=R.dbv.ap()))
    gw.inc(g.dma_start(out=R.permb[:, :], in_=R.dperm.ap()))
    gw.inc(g.dma_start(out=R.idnb[:, :], in_=R.didn.ap()))
    M["p1w"] = gw.n
    for ot in range(NOT):
        bsel = ot % 2
        gt = ot * OT
        pos = gt % N
        if ot >= 2:
            E.w(g, E.ss, M[("p1norm", ot - 2)])
            E.w(g, E.vv, M[("p1stat", ot - 2)])
            E.w(g, E.vv, M[("p1stg", ot - 2, 1)])
        gx = E.slots["gx%d" % bsel]
        gt_s = E.slots["gt%d" % bsel]
        gx.inc(g.dma_start(
            out=R.xts[bsel][:, :].rearrange("p (s c) -> p s c", s=4),
            in_=R.dxf.ap()[gt:gt + 512, :].rearrange("(s p) c -> p s c", s=4)))
        gt_s.inc(g.dma_start(out=R.tcs[bsel][:, :], in_=R.dtblc.ap()[:, pos:pos + 512]))
        gt_s.inc(g.dma_start(out=R.tss[bsel][:, :], in_=R.dtbls.ap()[:, pos:pos + 512]))
        M[("p1x", ot)] = (gx, gx.n)
        M[("p1tb", ot)] = (gt_s, gt_s.n)
    if STAGE < 2:
        M["a2a"] = E.cc.n
        return
    E.w(g, E.slots["sos0"], M["p2sos0_end"])
    E.w(g, E.slots["sos1"], M["p2sos1_end"])
    if E.em:
        with allow_ici_all_to_all():
            E.cc.inc(g.collective_compute(
                "AllToAll", ALU.bypass, replica_groups=[list(range(8))],
                ins=[R.A2I.ap().opt()], outs=[R.A2O.ap().opt()]))
    else:
        E.cc.inc(_DummyInst())
    M["a2a"] = E.cc.n


def prog_tensor_a(E, R):
    t, M = E.t, E.M
    E.w(t, E.slots["gw"], M["p1w"])
    E.w(t, E.gc, M["p1memset"])
    QKb = [R.S1p[0][:, :], R.S1p[1][:, :]]
    SWp = R.S2p[1][:, :]
    Vpp = R.S2p[0][:, :]
    for ot in range(NOT):
        bsel = ot % 2
        E.w(t, E.ss, M[("p1norm", ot)])
        if ot >= 1:
            pass
        for ccn in range(8):
            tp = R.TPq[ccn % 2][:, 0:512].bitcast(BF16) if False else R.TPq[ccn % 2]
            gcc = ot * 8 + ccn
            if gcc >= 2:
                ev = M[("p1tpev", gcc - 2)]
                if ev:
                    E.w(t, ev[0], ev[1])
            for sb in range(4):
                inst = t.transpose(tp[:, sb * 128:(sb + 1) * 128],
                                   R.xnb[bsel][:, sb * D + ccn * 128: sb * D + ccn * 128 + 128],
                                   R.idnb[:, :])
            E.tt.inc(inst)
            M[("p1tp", ot, ccn)] = E.tt.n
        E.w(t, E.vv, M[("p1trev_v", ot)])
        E.w(t, E.ss, M[("p1trev_s", ot)])
        for h in range(2):
            ft = ot * 2 + h
            if ft >= 2:
                E.w(t, E.vv, M[("p1stg", (ft - 2) // 2, (ft - 2) % 2)])
            for ccn in range(8):
                t.matmul(QKb[h], R.wqk[:, ccn * 256 + h * 128: ccn * 256 + h * 128 + 128],
                         R.xnT[:, ccn * OT: ccn * OT + 512], start=(ccn == 0), stop=False)
            E.tt.inc(t.matmul(QKb[h], R.bqk[:, h * 128:(h + 1) * 128], R.ones[:, 0:512],
                              start=False, stop=True))
            M[("p1qk", ot, h)] = E.tt.n
        for h in range(2):
            ft = ot * 2 + h
            sp = ft % 2
            E.w(t, E.vv, M[("p1E", ot, h)])
            if h == 1:
                E.w(t, E.vv, M[("p1ropeA", ot, 0)])
            E.tt.inc(t.matmul(SWp, R.permb[:, :], R.stg[sp][:, 512:1024],
                              start=True, stop=True))
            M[("p1sw", ot, h)] = E.tt.n
        if ot >= 1:
            E.w(t, E.vv, M[("p1vev", ot - 1)])
        for sb in range(4):
            for ccn in range(8):
                t.matmul(Vpp[:, sb * 128:(sb + 1) * 128],
                         R.xnT[:, ccn * OT + sb * 128: ccn * OT + sb * 128 + 128],
                         R.wv[:, ccn * 128:(ccn + 1) * 128], start=(ccn == 0), stop=False)
            inst = t.matmul(Vpp[:, sb * 128:(sb + 1) * 128], R.ones[:, 0:128], R.bv[:, :],
                            start=False, stop=True)
        E.tt.inc(inst)
        M[("p1v", ot)] = E.tt.n
    if STAGE < 2:
        return
    E.w(t, E.slots["sg0"], M["p1sg0_end"])
    E.w(t, E.slots["sg1"], M["p1sg1_end"])
    E.w(t, E.vv, M["p1vv_end"])
    E.w(t, E.ss, M["p1ss_end"])
    ACCap = [R.TPq[0][:, :].bitcast(F32), R.TPq[1][:, :].bitcast(F32)]
    u = 0
    for bi in range(B * NI):
        b, I = bi // NI, bi % NI
        buf = bi % 2
        qtm = M[("p2qt", bi)]
        if qtm:
            E.w(t, qtm[0], qtm[1])
        for h in range(2):
            hb = h * T + b * N
            for J in range(NJ):
                p = u % 3
                if u >= 3:
                    E.w(t, E.vv, M[("p2w", u - 3)])
                t.matmul(R.S1p[p][:, :], R.KK[0:64, hb + J * 128: hb + J * 128 + 128],
                         R.Qt[buf][0:64, h * 512:(h + 1) * 512],
                         start=True, stop=True, tile_position=(0, 0))
                E.tt.inc(t.matmul(R.S2p[p][:, :], R.KK[64:128, hb + J * 128: hb + J * 128 + 128],
                                  R.Qt[buf][64:128, h * 512:(h + 1) * 512],
                                  start=True, stop=True, tile_position=(64, 0)))
                M[("p2s", u)] = E.tt.n
                if J >= 3:
                    _mm3(E, R, ACCap, b, h, J - 3, u - 3, bi)
                u += 1
            _mm3(E, R, ACCap, b, h, NJ - 3, u - 3, bi)
            _mm3(E, R, ACCap, b, h, NJ - 2, u - 2, bi)
            _mm3(E, R, ACCap, b, h, NJ - 1, u - 1, bi)
        M[("p2tte", bi)] = E.tt.n
    M["p2tt_end"] = E.tt.n


def _mm3(E, R, ACCap, b, h, J, u, bi):
    t, M = E.t, E.M
    if J == 0 and bi >= 1:
        E.w(t, E.vv, M[("p2evac", bi - 1)])
    E.w(t, E.vv, M[("p2w", u)])
    vcol = (b * 64 + J) * 128 + h * 64
    if h == 0:
        inst = t.matmul(ACCap[0][0:64, :], R.Vt[:, vcol:vcol + 64], R.WT[u % 4][:, :],
                        start=(J == 0), stop=(J == NJ - 1))
    else:
        inst = t.matmul(ACCap[1][64:128, :], R.Vt[:, vcol:vcol + 64], R.WT[u % 4][:, :],
                        start=(J == 0), stop=(J == NJ - 1), tile_position=(0, 64))
    if J == NJ - 1:
        E.tt.inc(inst)


def prog_vector_a(E, R):
    v, M = E.v, E.M
    QKb = [R.S1p[0][:, :], R.S1p[1][:, :]]
    SWp = R.S2p[1][:, :]
    Vpp = R.S2p[0][:, :]
    for ot in range(NOT):
        bsel = ot % 2
        xm = M[("p1x", ot)]
        if xm:
            E.w(v, xm[0], xm[1])
        for sb in range(4):
            inst = v.reduce_sum(R.sums[:, ot * 4 + sb: ot * 4 + sb + 1],
                                R.xts[bsel][:, sb * D:(sb + 1) * D], axis=AXX)
        M[("p1stat", ot)] = E.vv.n
        E.w(v, E.ss, M[("p1sq", ot)])
        c4 = slice(ot * 4, ot * 4 + 4)
        v.tensor_scalar(R.lnt[:, 0:4], R.sums[:, c4], 1.0 / D, None, ALU.mult)
        v.tensor_scalar(R.lnt[:, 4:8], R.sumsq[:, c4], 1.0 / D, None, ALU.mult)
        v.tensor_tensor(R.lnt[:, 8:12], R.lnt[:, 0:4], R.lnt[:, 0:4], ALU.mult)
        v.tensor_tensor(R.lnt[:, 12:16], R.lnt[:, 4:8], R.lnt[:, 8:12], ALU.subtract)
        M[("p1var", ot)] = E.vv.n
        E.w(v, E.ss, M[("p1rstd", ot)])
        v.tensor_scalar(R.lnt[:, 20:24], R.lnt[:, 0:4], -1.0, None, ALU.mult)
        v.tensor_tensor(R.lnt[:, 20:24], R.lnt[:, 20:24], R.lnt[:, 16:20], ALU.mult)
        M[("p1negmr", ot)] = E.vv.n
        for ccn in range(0, 8, 2):
            E.w(v, E.tt, M[("p1tp", ot, ccn)])
            if ot >= 1 and ccn == 0:
                E.w(v, E.tt, M[("p1v", ot - 1)])
            v.tensor_copy(R.xnT[:, ccn * OT: ccn * OT + 512], R.TPq[ccn % 2][:, 0:512])
            M[("p1tpev", ot * 8 + ccn)] = (E.vv, E.vv.n)
        M[("p1trev_v", ot)] = E.vv.n
        for h in range(2):
            ft = ot * 2 + h
            sp = ft % 2
            E.w(v, E.tt, M[("p1qk", ot, h)])
            if ft >= 1:
                E.w(v, E.ss, M[("p1exp", *divmod(ft - 1, 2))])
            v.tensor_scalar_min(R.es[:, :], QKb[h], 0.0)
            M[("p1min", ot, h)] = E.vv.n
            E.w(v, E.ss, M[("p1exp", ot, h)])
            if ft >= 2:
                sgm = M[("p1syn_stg", (ft - 2) // 2, (ft - 2) % 2)]
                if sgm:
                    E.w(v, sgm[0], sgm[1])
            v.scalar_tensor_tensor(R.stg[sp][:, 512:1024], QKb[h], 0.0,
                                            R.expm[:, :], ALU.max, ALU.add)
            M[("p1E", ot, h)] = E.vv.n
            E.w(v, E.tt, M[("p1sw", ot, h)])
            tbm = M[("p1tb", ot)]
            if tbm:
                E.w(v, tbm[0], tbm[1])
            v.tensor_tensor(R.asb[:, :], SWp, R.tss[bsel][:, :], ALU.mult)
            M[("p1ropeA", ot, h)] = E.vv.n
            v.tensor_tensor(R.bsb[:, :], R.stg[sp][:, 512:1024], R.tcs[bsel][:, :], ALU.mult)
            v.tensor_tensor(R.stg[sp][:, 0:512], R.asb[:, :], R.bsb[:, :], ALU.add)
            M[("p1stg", ot, h)] = E.vv.n
        E.w(v, E.tt, M[("p1v", ot)])
        v.tensor_copy(R.Vt[:, ot * 512:(ot + 1) * 512], Vpp)
        M[("p1vev", ot)] = E.vv.n
    M["p1vv_end"] = E.vv.n
    if STAGE < 2:
        return
    ACCap = [R.TPq[0][:, :].bitcast(F32), R.TPq[1][:, :].bitcast(F32)]
    u = 0
    E.vv.lag = 3
    for bi in range(B * NI):
        for h in range(2):
            for J in range(NJ):
                p = u % 3
                E.w(v, E.ss, M[("p2r", u)])
                v.tensor_tensor(R.WT[u % 4][:, :], R.S1p[p][:, :], R.rp[p][:, :], ALU.mult)
                M[("p2w", u)] = E.vv.n
                u += 1
        E.w(v, E.tt, M[("p2tte", bi)])
        if bi >= 2:
            osm = M[("p2ost", bi - 2)]
            if osm:
                E.w(v, osm[0], osm[1])
        v.tensor_copy(R.Ost[bi % 2][0:64, :], ACCap[0][0:64, 0:512])
        v.tensor_copy(R.Ost[bi % 2][64:128, :], ACCap[1][64:128, 0:512])
        M[("p2evac", bi)] = E.vv.n
    E.vv.lag = 1


def prog_scalar_a(E, R):
    s, M = E.s, E.M
    for ot in range(NOT):
        bsel = ot % 2
        xm = M[("p1x", ot)]
        if xm:
            E.w(s, xm[0], xm[1])
        E.w(s, E.gc, M["p1memset"])
        for sb in range(4):
            inst = s.activation(R.sqsc[:, :], R.xts[bsel][:, sb * D:(sb + 1) * D],
                                AF.Square, bias=R.zb[:, :],
                                accum_out=R.sumsq[:, ot * 4 + sb: ot * 4 + sb + 1])
        M[("p1sq", ot)] = E.ss.n
        E.w(s, E.vv, M[("p1var", ot)])
        s.activation(R.lnt[:, 16:20], R.lnt[:, 12:16], AF.Sqrt, bias=R.epsb[:, :])
        act_raw(E, s, R.lnt[:, 16:20], R.lnt[:, 16:20], AF.Reciprocal, R.zb[:, :])
        M[("p1rstd", ot)] = E.ss.n
        E.w(s, E.vv, M[("p1negmr", ot)])
        if ot >= 1:
            E.w(s, E.tt, M[("p1tp", ot - 1, 7)])
        for sb in range(4):
            inst = s.activation(R.xnb[bsel][:, sb * D:(sb + 1) * D],
                                R.xts[bsel][:, sb * D:(sb + 1) * D], AF.Identity,
                                bias=R.lnt[:, 20 + sb:21 + sb], scale=R.lnt[:, 16 + sb:17 + sb])
        M[("p1norm", ot)] = E.ss.n
        for ccn in range(1, 8, 2):
            E.w(s, E.tt, M[("p1tp", ot, ccn)])
            if ot >= 1 and ccn == 1:
                E.w(s, E.tt, M[("p1v", ot - 1)])
            s.copy(R.xnT[:, ccn * OT: ccn * OT + 512], R.TPq[ccn % 2][:, 0:512])
            M[("p1tpev", ot * 8 + ccn)] = (E.ss, E.ss.n)
        M[("p1trev_s", ot)] = E.ss.n
        for h in range(2):
            E.w(s, E.vv, M[("p1min", ot, h)])
            if ot * 2 + h >= 1:
                E.w(s, E.vv, M[("p1E", *divmod(ot * 2 + h - 1, 2))])
            s.activation(R.expm[:, :], R.es[:, :], AF.Exp, bias=R.zb[:, :])
            M[("p1exp", ot, h)] = E.ss.n
    M["p1ss_end"] = E.ss.n
    if STAGE < 2:
        return
    u = 0
    E.ss.lag = 3
    for bi in range(B * NI):
        for h in range(2):
            for J in range(NJ):
                p = u % 3
                E.w(s, E.tt, M[("p2s", u)])
                if u >= 3:
                    E.w(s, E.vv, M[("p2w", u - 3)])
                act_raw(E, s, R.rp[p][:, :], R.S2p[p][:, :], AF.Reciprocal, R.zb[:, :])
                M[("p2r", u)] = E.ss.n
                u += 1
    E.ss.lag = 1



def prog_gpsimd_b(E, R):
    g, M = E.g, E.M
    E.w(g, E.cc, M["a2a"])
    g.memset(R.zb3[:, :], 0.0)
    g.memset(R.epsb3[:, :], EPS)
    g.memset(R.ones3[:, :], 1.0)
    M["p3memset"] = E.gc.n
    gw3 = E.slots["gw3"]
    for src, dst in ((R.dwout, R.wouts), (R.dwr, R.wrs), (R.dwk, R.wks), (R.dwv2, R.wv2s)):
        gw3.inc(g.dma_start(out=dst[:, :].rearrange("p (c f) -> p c f", c=8),
                            in_=src.ap().rearrange("(c p) f -> p c f", c=8)))
    gw3.inc(g.dma_start(out=R.bouts[:, :], in_=R.dbout.ap()))
    gw3.inc(g.dma_start(out=R.idn3[:, :], in_=R.didn.ap()))
    gw3.inc(g.dma_start(out=R.hmm[:, :], in_=R.dhm.ap()))
    gw3.inc(g.dma_start(out=R.murs[:, :], in_=R.dmur.ap()))
    gw3.inc(g.dma_start(out=R.muks[:, :], in_=R.dmuk.ap()))
    M["p3w"] = gw3.n
    if P3SUB < 2:
        return
    E.w(g, E.slots["shgi"], M["p3hgi"])
    if E.em:
        E.cc.inc(g.collective_compute(
            "AllGather", ALU.bypass, replica_groups=[list(range(8))],
            ins=[R.HGI.ap().opt()], outs=[R.HGO.ap().opt()]))
    else:
        E.cc.inc(_DummyInst())
    M["hag"] = E.cc.n
    E.w(g, E.cc, M["hag"])
    E.slots["ghg"].inc(g.dma_start(out=R.hgos[:, :], in_=R.HGO.ap()))
    M["p3hgo"] = E.slots["ghg"].n


def prog_sync_b(E, R):
    sy, M = E.sy, E.M
    E.w(sy, E.cc, M["a2a"])
    for it in range(NT3):
        buf = it % 2
        if it >= 2:
            E.w(sy, E.tt, M[("p3op", it - 2)])
        sot = E.slots["sot%d" % buf]
        sot.inc(sy.dma_start(
            out=R.OTt[buf][:, :].rearrange("p (c t) -> p c t", c=8),
            in_=R.A2O.ap().rearrange("(c p) t -> p c t", c=8)[:, :, it * 128:(it + 1) * 128]))
        M[("p3ot", it)] = (sot, sot.n)
        if it >= 2:
            E.w(sy, E.vv, M[("p3xr1", it - 2)])
        sxw = E.slots["sxw%d" % buf]
        sxw.inc(sy.dma_start(out=R.xw[buf][:, :], in_=R.dxw.ap()[it * 128:(it + 1) * 128, :]))
        M[("p3xw", it)] = (sxw, sxw.n)
    if P3SUB < 2:
        E.w(sy, E.vv, M[("p3xn3", NT3 - 1)])
        for half in range(2):
            ev = M[("p3t3ev", (NT3 - 1) * 2 + half)]
            if ev:
                E.w(sy, ev[0], ev[1])
        return
    E.w(sy, E.vv, M[("p3xn3", NT3 - 1)])
    E.slots["shgi"].inc(sy.dma_start(out=R.HGI.ap(), in_=R.xn3b[127:128, :]))
    M["p3hgi"] = E.slots["shgi"].n
    if P3SUB < 5:
        return
    for it in range(NT3):
        buf = it % 2
        if it < 2:
            E.w(sy, E.vv, M[("p3xr1", NT3 - 2 + it)])
        else:
            E.w(sy, E.vv, M[("p3y", it - 2)])
        sxw = E.slots["sxw%d" % buf]
        sxw.inc(sy.dma_start(out=R.xw[buf][:, :], in_=R.dxw.ap()[it * 128:(it + 1) * 128, :]))
        M[("p3xw2", it)] = (sxw, sxw.n)
        E.w(sy, E.vv, M[("p3y", it)])
        syt = E.slots["sy%d" % (it % 2)]
        syt.inc(sy.dma_start(out=R.dy.ap()[it * 128:(it + 1) * 128, :], in_=R.ysb[it % 2][:, :]))
        M[("p3yst", it)] = (syt, syt.n)
    if E.em:
        sy.wait_ge(E.slots["sy0"].sem, E.slots["sy0"].n)
        sy.wait_ge(E.slots["sy1"].sem, E.slots["sy1"].n)


def prog_tensor_b(E, R):
    t, M = E.t, E.M
    E.w(t, E.slots["gw3"], M["p3w"])
    E.w(t, E.gc, M["p3memset"])
    for it in range(NT3):
        buf = it % 2
        if not (P3OPS & 1):
            break
        otm = M[("p3ot", it)]
        if otm:
            E.w(t, otm[0], otm[1])
        if it >= 2:
            E.w(t, E.ss, M[("p3pev", it - 2)])
            E.w(t, E.vv, M[("p3ps", it - 2)])
        for fh in range(2):
            for ccn in range(8):
                t.matmul(R.Pp2[it % 2][:, fh * 512:(fh + 1) * 512],
                         R.OTt[buf][:, ccn * 128:(ccn + 1) * 128],
                         R.wouts[:, ccn * D + fh * 512: ccn * D + fh * 512 + 512],
                         start=(ccn == 0), stop=False)
            inst = t.matmul(R.Pp2[it % 2][:, fh * 512:(fh + 1) * 512], R.ones3[:, 0:128],
                            R.bouts[:, fh * 512:(fh + 1) * 512], start=False, stop=True)
        E.tt.inc(inst)
        M[("p3op", it)] = E.tt.n
        if not (P3OPS & 32):
            continue
        E.w(t, E.vv, M[("p3xn3", it)])
        for half in range(2):
            tp = R.TP3[half]
            ghalf = it * 2 + half
            if ghalf >= 2:
                ev = M[("p3t3ev", ghalf - 2)]
                if ev:
                    E.w(t, ev[0], ev[1])
            for q in range(4):
                ccn = half * 4 + q
                inst = t.transpose(tp[:, q * 128:(q + 1) * 128],
                                   R.xn3b[:, ccn * 128:(ccn + 1) * 128], R.idn3[:, :])
            E.tt.inc(inst)
            M[("p3t3", it, half)] = E.tt.n
    if P3SUB < 2:
        return
    E.w(t, E.slots["ghg"], M["p3hgo"])
    for fh in range(2):
        E.tt.inc(t.matmul(R.HP[0:1, :], R.hmm[:, :], R.hgos[:, fh * 512:(fh + 1) * 512],
                          start=True, stop=True))
        M[("p3hp", fh)] = E.tt.n
        E.w(t, E.ss, M[("p3hev", fh)])
    E.w(t, E.ss, M[("p3hev", 1)])
    for ccn in range(8):
        inst = t.transpose(R.HS[:, 2 * ccn:2 * ccn + 1], R.halo[0:1, ccn * 128:(ccn + 1) * 128],
                           R.idn3[0:1, 0:1])
    E.tt.inc(inst)
    M["p3hst"] = E.tt.n
    if P3SUB < 4:
        return
    for ts4 in range(4):
        for dt_ in range(8):
            u = ts4 * 8 + dt_
            p = u % 2
            if u >= 2:
                E.w(t, E.ss, M[("p3kact", u - 2)])
            E.w(t, E.vv, M[("p3mixk", ts4)])
            for ccn in range(8):
                inst = t.matmul(R.Kp[p][:, :],
                                R.wks[:, ccn * D + dt_ * 128: ccn * D + dt_ * 128 + 128],
                                R.xmkT[:, ccn * 512:(ccn + 1) * 512],
                                start=(ccn == 0), stop=(ccn == 7))
            E.tt.inc(inst)
            M[("p3k", u)] = E.tt.n
        if P3SUB < 5:
            continue
        for q in range(4):
            it = ts4 * 4 + q
            E.w(t, E.vv, M[("p3mixr", ts4)])
            if it >= 2:
                E.w(t, E.ss, M[("p3sig", it - 2)])
                E.w(t, E.vv, M[("p3ff", it - 2)])
            for fh in range(2):
                for ccn in range(8):
                    inst = t.matmul(R.Pp2[it % 2][:, fh * 512:(fh + 1) * 512],
                                    R.xmrT[:, ccn * 512 + q * 128: ccn * 512 + q * 128 + 128],
                                    R.wrs[:, ccn * D + fh * 512: ccn * D + fh * 512 + 512],
                                    start=(ccn == 0), stop=(ccn == 7))
            E.tt.inc(inst)
            M[("p3r", it)] = E.tt.n
            E.w(t, E.ss, M[("p3sig", it)])
            E.w(t, E.ss, M[("p3kact", ts4 * 8 + 7)])
            for fh in range(2):
                for ccn in range(8):
                    inst = t.matmul(R.Pp2[it % 2][:, fh * 512:(fh + 1) * 512],
                                    R.kactS[:, ccn * 512 + q * 128: ccn * 512 + q * 128 + 128],
                                    R.wv2s[:, ccn * D + fh * 512: ccn * D + fh * 512 + 512],
                                    start=(ccn == 0), stop=(ccn == 7))
            E.tt.inc(inst)
            M[("p3vo", it)] = E.tt.n


def prog_vector_b(E, R):
    v, M = E.v, E.M
    E.w(v, E.slots["gw3"], M["p3w"])
    v.tensor_scalar(R.omurs[:, :], R.murs[:, :], -1.0, 1.0, ALU.mult, ALU.add)
    v.tensor_scalar(R.omuks[:, :], R.muks[:, :], -1.0, 1.0, ALU.mult, ALU.add)
    for it in range(NT3):
        if not (P3OPS & 2):
            break
        E.w(v, E.tt, M[("p3op", it)])
        if VB_MODE == 0:
            v.reduce_sum(R.l3[:, 0:1], R.Pp2[it % 2][:, :], axis=AXX)
        elif VB_MODE == 1:
            v.reduce_sum(R.l3[:, 0:1], R.Pp2[it % 2][:, 0:512], axis=AXX)
        elif VB_MODE == 2:
            v.tensor_copy(R.l3[:, 0:8], R.Pp2[it % 2][:, 0:8])
        else:
            v.tensor_copy(R.l3[:, 0:8], R.xw[0][:, 0:8])
        M[("p3ps", it)] = E.vv.n
        if not (P3OPS & 8):
            continue
        E.w(v, E.ss, M[("p3pev", it)])
        v.tensor_scalar(R.l3[:, 2:3], R.l3[:, 0:1], 1.0 / D, None, ALU.mult)
        v.tensor_scalar(R.l3[:, 3:4], R.l3[:, 1:2], 1.0 / D, None, ALU.mult)
        v.tensor_tensor(R.l3[:, 4:5], R.l3[:, 2:3], R.l3[:, 2:3], ALU.mult)
        v.tensor_tensor(R.l3[:, 5:6], R.l3[:, 3:4], R.l3[:, 4:5], ALU.subtract)
        M[("p3var2", it)] = E.vv.n
        E.w(v, E.ss, M[("p3rs2", it)])
        v.tensor_scalar(R.ln2o[:, it * D:(it + 1) * D], R.Psb[:, :],
                                 R.l3[:, 2:3], R.l3[:, 6:7], ALU.subtract, ALU.mult)
        M[("p3ln2", it)] = E.vv.n
        if not (P3OPS & 16):
            continue
        xwm = M[("p3xw", it)]
        if xwm:
            E.w(v, xwm[0], xwm[1])
        v.tensor_tensor(R.xr1[:, :], R.xw[it % 2][:, :],
                                 R.ln2o[:, it * D:(it + 1) * D], ALU.add)
        M[("p3xr1", it)] = E.vv.n
        v.reduce_sum(R.l3[:, 8:9], R.xr1[:, :], axis=AXX)
        E.w(v, E.ss, M[("p3sq3", it)])
        v.tensor_scalar(R.l3[:, 10:11], R.l3[:, 8:9], 1.0 / D, None, ALU.mult)
        v.tensor_scalar(R.l3[:, 11:12], R.l3[:, 9:10], 1.0 / D, None, ALU.mult)
        v.tensor_tensor(R.l3[:, 12:13], R.l3[:, 10:11], R.l3[:, 10:11], ALU.mult)
        v.tensor_tensor(R.l3[:, 13:14], R.l3[:, 11:12], R.l3[:, 12:13], ALU.subtract)
        M[("p3var3", it)] = E.vv.n
        E.w(v, E.ss, M[("p3rs3", it)])
        if it >= 1:
            E.w(v, E.tt, M[("p3t3", it - 1, 1)])
        v.tensor_scalar(R.xn3b[:, :], R.xr1[:, :],
                                 R.l3[:, 10:11], R.l3[:, 14:15], ALU.subtract, ALU.mult)
        M[("p3xn3", it)] = E.vv.n
        for half in range(2):
            E.w(v, E.tt, M[("p3t3", it, half)])
            dst = R.xn3T[:, :].rearrange("p (c w) -> p c w", c=8)[:, half * 4:half * 4 + 4,
                                                                 it * 128:it * 128 + 128]
            v.tensor_copy(dst, R.TP3[half][:, :].rearrange("p (c w) -> p c w", c=4))
            M[("p3t3ev", it * 2 + half)] = (E.vv, E.vv.n)
    if P3SUB < 2:
        return
    E.w(v, E.tt, M["p3hst"])
    v.tensor_copy(R.haloT[:, :], R.HS[:, 0:16:2])
    M["p3halot"] = E.vv.n
    if P3SUB < 3:
        return
    for ts4 in range(4):
        t0 = ts4 * 512
        for mu, omu, dst, mk in ((R.murs, R.omurs, R.xmrT, "p3mixr"), (R.muks, R.omuks, R.xmkT, "p3mixk")):
            if ts4 == 0:
                E.w(v, E.vv, M["p3halot"])
            if ts4 >= 1:
                E.w(v, E.tt, M[("p3r", ts4 * 4 - 1)] if mk == "p3mixr" else M[("p3k", (ts4 - 1) * 8 + 7)])
            for ccn in range(8):
                cbase = ccn * WIN
                if t0 == 0:
                    v.tensor_scalar(R.mixs[:, 1:512], R.xn3T[:, cbase: cbase + 511],
                                    omu[:, ccn:ccn + 1], None, ALU.mult)
                    v.tensor_scalar(R.mixs[:, 0:1], R.haloT[:, ccn:ccn + 1],
                                    omu[:, ccn:ccn + 1], None, ALU.mult)
                else:
                    v.tensor_scalar(R.mixs[:, 0:512], R.xn3T[:, cbase + t0 - 1: cbase + t0 + 511],
                                    omu[:, ccn:ccn + 1], None, ALU.mult)
                inst = v.scalar_tensor_tensor(dst[:, ccn * 512:(ccn + 1) * 512],
                                              R.xn3T[:, cbase + t0: cbase + t0 + 512],
                                              mu[:, ccn:ccn + 1], R.mixs[:, 0:512],
                                              ALU.mult, ALU.add)
            M[(mk, ts4)] = E.vv.n
        if P3SUB < 4:
            continue
        for dt_ in range(8):
            u = ts4 * 8 + dt_
            p = u % 2
            E.w(v, E.tt, M[("p3k", u)])
            if u >= 1:
                E.w(v, E.ss, M[("p3kact", u - 1)])
            v.tensor_scalar_max(R.kpre[:, :], R.Kp[p][:, :], 0.0)
            M[("p3relu", u)] = E.vv.n
        if P3SUB < 5:
            continue
        for q in range(4):
            it = ts4 * 4 + q
            E.w(v, E.tt, M[("p3vo", it)])
            if it >= 1:
                E.w(v, E.ss, M[("p3sq4", it - 1)])
            v.tensor_tensor(R.ffo[:, :], R.Pp2[it % 2][:, :], R.rsig[:, :], ALU.mult)
            M[("p3ff", it)] = E.vv.n
            v.reduce_sum(R.l3[:, 16:17], R.ffo[:, :], axis=AXX)
            E.w(v, E.ss, M[("p3sq4", it)])
            v.tensor_scalar(R.l3[:, 18:19], R.l3[:, 16:17], 1.0 / D, None, ALU.mult)
            v.tensor_scalar(R.l3[:, 19:20], R.l3[:, 17:18], 1.0 / D, None, ALU.mult)
            v.tensor_tensor(R.l3[:, 20:21], R.l3[:, 18:19], R.l3[:, 18:19], ALU.mult)
            v.tensor_tensor(R.l3[:, 21:22], R.l3[:, 19:20], R.l3[:, 20:21], ALU.subtract)
            M[("p3var4", it)] = E.vv.n
            E.w(v, E.ss, M[("p3rs4", it)])
            v.tensor_scalar(R.ln4o[:, :], R.ffo[:, :], R.l3[:, 18:19], R.l3[:, 22:23],
                            ALU.subtract, ALU.mult)
            xwm = M[("p3xw2", it)]
            if xwm:
                E.w(v, xwm[0], xwm[1])
            if it >= 2:
                ym = M[("p3yst", it - 2)]
                if ym:
                    E.w(v, ym[0], ym[1])
            v.tensor_tensor(R.ysb[it % 2][:, :], R.xw[it % 2][:, :],
                            R.ln2o[:, it * D:(it + 1) * D], ALU.add)
            v.tensor_tensor(R.ysb[it % 2][:, :], R.ysb[it % 2][:, :], R.ln4o[:, :], ALU.add)
            M[("p3y", it)] = E.vv.n


def prog_scalar_b(E, R):
    s, M = E.s, E.M
    for it in range(NT3):
        if not (P3OPS & 4):
            break
        E.w(s, E.tt, M[("p3op", it)])
        E.w(s, E.vv, M[("p3ps", it)])
        if it >= 1:
            E.w(s, E.vv, M[("p3xn3", it - 1)])
        s.copy(R.Psb[:, :], R.Pp2[it % 2][:, :])
        s.activation(R.sq3[:, :], R.Pp2[it % 2][:, :], AF.Square, bias=R.zb3[:, :],
                              accum_out=R.l3[:, 1:2])
        M[("p3pev", it)] = E.ss.n
        if not (P3OPS & 8):
            continue
        E.w(s, E.vv, M[("p3var2", it)])
        s.activation(R.l3[:, 6:7], R.l3[:, 5:6], AF.Sqrt, bias=R.epsb3[:, :])
        act_raw(E, s, R.l3[:, 6:7], R.l3[:, 6:7], AF.Reciprocal, R.zb3[:, :])
        M[("p3rs2", it)] = E.ss.n
        E.w(s, E.vv, M[("p3xr1", it)])
        s.activation(R.sq3[:, :], R.xr1[:, :], AF.Square, bias=R.zb3[:, :],
                              accum_out=R.l3[:, 9:10])
        M[("p3sq3", it)] = E.ss.n
        E.w(s, E.vv, M[("p3var3", it)])
        s.activation(R.l3[:, 14:15], R.l3[:, 13:14], AF.Sqrt, bias=R.epsb3[:, :])
        act_raw(E, s, R.l3[:, 14:15], R.l3[:, 14:15], AF.Reciprocal, R.zb3[:, :])
        M[("p3rs3", it)] = E.ss.n

    if P3SUB < 2:
        return
    for fh in range(2):
        E.w(s, E.tt, M[("p3hp", fh)])
        s.copy(R.halo[0:1, fh * 512:(fh + 1) * 512], R.HP[0:1, :])
        M[("p3hev", fh)] = E.ss.n
    if P3SUB < 4:
        return
    for ts4 in range(4):
        for dt_ in range(8):
            u = ts4 * 8 + dt_
            E.w(s, E.vv, M[("p3relu", u)])
            if dt_ == 0 and ts4 >= 1:
                E.w(s, E.tt, M[("p3vo", ts4 * 4 - 1)])
            s.activation(R.kactS[:, dt_ * 512:(dt_ + 1) * 512], R.kpre[:, :],
                                  AF.Square, bias=R.zb3[:, :])
            M[("p3kact", u)] = E.ss.n
        if P3SUB < 5:
            continue
        for q in range(4):
            it = ts4 * 4 + q
            E.w(s, E.tt, M[("p3r", it)])
            if it >= 1:
                E.w(s, E.vv, M[("p3ff", it - 1)])
            s.activation(R.rsig[:, :], R.Pp2[it % 2][:, :], AF.Sigmoid, bias=R.zb3[:, :])
            M[("p3sig", it)] = E.ss.n
            E.w(s, E.vv, M[("p3ff", it)])
            s.activation(R.sq3[:, :], R.ffo[:, :], AF.Square, bias=R.zb3[:, :],
                                  accum_out=R.l3[:, 17:18])
            M[("p3sq4", it)] = E.ss.n
            E.w(s, E.vv, M[("p3var4", it)])
            s.activation(R.l3[:, 22:23], R.l3[:, 21:22], AF.Sqrt, bias=R.epsb3[:, :])
            act_raw(E, s, R.l3[:, 22:23], R.l3[:, 22:23], AF.Reciprocal, R.zb3[:, :])
            M[("p3rs4", it)] = E.ss.n



class _R:
    pass


def build_nc():
    nc = bass.Bass("TRN2", target_bir_lowering=False)
    R = _R()
    R.dxf = nc.dram_tensor("x_full", [T, D], F32, kind="ExternalInput")
    R.dxw = nc.dram_tensor("x_win", [WIN, D], F32, kind="ExternalInput")
    R.dwqk = nc.dram_tensor("wqk", [D, 256], F32, kind="ExternalInput")
    R.dbqk = nc.dram_tensor("bqk", [1, 256], F32, kind="ExternalInput")
    R.dwv = nc.dram_tensor("wv", [D, 128], F32, kind="ExternalInput")
    R.dbv = nc.dram_tensor("bv", [1, 128], F32, kind="ExternalInput")
    R.dwout = nc.dram_tensor("wout", [D, D], F32, kind="ExternalInput")
    R.dbout = nc.dram_tensor("bout", [1, D], F32, kind="ExternalInput")
    R.dwr = nc.dram_tensor("wr_t", [D, D], F32, kind="ExternalInput")
    R.dwk = nc.dram_tensor("wk_t", [D, D], F32, kind="ExternalInput")
    R.dwv2 = nc.dram_tensor("wv2_t", [D, D], F32, kind="ExternalInput")
    R.dmur = nc.dram_tensor("mu_r", [128, 8], F32, kind="ExternalInput")
    R.dmuk = nc.dram_tensor("mu_k", [128, 8], F32, kind="ExternalInput")
    R.dtblc = nc.dram_tensor("tblc", [128, N], F32, kind="ExternalInput")
    R.dtbls = nc.dram_tensor("tbls", [128, N], F32, kind="ExternalInput")
    R.dperm = nc.dram_tensor("permm", [128, 128], F32, kind="ExternalInput")
    R.didn = nc.dram_tensor("idn", [128, 128], F32, kind="ExternalInput")
    R.dhm = nc.dram_tensor("hmask", [8, 1], F32, kind="ExternalInput")
    R.dy = nc.dram_tensor("y", [WIN, D], F32, kind="ExternalOutput")
    R.QD = nc.dram_tensor("QD", [128, 2 * T], BF16)
    R.A2I = nc.dram_tensor("a2i", [1024, WIN], BF16)
    R.A2O = nc.dram_tensor("a2o", [1024, WIN], BF16)
    R.HGI = nc.dram_tensor("hgi", [1, D], BF16)
    R.HGO = nc.dram_tensor("hgo", [8, D], BF16)

    marks = Marks()
    E = Env(nc, marks)

    with ExitStack() as top:
        E.sems(nc, top)
        with ExitStack() as pa:
            e = pa.enter_context
            R.KK = e(nc.sbuf_tensor("KK", [128, 2 * T], BF16))
            R.Vt = e(nc.sbuf_tensor("Vt", [128, T], BF16))
            R.wqk = e(nc.sbuf_tensor("wqk_s", [128, 8 * 256], BF16))
            R.wv = e(nc.sbuf_tensor("wv_s", [128, 8 * 128], BF16))
            R.bqk = e(nc.sbuf_tensor("bqk_s", [1, 256], BF16))
            R.bv = e(nc.sbuf_tensor("bv_s", [1, 128], BF16))
            R.ones = e(nc.sbuf_tensor("ones_s", [1, 512], BF16))
            R.permb = e(nc.sbuf_tensor("permb", [128, 128], BF16))
            R.idnb = e(nc.sbuf_tensor("idnb", [128, 128], BF16))
            R.zb = e(nc.sbuf_tensor("zb", [128, 1], F32))
            R.epsb = e(nc.sbuf_tensor("epsb", [128, 1], F32))
            R.xts = [e(nc.sbuf_tensor(f"xts{i}", [128, 4 * D], BF16)) for i in range(2)]
            R.sqsc = e(nc.sbuf_tensor("sqsc", [128, D], F32))
            R.xnb = [e(nc.sbuf_tensor(f"xnb{i}", [128, 4 * D], BF16)) for i in range(2)]
            R.xnT = e(nc.sbuf_tensor("xnT", [128, 8 * OT], BF16))
            R.sums = e(nc.sbuf_tensor("sums", [128, 4 * NOT], F32))
            R.sumsq = e(nc.sbuf_tensor("sumsq", [128, 4 * NOT], F32))
            R.lnt = e(nc.sbuf_tensor("lnt", [128, 24], F32))
            R.es = e(nc.sbuf_tensor("es", [128, 512], F32))
            R.expm = e(nc.sbuf_tensor("expm", [128, 512], F32))
            R.stg = [e(nc.sbuf_tensor(f"stg{i}", [128, 1024], BF16)) for i in range(2)]
            R.asb = e(nc.sbuf_tensor("asb", [128, 512], BF16))
            R.bsb = e(nc.sbuf_tensor("bsb", [128, 512], BF16))
            R.tcs = [e(nc.sbuf_tensor(f"tcs{i}", [128, 512], BF16)) for i in range(2)]
            R.tss = [e(nc.sbuf_tensor(f"tss{i}", [128, 512], BF16)) for i in range(2)]
            R.Qt = [e(nc.sbuf_tensor(f"Qt{i}", [128, 1024], BF16)) for i in range(2)]
            R.rp = [e(nc.sbuf_tensor(f"rp{i}", [128, 512], F32)) for i in range(3)]
            R.WT = [e(nc.sbuf_tensor(f"WT{i}", [128, 512], BF16)) for i in range(4)]
            R.Ost = [e(nc.sbuf_tensor(f"Ost{i}", [128, 512], BF16)) for i in range(2)]
            R.S1p = [e(nc.psum_tensor(f"S1p{i}", [128, 512], F32)) for i in range(3)]
            R.S2p = [e(nc.psum_tensor(f"S2p{i}", [128, 512], F32)) for i in range(3)]
            R.TPq = [e(nc.psum_tensor(f"TPq{i}", [128, 1024], BF16)) for i in range(2)]

            E.em = False
            E.reset()
            E.use_engines(t=_DummyEng(), v=_DummyEng(), s=_DummyEng(), g=_DummyEng(), sy=_DummyEng())
            for fn in (prog_sync_a, prog_gpsimd_a, prog_tensor_a, prog_vector_a, prog_scalar_a):
                fn(E, R)
            snapA = [c.n for c in E.ctrs]

            E._snapA = snapA

            Marks.strict = True
            E.em = True
            E.reset()
            with nc.Block() as blk:
                @blk.sync
                def _(sy):
                    E.use_engines(sy=sy)
                    prog_sync_a(E, R)

                @blk.gpsimd
                def _(g):
                    E.use_engines(g=g)
                    prog_gpsimd_a(E, R)

                @blk.tensor
                def _(t):
                    E.use_engines(t=t)
                    prog_tensor_a(E, R)

                @blk.vector
                def _(v):
                    E.use_engines(v=v)
                    prog_vector_a(E, R)

                @blk.scalar
                def _(s):
                    E.use_engines(s=s)
                    prog_scalar_a(E, R)
            assert [c.n for c in E.ctrs] == snapA, "pass A count mismatch"

        if STAGE < 3:
            return nc
        with ExitStack() as pb:
            e = pb.enter_context
            R.OTt = [e(nc.sbuf_tensor(f"OTt{i}", [128, 8 * 128], BF16)) for i in range(2)]
            R.ln2o = e(nc.sbuf_tensor("ln2o", [128, NT3 * D], BF16))
            R.xn3T = e(nc.sbuf_tensor("xn3T", [128, 8 * WIN], BF16))
            R.wouts = e(nc.sbuf_tensor("wouts", [128, 8 * D], BF16))
            R.bouts = e(nc.sbuf_tensor("bouts", [1, D], BF16))
            R.wrs = e(nc.sbuf_tensor("wrs", [128, 8 * D], BF16))
            R.wks = e(nc.sbuf_tensor("wks", [128, 8 * D], BF16))
            R.wv2s = e(nc.sbuf_tensor("wv2s", [128, 8 * D], BF16))
            R.murs = e(nc.sbuf_tensor("murs", [128, 8], F32))
            R.muks = e(nc.sbuf_tensor("muks", [128, 8], F32))
            R.omurs = e(nc.sbuf_tensor("omurs", [128, 8], F32))
            R.omuks = e(nc.sbuf_tensor("omuks", [128, 8], F32))
            R.zb3 = e(nc.sbuf_tensor("zb3", [128, 1], F32))
            R.epsb3 = e(nc.sbuf_tensor("epsb3", [128, 1], F32))
            R.ones3 = e(nc.sbuf_tensor("ones3", [1, 512], BF16))
            R.idn3 = e(nc.sbuf_tensor("idn3", [128, 128], BF16))
            R.hmm = e(nc.sbuf_tensor("hmm", [8, 1], BF16))
            R.hgos = e(nc.sbuf_tensor("hgos", [8, D], BF16))
            R.halo = e(nc.sbuf_tensor("halo", [1, D], BF16))
            R.haloT = e(nc.sbuf_tensor("haloT", [128, 8], BF16))
            R.xw = [e(nc.sbuf_tensor(f"xw{i}", [128, D], F32)) for i in range(2)]
            R.Psb = e(nc.sbuf_tensor("Psb", [128, D], F32))
            R.xr1 = e(nc.sbuf_tensor("xr1", [128, D], F32))
            R.xn3b = e(nc.sbuf_tensor("xn3b", [128, D], BF16))
            R.sq3 = e(nc.sbuf_tensor("sq3", [128, D], F32))
            R.ln4o = e(nc.sbuf_tensor("ln4o", [128, D], F32))
            R.l3 = e(nc.sbuf_tensor("l3", [128, 24], F32))
            R.mixs = e(nc.sbuf_tensor("mixs", [128, 512], BF16))
            R.xmrT = e(nc.sbuf_tensor("xmrT", [128, 8 * 512], BF16))
            R.xmkT = e(nc.sbuf_tensor("xmkT", [128, 8 * 512], BF16))
            R.kactS = e(nc.sbuf_tensor("kactS", [128, 8 * 512], BF16))
            R.kpre = e(nc.sbuf_tensor("kpre", [128, 512], BF16))
            R.rsig = e(nc.sbuf_tensor("rsig", [128, D], BF16))
            R.ffo = e(nc.sbuf_tensor("ffo", [128, D], F32))
            R.ysb = [e(nc.sbuf_tensor(f"ysb{i}", [128, D], F32)) for i in range(2)]
            R.Pp2 = [e(nc.psum_tensor(f"Pp{i}", [128, 1024], F32)) for i in range(2)]
            R.Kp = [e(nc.psum_tensor(f"Kp{i}", [128, 512], F32)) for i in range(2)]
            R.TP3 = [e(nc.psum_tensor(f"TP3{i}", [128, 512], BF16)) for i in range(2)]
            R.HP = R.Kp[0][0:1, 0:512]
            R.HS = R.Kp[1][:, 0:8].bitcast(BF16)

            Marks.strict = False
            E.em = False
            for c, n0 in zip(E.ctrs, E._snapA):
                c.n = n0
            E.use_engines(t=_DummyEng(), v=_DummyEng(), s=_DummyEng(), g=_DummyEng(), sy=_DummyEng())
            for fn in (prog_gpsimd_b, prog_sync_b, prog_tensor_b, prog_vector_b, prog_scalar_b):
                fn(E, R)
            snapB = [c.n for c in E.ctrs]

            Marks.strict = True
            E.em = True
            for c, n0 in zip(E.ctrs, E._snapA):
                c.n = n0
            with nc.Block() as blk3:
                @blk3.gpsimd
                def _(g):
                    E.use_engines(g=g)
                    prog_gpsimd_b(E, R)

                @blk3.sync
                def _(sy):
                    E.use_engines(sy=sy)
                    prog_sync_b(E, R)

                @blk3.tensor
                def _(t):
                    E.use_engines(t=t)
                    prog_tensor_b(E, R)

                @blk3.vector
                def _(v):
                    E.use_engines(v=v)
                    prog_vector_b(E, R)

                @blk3.scalar
                def _(s):
                    E.use_engines(s=s)
                    prog_scalar_b(E, R)
            assert [c.n for c in E.ctrs] == snapB, "pass B count mismatch"

    return nc


def _host_prep(inputs):
    x = np.ascontiguousarray(np.asarray(inputs["L0_attn_x"], dtype=np.float32).reshape(T, D))
    wqkv = np.asarray(inputs["L0_attn_w_qkv"], dtype=np.float32)
    bqkv = np.asarray(inputs["L0_attn_b_qkv"], dtype=np.float32)
    wout = np.ascontiguousarray(np.asarray(inputs["L0_attn_w_out"], dtype=np.float32))
    bout = np.asarray(inputs["L0_attn_b_out"], dtype=np.float32).reshape(1, D)
    Wr = np.asarray(inputs["L0_ffn_W_r"], dtype=np.float32)
    Wk = np.asarray(inputs["L0_ffn_W_k"], dtype=np.float32)
    Wv2 = np.asarray(inputs["L0_ffn_W_v"], dtype=np.float32)
    mur = np.ascontiguousarray(np.asarray(inputs["L0_ffn_mu_r"], dtype=np.float32).reshape(8, 128).T)
    muk = np.ascontiguousarray(np.asarray(inputs["L0_ffn_mu_k"], dtype=np.float32).reshape(8, 128).T)

    perm = np.concatenate([np.arange(0, 64, 2), np.arange(1, 64, 2)])
    p32 = np.arange(32, dtype=np.float64)
    theta = 1.0 / (10000.0 ** (p32 / 32.0))
    pos = np.arange(N, dtype=np.float64)
    ang = pos[:, None] * theta[None, :]
    cosv = np.cos(ang).T
    sinv = np.sin(ang).T
    tblc = np.empty((128, N), dtype=np.float32)
    tbls = np.empty((128, N), dtype=np.float32)
    for g in range(4):
        r0 = g * 32
        tblc[r0:r0 + 32] = cosv
        if g % 2 == 0:
            tbls[r0:r0 + 32] = -sinv
        else:
            tbls[r0:r0 + 32] = sinv
    permm = np.zeros((128, 128), dtype=np.float32)
    for i in range(128):
        j = i + 32 if (i // 32) % 2 == 0 else i - 32
        permm[j, i] = 1.0
    idn = np.eye(128, dtype=np.float32)

    wr_t = np.ascontiguousarray(Wr.T)
    wk_t = np.ascontiguousarray(Wk.T)
    wv2_t = np.ascontiguousarray(Wv2.T)

    in_maps = []
    for c in range(NCORE):
        h0 = 2 * c
        qcols = []
        for h in (h0, h0 + 1):
            qcols.append(np.concatenate([
                wqkv[:, h * 64 + perm],
                wqkv[:, D + h * 64 + perm],
            ], axis=1))
        wqk = np.ascontiguousarray(np.concatenate(qcols, axis=1))
        bqk = np.concatenate([
            np.concatenate([bqkv[h * 64 + perm], bqkv[D + h * 64 + perm]])
            for h in (h0, h0 + 1)]).reshape(1, 256).astype(np.float32)
        wv = np.ascontiguousarray(wqkv[:, 2 * D + h0 * 64: 2 * D + h0 * 64 + 128])
        bv = bqkv[2 * D + h0 * 64: 2 * D + h0 * 64 + 128].reshape(1, 128).astype(np.float32)
        hm = np.zeros((8, 1), dtype=np.float32)
        if c not in (0, 4):
            hm[c - 1, 0] = 1.0
        in_maps.append({
            "x_full": x,
            "x_win": np.ascontiguousarray(x[c * WIN:(c + 1) * WIN]),
            "wqk": wqk, "bqk": np.ascontiguousarray(bqk),
            "wv": wv, "bv": np.ascontiguousarray(bv),
            "wout": wout, "bout": np.ascontiguousarray(bout),
            "wr_t": wr_t, "wk_t": wk_t, "wv2_t": wv2_t,
            "mu_r": mur, "mu_k": muk,
            "tblc": tblc, "tbls": tbls, "permm": permm, "idn": idn,
            "hmask": hm,
        })
    return in_maps


def kernel(**inputs):
    in_maps = _host_prep(inputs)
    nc = build_nc()
    res = run_bass_kernel_spmd(nc, in_maps, core_ids=list(range(NCORE)))
    y = np.concatenate([np.asarray(res.results[c]["y"], dtype=np.float32)
                        for c in range(NCORE)], axis=0)
    return y.reshape(B, N, D)


# version 11
# speedup vs baseline: 1.9528x; 1.0770x over previous
import sys
sys.path.insert(0, "/opt/trn_rl_repo")
import numpy as np
import concourse.bass as bass
import concourse.mybir as mybir
from concourse.bass_utils import run_bass_kernel_spmd
from concourse.aot_env import allow_ici_all_to_all
from contextlib import ExitStack

F32 = mybir.dt.float32
BF16 = mybir.dt.bfloat16
AF = mybir.ActivationFunctionType
ALU = mybir.AluOpType
AXX = mybir.AxisListType.X

B = 2
N = 8192
T = 16384
D = 1024
NCORE = 8
WIN = 2048
OT = 512
NOT = T // OT
NI = N // OT
NJ = N // 128
NT3 = WIN // 128
NOT1 = WIN // OT
NFT = 16
A2C = 5 * WIN
EPS = 1e-5
STAGE = 3
P3SUB = 5
P3OPS = 63
LOOSE = False
VB_MODE = 0
SYB_MODE = 0


class Marks(dict):
    strict = False

    def __missing__(self, key):
        if Marks.strict and not LOOSE:
            raise KeyError(key)
        return 0


class _DummyInst:
    def then_inc(self, *a, **k):
        return self


class _DummyEng:
    def __getattr__(self, name):
        return lambda *a, **k: _DummyInst()


_PASSTHRU = {"wait_ge", "dma_start", "collective_compute", "nop", "drain",
             "indirect_dma_start", "sem_inc"}


class EngProxy:

    def __init__(self, real, ctr, em):
        self.raw = real
        self._ctr = ctr
        self._em = em

    def __getattr__(self, name):
        if name in _PASSTHRU:
            return getattr(self.raw, name)
        real_fn = getattr(self.raw, name)
        def call(*a, **k):
            c = self._ctr
            if self._em and c.n - (c.lag - 1) * c.step > 0:
                self.raw.wait_ge(c.sem, c.n - (c.lag - 1) * c.step)
            inst = real_fn(*a, **k)
            c.n += c.step
            if self._em and c.sem is not None:
                inst.then_inc(c.sem, c.step)
            return inst
        return call


class Ctr:
    def __init__(self, name, step):
        self.name = name
        self.step = step
        self.n = 0
        self.sem = None
        self.lag = 1

    def inc(self, inst):
        self.n += self.step
        inst.then_inc(self.sem, self.step) if self.sem is not None else None
        return inst


class Env:

    def __init__(self, nc, marks):
        self.nc = nc
        self.M = marks
        self.em = False
        self.t = self.sy = _DummyEng()
        self.v = self.s = self.g = None
        self.tt = Ctr("tt", 1)
        self.vv = Ctr("vv", 1)
        self.ss = Ctr("ss", 1)
        self.gc = Ctr("gc", 1)
        self.cc = Ctr("cc", 1)
        self.syn = Ctr("syn", 16)
        self.gd = Ctr("gd", 16)
        self.st = Ctr("st", 16)
        slot_names = ["gw", "gx0", "gx1", "gt0", "gt1", "gw3", "ghg",
                      "sg0", "sg1", "skk", "sqt0", "sqt1", "sos0", "sos1",
                      "sot0", "sot1", "sxw0", "sxw1", "shgi", "sy0", "sy1", "sv0", "sv1"]
        self.slots = {n: Ctr(n, 16) for n in slot_names}
        self.ctrs = [self.tt, self.vv, self.ss, self.gc, self.cc, self.syn, self.gd, self.st] + \
            [self.slots[n] for n in slot_names]

    def use_engines(self, t=None, v=None, s=None, g=None, sy=None):
        if t is not None:
            self.t = t
        if sy is not None:
            self.sy = sy
        if v is not None:
            self.v = EngProxy(v, self.vv, self.em)
        if s is not None:
            self.s = EngProxy(s, self.ss, self.em)
        if g is not None:
            self.g = EngProxy(g, self.gc, self.em)

    def reset(self):
        for c in self.ctrs:
            c.n = 0

    def sems(self, nc, stack):
        for c in self.ctrs:
            c.sem = stack.enter_context(nc.semaphore("sem_" + c.name))

    def w(self, eng, ctr, val):
        if self.em and val > 0:
            eng.wait_ge(ctr.sem, val)


def act_raw(E, eng, out_ap, in_ap, func, bias_ap, scale=1.0):
    c = eng._ctr
    if not E.em:
        c.n += c.step
        return _DummyInst()
    nc = E.nc
    with nc.discard():
        proto = eng.raw.activation(out_ap, in_ap, AF.Exp, bias=bias_ap, scale=scale).ins
    new = mybir.InstActivation(
        name=nc.get_next_instruction_name(), engine=proto.engine,
        ins=list(proto.ins), outs=list(proto.outs), func=func, debug=proto.debug)
    if c.n - (c.lag - 1) * c.step > 0:
        eng.raw.wait_ge(c.sem, c.n - (c.lag - 1) * c.step)
    inst = eng.raw.add_instruction(new)
    c.n += c.step
    inst.then_inc(c.sem, c.step)
    return inst



def prog_sync_a(E, R):
    sy, M = E.sy, E.M
    for ot in range(NOT1):
        lt = ot * OT
        for h in range(NFT):
            ft = ot * NFT + h
            sp = ft % 2
            d, hl = h // 2, h % 2
            sg = E.slots["sg%d" % sp]
            E.w(sy, E.vv, M[("p1stg", ot, h)])
            sg.inc(sy.dma_start(
                out=R.A2I2.ap()[128 * d:128 * d + 128,
                                2 * WIN + hl * WIN + lt: 2 * WIN + hl * WIN + lt + 512]
                    .rearrange("(g p) t -> p g t", g=2),
                in_=R.stg[sp][0:64, :].rearrange("p (g t) -> p g t", g=2)))
            sg.inc(sy.dma_start(
                out=R.A2I2.ap()[128 * d:128 * d + 128,
                                hl * WIN + lt: hl * WIN + lt + 512]
                    .rearrange("(g p) t -> p g t", g=2),
                in_=R.stg[sp][64:128, :].rearrange("p (g t) -> p g t", g=2)))
            M[("p1syn_stg", ot, h)] = (sg, sg.n)
        sv = E.slots["sv%d" % (ot % 2)]
        for sb in range(4):
            jt = ot * 4 + sb
            E.w(sy, E.ss, M[("p1vst", ot, sb)])
            sv.inc(sy.dma_start(
                out=R.A2I2.ap().rearrange("(d p) c -> p d c", d=8)[:, :,
                    4 * WIN + jt * 128: 4 * WIN + jt * 128 + 128],
                in_=R.Vst[:, :].rearrange("p (d c) -> p d c", d=8)))
            M[("p1vstd", ot, sb)] = (sv, sv.n)
    M["p1sg0_end"] = E.slots["sg0"].n
    M["p1sg1_end"] = E.slots["sg1"].n
    M["p1sv0_end"] = E.slots["sv0"].n
    M["p1sv1_end"] = E.slots["sv1"].n
    if STAGE < 2:
        M["p2syn_end"] = E.syn.n
        return
    def store_O(bi):
        b, I = bi // NI, bi % NI
        gtok = b * N + I * OT
        w_, c0 = gtok // WIN, gtok % WIN
        so = E.slots["sos%d" % (bi % 2)]
        so.inc(sy.dma_start(out=R.A2I.ap()[128 * w_:128 * w_ + 128, c0:c0 + 512],
                            in_=R.Ost[bi % 2][:, :]))
        M[("p2ost", bi)] = (so, so.n)

    E.w(sy, E.cc, M["a2a2"])
    for bi in range(B * NI):
        b, I = bi // NI, bi % NI
        buf = bi % 2
        if bi >= 2:
            E.w(sy, E.tt, M[("p2tte", bi - 2)])
        gt = b * N + I * OT
        s_src = gt // WIN
        off = gt % WIN
        sq = E.slots["sqt%d" % buf]
        sq.inc(sy.dma_start(
            out=R.Qt[buf][:, :].rearrange("p (h t) -> p h t", h=2),
            in_=R.A2O2.ap()[128 * s_src:128 * s_src + 128, 2 * WIN:4 * WIN]
                .rearrange("p (h n) -> p h n", h=2)[:, :, off:off + 512]))
        M[("p2qt", bi)] = (sq, sq.n)
        if bi >= 1:
            E.w(sy, E.vv, M[("p2evac", bi - 1)])
            store_O(bi - 1)
    E.w(sy, E.vv, M[("p2evac", B * NI - 1)])
    store_O(B * NI - 1)
    M["p2sos0_end"] = E.slots["sos0"].n
    M["p2sos1_end"] = E.slots["sos1"].n


def prog_gpsimd_a(E, R):
    g, M = E.g, E.M
    g.memset(R.zb[:, :], 0.0)
    g.memset(R.epsb[:, :], EPS)
    g.memset(R.ones[:, :], 1.0)
    M["p1memset"] = E.gc.n
    gw = E.slots["gw"]
    gw.inc(g.dma_start(out=R.wqk[:, :].rearrange("p (c f) -> p c f", c=8),
                       in_=R.dwqk.ap().rearrange("(c p) f -> p c f", c=8)))
    gw.inc(g.dma_start(out=R.wv[:, :].rearrange("p (c f) -> p c f", c=8),
                       in_=R.dwv.ap().rearrange("(c p) f -> p c f", c=8)))
    gw.inc(g.dma_start(out=R.bqk[:, :], in_=R.dbqk.ap()))
    gw.inc(g.dma_start(out=R.bv[:, :], in_=R.dbv.ap()))
    gw.inc(g.dma_start(out=R.permb[:, :], in_=R.dperm.ap()))
    gw.inc(g.dma_start(out=R.idnb[:, :], in_=R.didn.ap()))
    M["p1w"] = gw.n
    for ot in range(NOT1):
        bsel = 0
        lt = ot * OT
        if ot >= 1:
            E.w(g, E.ss, M[("p1norm", ot - 1)])
            E.w(g, E.vv, M[("p1stat", ot - 1)])
            E.w(g, E.vv, M[("p1stg", ot - 1, NFT - 1)])
        gx = E.slots["gx0"]
        gt_s = E.slots["gt0"]
        gx.inc(g.dma_start(
            out=R.xts[bsel][:, :].rearrange("p (s c) -> p s c", s=4),
            in_=R.dxw.ap()[lt:lt + 512, :].rearrange("(s p) c -> p s c", s=4)))
        gt_s.inc(g.dma_start(out=R.tcs[0][:, :], in_=R.dtblc.ap()[:, lt:lt + 512]))
        gt_s.inc(g.dma_start(out=R.tss[0][:, :], in_=R.dtbls.ap()[:, lt:lt + 512]))
        M[("p1x", ot)] = (gx, gx.n)
        M[("p1tb", ot)] = (gt_s, gt_s.n)
    for sl in ("sg0", "sg1", "sv0", "sv1"):
        E.w(g, E.slots[sl], M["p1" + sl + "_end"])
    if E.em:
        with allow_ici_all_to_all():
            E.cc.inc(g.collective_compute(
                "AllToAll", ALU.bypass, replica_groups=[list(range(8))],
                ins=[R.A2I2.ap().opt()], outs=[R.A2O2.ap().opt()]))
    else:
        E.cc.inc(_DummyInst())
    M["a2a2"] = E.cc.n
    E.w(g, E.cc, M["a2a2"])
    gkv = E.slots["skk"]
    for s_src in range(8):
        for hl in range(2):
            gkv.inc(g.dma_start(
                out=R.KK[:, hl * T + s_src * WIN: hl * T + s_src * WIN + WIN],
                in_=R.A2O2.ap()[128 * s_src:128 * s_src + 128, hl * WIN:(hl + 1) * WIN]))
        gkv.inc(g.dma_start(
            out=R.Vt[:, s_src * WIN:(s_src + 1) * WIN],
            in_=R.A2O2.ap()[128 * s_src:128 * s_src + 128, 4 * WIN:5 * WIN]))
    M["p1kv"] = gkv.n
    if STAGE < 2:
        M["a2a"] = E.cc.n
        return
    E.w(g, E.slots["sos0"], M["p2sos0_end"])
    E.w(g, E.slots["sos1"], M["p2sos1_end"])
    if E.em:
        with allow_ici_all_to_all():
            E.cc.inc(g.collective_compute(
                "AllToAll", ALU.bypass, replica_groups=[list(range(8))],
                ins=[R.A2I.ap().opt()], outs=[R.A2O.ap().opt()]))
    else:
        E.cc.inc(_DummyInst())
    M["a2a"] = E.cc.n


def prog_tensor_a(E, R):
    t, M = E.t, E.M
    E.w(t, E.slots["gw"], M["p1w"])
    E.w(t, E.gc, M["p1memset"])
    QKb = [R.S1p[0][:, :], R.S1p[1][:, :]]
    SWp = R.S2p[1][:, :]
    Vpp = R.S2p[0][:, :]
    VB = [Vpp, SWp]
    for ot in range(NOT1):
        E.w(t, E.ss, M[("p1norm", ot)])
        for ccn in range(8):
            tp = R.TPq[ccn % 2]
            gcc = ot * 8 + ccn
            if gcc >= 2:
                ev = M[("p1tpev", gcc - 2)]
                if ev:
                    E.w(t, ev[0], ev[1])
            for sb in range(4):
                inst = t.transpose(tp[:, sb * 128:(sb + 1) * 128],
                                   R.xnb[0][:, sb * D + ccn * 128: sb * D + ccn * 128 + 128],
                                   R.idnb[:, :])
            E.tt.inc(inst)
            M[("p1tp", ot, ccn)] = E.tt.n
        E.w(t, E.vv, M[("p1trev_v", ot)])
        E.w(t, E.ss, M[("p1trev_s", ot)])
        for h in range(NFT):
            ft = ot * NFT + h
            if ft >= 2:
                E.w(t, E.vv, M[("p1stg", (ft - 2) // NFT, (ft - 2) % NFT)])
            for ccn in range(8):
                t.matmul(QKb[h % 2], R.wqk[:, ccn * 2048 + h * 128: ccn * 2048 + h * 128 + 128],
                         R.xnT[:, ccn * OT: ccn * OT + 512], start=(ccn == 0), stop=False)
            E.tt.inc(t.matmul(QKb[h % 2], R.bqk[:, h * 128:(h + 1) * 128], R.ones[:, 0:512],
                              start=False, stop=True))
            M[("p1qk", ot, h)] = E.tt.n
            sp = ft % 2
            E.w(t, E.vv, M[("p1E", ot, h)])
            if ft >= 1:
                E.w(t, E.vv, M[("p1ropeA", (ft - 1) // NFT, (ft - 1) % NFT)])
            E.tt.inc(t.matmul(SWp, R.permb[:, :], R.stg[sp][:, 512:1024],
                              start=True, stop=True))
            M[("p1sw", ot, h)] = E.tt.n
        E.w(t, E.vv, M[("p1ropeA", ot, NFT - 1)])
        for sb in range(4):
            for fh in range(2):
                ch = sb * 2 + fh
                if ot * 8 + ch >= 2:
                    ev = M[("p1vch", ot * 8 + ch - 2)]
                    if ev:
                        E.w(t, ev[0], ev[1])
                for ccn in range(8):
                    t.matmul(VB[ch % 2],
                             R.xnT[:, ccn * OT + sb * 128: ccn * OT + sb * 128 + 128],
                             R.wv[:, ccn * 1024 + fh * 512: ccn * 1024 + fh * 512 + 512],
                             start=(ccn == 0), stop=False)
                inst = t.matmul(VB[ch % 2], R.ones[:, 0:128], R.bv[:, fh * 512:(fh + 1) * 512],
                                start=False, stop=True)
                E.tt.inc(inst)
                M[("p1vmm", ot, ch)] = E.tt.n
    if STAGE < 2:
        return
    E.w(t, E.slots["skk"], M["p1kv"])
    E.w(t, E.vv, M["p1vv_end"])
    E.w(t, E.ss, M["p1ss_end"])
    ACCap = [R.TPq[0][:, :].bitcast(F32), R.TPq[1][:, :].bitcast(F32)]
    u = 0
    for bi in range(B * NI):
        b, I = bi // NI, bi % NI
        buf = bi % 2
        qtm = M[("p2qt", bi)]
        if qtm:
            E.w(t, qtm[0], qtm[1])
        for h in range(2):
            hb = h * T + b * N
            for J in range(NJ):
                p = u % 3
                if u >= 3:
                    E.w(t, E.vv, M[("p2w", u - 3)])
                t.matmul(R.S1p[p][:, :], R.KK[0:64, hb + J * 128: hb + J * 128 + 128],
                         R.Qt[buf][0:64, h * 512:(h + 1) * 512],
                         start=True, stop=True, tile_position=(0, 0))
                E.tt.inc(t.matmul(R.S2p[p][:, :], R.KK[64:128, hb + J * 128: hb + J * 128 + 128],
                                  R.Qt[buf][64:128, h * 512:(h + 1) * 512],
                                  start=True, stop=True, tile_position=(64, 0)))
                M[("p2s", u)] = E.tt.n
                if J >= 3:
                    _mm3(E, R, ACCap, b, h, J - 3, u - 3, bi)
                u += 1
            _mm3(E, R, ACCap, b, h, NJ - 3, u - 3, bi)
            _mm3(E, R, ACCap, b, h, NJ - 2, u - 2, bi)
            _mm3(E, R, ACCap, b, h, NJ - 1, u - 1, bi)
        M[("p2tte", bi)] = E.tt.n
    M["p2tt_end"] = E.tt.n


def _mm3(E, R, ACCap, b, h, J, u, bi):
    t, M = E.t, E.M
    if J == 0 and bi >= 1:
        E.w(t, E.vv, M[("p2evac", bi - 1)])
    E.w(t, E.vv, M[("p2w", u)])
    vcol = (b * 64 + J) * 128 + h * 64
    if h == 0:
        inst = t.matmul(ACCap[0][0:64, :], R.Vt[:, vcol:vcol + 64], R.WT[u % 4][:, :],
                        start=(J == 0), stop=(J == NJ - 1))
    else:
        inst = t.matmul(ACCap[1][64:128, :], R.Vt[:, vcol:vcol + 64], R.WT[u % 4][:, :],
                        start=(J == 0), stop=(J == NJ - 1), tile_position=(0, 64))
    if J == NJ - 1:
        E.tt.inc(inst)


def prog_vector_a(E, R):
    v, M = E.v, E.M
    QKb = [R.S1p[0][:, :], R.S1p[1][:, :]]
    SWp = R.S2p[1][:, :]
    Vpp = R.S2p[0][:, :]
    for ot in range(NOT1):
        xm = M[("p1x", ot)]
        if xm:
            E.w(v, xm[0], xm[1])
        for sb in range(4):
            inst = v.reduce_sum(R.sums[:, ot * 4 + sb: ot * 4 + sb + 1],
                                R.xts[0][:, sb * D:(sb + 1) * D], axis=AXX)
        M[("p1stat", ot)] = E.vv.n
        E.w(v, E.ss, M[("p1sq", ot)])
        c4 = slice(ot * 4, ot * 4 + 4)
        v.tensor_scalar(R.lnt[:, 0:4], R.sums[:, c4], 1.0 / D, None, ALU.mult)
        v.tensor_scalar(R.lnt[:, 4:8], R.sumsq[:, c4], 1.0 / D, None, ALU.mult)
        v.tensor_tensor(R.lnt[:, 8:12], R.lnt[:, 0:4], R.lnt[:, 0:4], ALU.mult)
        v.tensor_tensor(R.lnt[:, 12:16], R.lnt[:, 4:8], R.lnt[:, 8:12], ALU.subtract)
        M[("p1var", ot)] = E.vv.n
        E.w(v, E.ss, M[("p1rstd", ot)])
        v.tensor_scalar(R.lnt[:, 20:24], R.lnt[:, 0:4], -1.0, None, ALU.mult)
        v.tensor_tensor(R.lnt[:, 20:24], R.lnt[:, 20:24], R.lnt[:, 16:20], ALU.mult)
        M[("p1negmr", ot)] = E.vv.n
        for ccn in range(0, 8, 2):
            E.w(v, E.tt, M[("p1tp", ot, ccn)])
            if ot >= 1 and ccn == 0:
                E.w(v, E.tt, M[("p1vmm", ot - 1, 7)])
            v.tensor_copy(R.xnT[:, ccn * OT: ccn * OT + 512], R.TPq[ccn % 2][:, 0:512])
            M[("p1tpev", ot * 8 + ccn)] = (E.vv, E.vv.n)
        M[("p1trev_v", ot)] = E.vv.n
        for h in range(NFT):
            ft = ot * NFT + h
            sp = ft % 2
            E.w(v, E.tt, M[("p1qk", ot, h)])
            if ft >= 1:
                E.w(v, E.ss, M[("p1exp", (ft - 1) // NFT, (ft - 1) % NFT)])
            v.tensor_scalar_min(R.es[:, :], QKb[h % 2], 0.0)
            M[("p1min", ot, h)] = E.vv.n
            E.w(v, E.ss, M[("p1exp", ot, h)])
            if ft >= 2:
                sgm = M[("p1syn_stg", (ft - 2) // NFT, (ft - 2) % NFT)]
                if sgm:
                    E.w(v, sgm[0], sgm[1])
            v.scalar_tensor_tensor(R.stg[sp][:, 512:1024], QKb[h % 2], 0.0,
                                            R.expm[:, :], ALU.max, ALU.add)
            M[("p1E", ot, h)] = E.vv.n
            E.w(v, E.tt, M[("p1sw", ot, h)])
            tbm = M[("p1tb", ot)]
            if tbm:
                E.w(v, tbm[0], tbm[1])
            v.tensor_tensor(R.asb[:, :], SWp, R.tss[0][:, :], ALU.mult)
            M[("p1ropeA", ot, h)] = E.vv.n
            v.tensor_tensor(R.bsb[:, :], R.stg[sp][:, 512:1024], R.tcs[0][:, :], ALU.mult)
            v.tensor_tensor(R.stg[sp][:, 0:512], R.asb[:, :], R.bsb[:, :], ALU.add)
            M[("p1stg", ot, h)] = E.vv.n
    M["p1vv_end"] = E.vv.n
    if STAGE < 2:
        return
    ACCap = [R.TPq[0][:, :].bitcast(F32), R.TPq[1][:, :].bitcast(F32)]
    u = 0
    E.vv.lag = 3
    for bi in range(B * NI):
        for h in range(2):
            for J in range(NJ):
                p = u % 3
                E.w(v, E.ss, M[("p2r", u)])
                v.tensor_tensor(R.WT[u % 4][:, :], R.S1p[p][:, :], R.rp[p][:, :], ALU.mult)
                M[("p2w", u)] = E.vv.n
                u += 1
        E.w(v, E.tt, M[("p2tte", bi)])
        if bi >= 2:
            osm = M[("p2ost", bi - 2)]
            if osm:
                E.w(v, osm[0], osm[1])
        v.tensor_copy(R.Ost[bi % 2][0:64, :], ACCap[0][0:64, 0:512])
        v.tensor_copy(R.Ost[bi % 2][64:128, :], ACCap[1][64:128, 0:512])
        M[("p2evac", bi)] = E.vv.n
    E.vv.lag = 1


def prog_scalar_a(E, R):
    s, M = E.s, E.M
    VB = [R.S2p[0][:, :], R.S2p[1][:, :]]
    for ot in range(NOT1):
        xm = M[("p1x", ot)]
        if xm:
            E.w(s, xm[0], xm[1])
        E.w(s, E.gc, M["p1memset"])
        for sb in range(4):
            inst = s.activation(R.sqsc[:, :], R.xts[0][:, sb * D:(sb + 1) * D],
                                AF.Square, bias=R.zb[:, :],
                                accum_out=R.sumsq[:, ot * 4 + sb: ot * 4 + sb + 1])
        M[("p1sq", ot)] = E.ss.n
        E.w(s, E.vv, M[("p1var", ot)])
        s.activation(R.lnt[:, 16:20], R.lnt[:, 12:16], AF.Sqrt, bias=R.epsb[:, :])
        act_raw(E, s, R.lnt[:, 16:20], R.lnt[:, 16:20], AF.Reciprocal, R.zb[:, :])
        M[("p1rstd", ot)] = E.ss.n
        E.w(s, E.vv, M[("p1negmr", ot)])
        if ot >= 1:
            E.w(s, E.tt, M[("p1tp", ot - 1, 7)])
        for sb in range(4):
            inst = s.activation(R.xnb[0][:, sb * D:(sb + 1) * D],
                                R.xts[0][:, sb * D:(sb + 1) * D], AF.Identity,
                                bias=R.lnt[:, 20 + sb:21 + sb], scale=R.lnt[:, 16 + sb:17 + sb])
        M[("p1norm", ot)] = E.ss.n
        for ccn in range(1, 8, 2):
            E.w(s, E.tt, M[("p1tp", ot, ccn)])
            if ot >= 1 and ccn == 1:
                E.w(s, E.tt, M[("p1vmm", ot - 1, 7)])
            s.copy(R.xnT[:, ccn * OT: ccn * OT + 512], R.TPq[ccn % 2][:, 0:512])
            M[("p1tpev", ot * 8 + ccn)] = (E.ss, E.ss.n)
        M[("p1trev_s", ot)] = E.ss.n
        for h in range(NFT):
            ft = ot * NFT + h
            E.w(s, E.vv, M[("p1min", ot, h)])
            if ft >= 1:
                E.w(s, E.vv, M[("p1E", (ft - 1) // NFT, (ft - 1) % NFT)])
            s.activation(R.expm[:, :], R.es[:, :], AF.Exp, bias=R.zb[:, :])
            M[("p1exp", ot, h)] = E.ss.n
        for sb in range(4):
            for fh in range(2):
                ch = sb * 2 + fh
                E.w(s, E.tt, M[("p1vmm", ot, ch)])
                if sb >= 1 and fh == 0:
                    vdm = M[("p1vstd", ot, sb - 1)]
                    if vdm:
                        E.w(s, vdm[0], vdm[1])
                if ot >= 1 and sb == 0 and fh == 0:
                    vdm = M[("p1vstd", ot - 1, 3)]
                    if vdm:
                        E.w(s, vdm[0], vdm[1])
                s.copy(R.Vst[:, fh * 512:(fh + 1) * 512], VB[ch % 2])
                M[("p1vch", ot * 8 + ch)] = (E.ss, E.ss.n)
                if fh == 1:
                    M[("p1vst", ot, sb)] = E.ss.n
    M["p1ss_end"] = E.ss.n
    if STAGE < 2:
        return
    u = 0
    E.ss.lag = 3
    for bi in range(B * NI):
        for h in range(2):
            for J in range(NJ):
                p = u % 3
                E.w(s, E.tt, M[("p2s", u)])
                if u >= 3:
                    E.w(s, E.vv, M[("p2w", u - 3)])
                act_raw(E, s, R.rp[p][:, :], R.S2p[p][:, :], AF.Reciprocal, R.zb[:, :])
                M[("p2r", u)] = E.ss.n
                u += 1
    E.ss.lag = 1



def prog_gpsimd_b(E, R):
    g, M = E.g, E.M
    E.w(g, E.cc, M["a2a"])
    g.memset(R.zb3[:, :], 0.0)
    g.memset(R.epsb3[:, :], EPS)
    g.memset(R.ones3[:, :], 1.0)
    M["p3memset"] = E.gc.n
    gw3 = E.slots["gw3"]
    for src, dst in ((R.dwout, R.wouts), (R.dwr, R.wrs), (R.dwk, R.wks), (R.dwv2, R.wv2s)):
        gw3.inc(g.dma_start(out=dst[:, :].rearrange("p (c f) -> p c f", c=8),
                            in_=src.ap().rearrange("(c p) f -> p c f", c=8)))
    gw3.inc(g.dma_start(out=R.bouts[:, :], in_=R.dbout.ap()))
    gw3.inc(g.dma_start(out=R.idn3[:, :], in_=R.didn.ap()))
    gw3.inc(g.dma_start(out=R.hmm[:, :], in_=R.dhm.ap()))
    gw3.inc(g.dma_start(out=R.murs[:, :], in_=R.dmur.ap()))
    gw3.inc(g.dma_start(out=R.muks[:, :], in_=R.dmuk.ap()))
    M["p3w"] = gw3.n
    if P3SUB < 2:
        return
    E.w(g, E.slots["shgi"], M["p3hgi"])
    if E.em:
        E.cc.inc(g.collective_compute(
            "AllGather", ALU.bypass, replica_groups=[list(range(8))],
            ins=[R.HGI.ap().opt()], outs=[R.HGO.ap().opt()]))
    else:
        E.cc.inc(_DummyInst())
    M["hag"] = E.cc.n
    E.w(g, E.cc, M["hag"])
    E.slots["ghg"].inc(g.dma_start(out=R.hgos[:, :], in_=R.HGO.ap()))
    M["p3hgo"] = E.slots["ghg"].n


def prog_sync_b(E, R):
    sy, M = E.sy, E.M
    E.w(sy, E.cc, M["a2a"])
    for it in range(NT3):
        buf = it % 2
        if it >= 2:
            E.w(sy, E.tt, M[("p3op", it - 2)])
        sot = E.slots["sot%d" % buf]
        sot.inc(sy.dma_start(
            out=R.OTt[buf][:, :].rearrange("p (c t) -> p c t", c=8),
            in_=R.A2O.ap().rearrange("(c p) t -> p c t", c=8)[:, :, it * 128:(it + 1) * 128]))
        M[("p3ot", it)] = (sot, sot.n)
        if it >= 2:
            E.w(sy, E.vv, M[("p3xr1", it - 2)])
        sxw = E.slots["sxw%d" % buf]
        sxw.inc(sy.dma_start(out=R.xw[buf][:, :], in_=R.dxw.ap()[it * 128:(it + 1) * 128, :]))
        M[("p3xw", it)] = (sxw, sxw.n)
    if P3SUB < 2:
        E.w(sy, E.vv, M[("p3xn3", NT3 - 1)])
        for half in range(2):
            ev = M[("p3t3ev", (NT3 - 1) * 2 + half)]
            if ev:
                E.w(sy, ev[0], ev[1])
        return
    E.w(sy, E.vv, M[("p3xn3", NT3 - 1)])
    E.slots["shgi"].inc(sy.dma_start(out=R.HGI.ap(), in_=R.xn3b[127:128, :]))
    M["p3hgi"] = E.slots["shgi"].n
    if P3SUB < 5:
        return
    for it in range(NT3):
        buf = it % 2
        if it < 2:
            E.w(sy, E.vv, M[("p3xr1", NT3 - 2 + it)])
        else:
            E.w(sy, E.vv, M[("p3y", it - 2)])
        sxw = E.slots["sxw%d" % buf]
        sxw.inc(sy.dma_start(out=R.xw[buf][:, :], in_=R.dxw.ap()[it * 128:(it + 1) * 128, :]))
        M[("p3xw2", it)] = (sxw, sxw.n)
        E.w(sy, E.vv, M[("p3y", it)])
        syt = E.slots["sy%d" % (it % 2)]
        syt.inc(sy.dma_start(out=R.dy.ap()[it * 128:(it + 1) * 128, :], in_=R.ysb[it % 2][:, :]))
        M[("p3yst", it)] = (syt, syt.n)
    if E.em:
        sy.wait_ge(E.slots["sy0"].sem, E.slots["sy0"].n)
        sy.wait_ge(E.slots["sy1"].sem, E.slots["sy1"].n)


def prog_tensor_b(E, R):
    t, M = E.t, E.M
    E.w(t, E.slots["gw3"], M["p3w"])
    E.w(t, E.gc, M["p3memset"])
    for it in range(NT3):
        buf = it % 2
        if not (P3OPS & 1):
            break
        otm = M[("p3ot", it)]
        if otm:
            E.w(t, otm[0], otm[1])
        if it >= 2:
            E.w(t, E.ss, M[("p3pev", it - 2)])
            E.w(t, E.vv, M[("p3ps", it - 2)])
        for fh in range(2):
            for ccn in range(8):
                t.matmul(R.Pp2[it % 2][:, fh * 512:(fh + 1) * 512],
                         R.OTt[buf][:, ccn * 128:(ccn + 1) * 128],
                         R.wouts[:, ccn * D + fh * 512: ccn * D + fh * 512 + 512],
                         start=(ccn == 0), stop=False)
            inst = t.matmul(R.Pp2[it % 2][:, fh * 512:(fh + 1) * 512], R.ones3[:, 0:128],
                            R.bouts[:, fh * 512:(fh + 1) * 512], start=False, stop=True)
        E.tt.inc(inst)
        M[("p3op", it)] = E.tt.n
        if not (P3OPS & 32):
            continue
        E.w(t, E.vv, M[("p3xn3", it)])
        for half in range(2):
            tp = R.TP3[half]
            ghalf = it * 2 + half
            if ghalf >= 2:
                ev = M[("p3t3ev", ghalf - 2)]
                if ev:
                    E.w(t, ev[0], ev[1])
            for q in range(4):
                ccn = half * 4 + q
                inst = t.transpose(tp[:, q * 128:(q + 1) * 128],
                                   R.xn3b[:, ccn * 128:(ccn + 1) * 128], R.idn3[:, :])
            E.tt.inc(inst)
            M[("p3t3", it, half)] = E.tt.n
    if P3SUB < 2:
        return
    E.w(t, E.slots["ghg"], M["p3hgo"])
    for fh in range(2):
        E.tt.inc(t.matmul(R.HP[0:1, :], R.hmm[:, :], R.hgos[:, fh * 512:(fh + 1) * 512],
                          start=True, stop=True))
        M[("p3hp", fh)] = E.tt.n
        E.w(t, E.ss, M[("p3hev", fh)])
    E.w(t, E.ss, M[("p3hev", 1)])
    for ccn in range(8):
        inst = t.transpose(R.HS[:, 2 * ccn:2 * ccn + 1], R.halo[0:1, ccn * 128:(ccn + 1) * 128],
                           R.idn3[0:1, 0:1])
    E.tt.inc(inst)
    M["p3hst"] = E.tt.n
    if P3SUB < 4:
        return
    for ts4 in range(4):
        for dt_ in range(8):
            u = ts4 * 8 + dt_
            p = u % 2
            if u >= 2:
                E.w(t, E.ss, M[("p3kact", u - 2)])
            E.w(t, E.vv, M[("p3mixk", ts4)])
            for ccn in range(8):
                inst = t.matmul(R.Kp[p][:, :],
                                R.wks[:, ccn * D + dt_ * 128: ccn * D + dt_ * 128 + 128],
                                R.xmkT[:, ccn * 512:(ccn + 1) * 512],
                                start=(ccn == 0), stop=(ccn == 7))
            E.tt.inc(inst)
            M[("p3k", u)] = E.tt.n
        if P3SUB < 5:
            continue
        for q in range(4):
            it = ts4 * 4 + q
            E.w(t, E.vv, M[("p3mixr", ts4)])
            if it >= 2:
                E.w(t, E.ss, M[("p3sig", it - 2)])
                E.w(t, E.vv, M[("p3ff", it - 2)])
            for fh in range(2):
                for ccn in range(8):
                    inst = t.matmul(R.Pp2[it % 2][:, fh * 512:(fh + 1) * 512],
                                    R.xmrT[:, ccn * 512 + q * 128: ccn * 512 + q * 128 + 128],
                                    R.wrs[:, ccn * D + fh * 512: ccn * D + fh * 512 + 512],
                                    start=(ccn == 0), stop=(ccn == 7))
            E.tt.inc(inst)
            M[("p3r", it)] = E.tt.n
            E.w(t, E.ss, M[("p3sig", it)])
            E.w(t, E.ss, M[("p3kact", ts4 * 8 + 7)])
            for fh in range(2):
                for ccn in range(8):
                    inst = t.matmul(R.Pp2[it % 2][:, fh * 512:(fh + 1) * 512],
                                    R.kactS[:, ccn * 512 + q * 128: ccn * 512 + q * 128 + 128],
                                    R.wv2s[:, ccn * D + fh * 512: ccn * D + fh * 512 + 512],
                                    start=(ccn == 0), stop=(ccn == 7))
            E.tt.inc(inst)
            M[("p3vo", it)] = E.tt.n


def prog_vector_b(E, R):
    v, M = E.v, E.M
    E.w(v, E.slots["gw3"], M["p3w"])
    v.tensor_scalar(R.omurs[:, :], R.murs[:, :], -1.0, 1.0, ALU.mult, ALU.add)
    v.tensor_scalar(R.omuks[:, :], R.muks[:, :], -1.0, 1.0, ALU.mult, ALU.add)
    for it in range(NT3):
        if not (P3OPS & 2):
            break
        E.w(v, E.tt, M[("p3op", it)])
        if VB_MODE == 0:
            v.reduce_sum(R.l3[:, 0:1], R.Pp2[it % 2][:, :], axis=AXX)
        elif VB_MODE == 1:
            v.reduce_sum(R.l3[:, 0:1], R.Pp2[it % 2][:, 0:512], axis=AXX)
        elif VB_MODE == 2:
            v.tensor_copy(R.l3[:, 0:8], R.Pp2[it % 2][:, 0:8])
        else:
            v.tensor_copy(R.l3[:, 0:8], R.xw[0][:, 0:8])
        M[("p3ps", it)] = E.vv.n
        if not (P3OPS & 8):
            continue
        E.w(v, E.ss, M[("p3pev", it)])
        v.tensor_scalar(R.l3[:, 2:3], R.l3[:, 0:1], 1.0 / D, None, ALU.mult)
        v.tensor_scalar(R.l3[:, 3:4], R.l3[:, 1:2], 1.0 / D, None, ALU.mult)
        v.tensor_tensor(R.l3[:, 4:5], R.l3[:, 2:3], R.l3[:, 2:3], ALU.mult)
        v.tensor_tensor(R.l3[:, 5:6], R.l3[:, 3:4], R.l3[:, 4:5], ALU.subtract)
        M[("p3var2", it)] = E.vv.n
        E.w(v, E.ss, M[("p3rs2", it)])
        v.tensor_scalar(R.ln2o[:, it * D:(it + 1) * D], R.Psb[:, :],
                                 R.l3[:, 2:3], R.l3[:, 6:7], ALU.subtract, ALU.mult)
        M[("p3ln2", it)] = E.vv.n
        if not (P3OPS & 16):
            continue
        xwm = M[("p3xw", it)]
        if xwm:
            E.w(v, xwm[0], xwm[1])
        v.tensor_tensor(R.xr1[:, :], R.xw[it % 2][:, :],
                                 R.ln2o[:, it * D:(it + 1) * D], ALU.add)
        M[("p3xr1", it)] = E.vv.n
        v.reduce_sum(R.l3[:, 8:9], R.xr1[:, :], axis=AXX)
        E.w(v, E.ss, M[("p3sq3", it)])
        v.tensor_scalar(R.l3[:, 10:11], R.l3[:, 8:9], 1.0 / D, None, ALU.mult)
        v.tensor_scalar(R.l3[:, 11:12], R.l3[:, 9:10], 1.0 / D, None, ALU.mult)
        v.tensor_tensor(R.l3[:, 12:13], R.l3[:, 10:11], R.l3[:, 10:11], ALU.mult)
        v.tensor_tensor(R.l3[:, 13:14], R.l3[:, 11:12], R.l3[:, 12:13], ALU.subtract)
        M[("p3var3", it)] = E.vv.n
        E.w(v, E.ss, M[("p3rs3", it)])
        if it >= 1:
            E.w(v, E.tt, M[("p3t3", it - 1, 1)])
        v.tensor_scalar(R.xn3b[:, :], R.xr1[:, :],
                                 R.l3[:, 10:11], R.l3[:, 14:15], ALU.subtract, ALU.mult)
        M[("p3xn3", it)] = E.vv.n
        for half in range(2):
            E.w(v, E.tt, M[("p3t3", it, half)])
            dst = R.xn3T[:, :].rearrange("p (c w) -> p c w", c=8)[:, half * 4:half * 4 + 4,
                                                                 it * 128:it * 128 + 128]
            v.tensor_copy(dst, R.TP3[half][:, :].rearrange("p (c w) -> p c w", c=4))
            M[("p3t3ev", it * 2 + half)] = (E.vv, E.vv.n)
    if P3SUB < 2:
        return
    E.w(v, E.tt, M["p3hst"])
    v.tensor_copy(R.haloT[:, :], R.HS[:, 0:16:2])
    M["p3halot"] = E.vv.n
    if P3SUB < 3:
        return
    for ts4 in range(4):
        t0 = ts4 * 512
        for mu, omu, dst, mk in ((R.murs, R.omurs, R.xmrT, "p3mixr"), (R.muks, R.omuks, R.xmkT, "p3mixk")):
            if ts4 == 0:
                E.w(v, E.vv, M["p3halot"])
            if ts4 >= 1:
                E.w(v, E.tt, M[("p3r", ts4 * 4 - 1)] if mk == "p3mixr" else M[("p3k", (ts4 - 1) * 8 + 7)])
            for ccn in range(8):
                cbase = ccn * WIN
                if t0 == 0:
                    v.tensor_scalar(R.mixs[:, 1:512], R.xn3T[:, cbase: cbase + 511],
                                    omu[:, ccn:ccn + 1], None, ALU.mult)
                    v.tensor_scalar(R.mixs[:, 0:1], R.haloT[:, ccn:ccn + 1],
                                    omu[:, ccn:ccn + 1], None, ALU.mult)
                else:
                    v.tensor_scalar(R.mixs[:, 0:512], R.xn3T[:, cbase + t0 - 1: cbase + t0 + 511],
                                    omu[:, ccn:ccn + 1], None, ALU.mult)
                inst = v.scalar_tensor_tensor(dst[:, ccn * 512:(ccn + 1) * 512],
                                              R.xn3T[:, cbase + t0: cbase + t0 + 512],
                                              mu[:, ccn:ccn + 1], R.mixs[:, 0:512],
                                              ALU.mult, ALU.add)
            M[(mk, ts4)] = E.vv.n
        if P3SUB < 4:
            continue
        for dt_ in range(8):
            u = ts4 * 8 + dt_
            p = u % 2
            E.w(v, E.tt, M[("p3k", u)])
            if u >= 1:
                E.w(v, E.ss, M[("p3kact", u - 1)])
            v.tensor_scalar_max(R.kpre[:, :], R.Kp[p][:, :], 0.0)
            M[("p3relu", u)] = E.vv.n
        if P3SUB < 5:
            continue
        for q in range(4):
            it = ts4 * 4 + q
            E.w(v, E.tt, M[("p3vo", it)])
            if it >= 1:
                E.w(v, E.ss, M[("p3sq4", it - 1)])
            v.tensor_tensor(R.ffo[:, :], R.Pp2[it % 2][:, :], R.rsig[:, :], ALU.mult)
            M[("p3ff", it)] = E.vv.n
            v.reduce_sum(R.l3[:, 16:17], R.ffo[:, :], axis=AXX)
            E.w(v, E.ss, M[("p3sq4", it)])
            v.tensor_scalar(R.l3[:, 18:19], R.l3[:, 16:17], 1.0 / D, None, ALU.mult)
            v.tensor_scalar(R.l3[:, 19:20], R.l3[:, 17:18], 1.0 / D, None, ALU.mult)
            v.tensor_tensor(R.l3[:, 20:21], R.l3[:, 18:19], R.l3[:, 18:19], ALU.mult)
            v.tensor_tensor(R.l3[:, 21:22], R.l3[:, 19:20], R.l3[:, 20:21], ALU.subtract)
            M[("p3var4", it)] = E.vv.n
            E.w(v, E.ss, M[("p3rs4", it)])
            v.tensor_scalar(R.ln4o[:, :], R.ffo[:, :], R.l3[:, 18:19], R.l3[:, 22:23],
                            ALU.subtract, ALU.mult)
            xwm = M[("p3xw2", it)]
            if xwm:
                E.w(v, xwm[0], xwm[1])
            if it >= 2:
                ym = M[("p3yst", it - 2)]
                if ym:
                    E.w(v, ym[0], ym[1])
            v.tensor_tensor(R.ysb[it % 2][:, :], R.xw[it % 2][:, :],
                            R.ln2o[:, it * D:(it + 1) * D], ALU.add)
            v.tensor_tensor(R.ysb[it % 2][:, :], R.ysb[it % 2][:, :], R.ln4o[:, :], ALU.add)
            M[("p3y", it)] = E.vv.n


def prog_scalar_b(E, R):
    s, M = E.s, E.M
    for it in range(NT3):
        if not (P3OPS & 4):
            break
        E.w(s, E.tt, M[("p3op", it)])
        E.w(s, E.vv, M[("p3ps", it)])
        if it >= 1:
            E.w(s, E.vv, M[("p3xn3", it - 1)])
        s.copy(R.Psb[:, :], R.Pp2[it % 2][:, :])
        s.activation(R.sq3[:, :], R.Pp2[it % 2][:, :], AF.Square, bias=R.zb3[:, :],
                              accum_out=R.l3[:, 1:2])
        M[("p3pev", it)] = E.ss.n
        if not (P3OPS & 8):
            continue
        E.w(s, E.vv, M[("p3var2", it)])
        s.activation(R.l3[:, 6:7], R.l3[:, 5:6], AF.Sqrt, bias=R.epsb3[:, :])
        act_raw(E, s, R.l3[:, 6:7], R.l3[:, 6:7], AF.Reciprocal, R.zb3[:, :])
        M[("p3rs2", it)] = E.ss.n
        E.w(s, E.vv, M[("p3xr1", it)])
        s.activation(R.sq3[:, :], R.xr1[:, :], AF.Square, bias=R.zb3[:, :],
                              accum_out=R.l3[:, 9:10])
        M[("p3sq3", it)] = E.ss.n
        E.w(s, E.vv, M[("p3var3", it)])
        s.activation(R.l3[:, 14:15], R.l3[:, 13:14], AF.Sqrt, bias=R.epsb3[:, :])
        act_raw(E, s, R.l3[:, 14:15], R.l3[:, 14:15], AF.Reciprocal, R.zb3[:, :])
        M[("p3rs3", it)] = E.ss.n

    if P3SUB < 2:
        return
    for fh in range(2):
        E.w(s, E.tt, M[("p3hp", fh)])
        s.copy(R.halo[0:1, fh * 512:(fh + 1) * 512], R.HP[0:1, :])
        M[("p3hev", fh)] = E.ss.n
    if P3SUB < 4:
        return
    for ts4 in range(4):
        for dt_ in range(8):
            u = ts4 * 8 + dt_
            E.w(s, E.vv, M[("p3relu", u)])
            if dt_ == 0 and ts4 >= 1:
                E.w(s, E.tt, M[("p3vo", ts4 * 4 - 1)])
            s.activation(R.kactS[:, dt_ * 512:(dt_ + 1) * 512], R.kpre[:, :],
                                  AF.Square, bias=R.zb3[:, :])
            M[("p3kact", u)] = E.ss.n
        if P3SUB < 5:
            continue
        for q in range(4):
            it = ts4 * 4 + q
            E.w(s, E.tt, M[("p3r", it)])
            if it >= 1:
                E.w(s, E.vv, M[("p3ff", it - 1)])
            s.activation(R.rsig[:, :], R.Pp2[it % 2][:, :], AF.Sigmoid, bias=R.zb3[:, :])
            M[("p3sig", it)] = E.ss.n
            E.w(s, E.vv, M[("p3ff", it)])
            s.activation(R.sq3[:, :], R.ffo[:, :], AF.Square, bias=R.zb3[:, :],
                                  accum_out=R.l3[:, 17:18])
            M[("p3sq4", it)] = E.ss.n
            E.w(s, E.vv, M[("p3var4", it)])
            s.activation(R.l3[:, 22:23], R.l3[:, 21:22], AF.Sqrt, bias=R.epsb3[:, :])
            act_raw(E, s, R.l3[:, 22:23], R.l3[:, 22:23], AF.Reciprocal, R.zb3[:, :])
            M[("p3rs4", it)] = E.ss.n



class _R:
    pass


def build_nc():
    nc = bass.Bass("TRN2", target_bir_lowering=False)
    R = _R()
    R.dxw = nc.dram_tensor("x_win", [WIN, D], F32, kind="ExternalInput")
    R.dwqk = nc.dram_tensor("wqk", [D, 2048], F32, kind="ExternalInput")
    R.dbqk = nc.dram_tensor("bqk", [1, 2048], F32, kind="ExternalInput")
    R.dwv = nc.dram_tensor("wv", [D, D], F32, kind="ExternalInput")
    R.dbv = nc.dram_tensor("bv", [1, D], F32, kind="ExternalInput")
    R.dwout = nc.dram_tensor("wout", [D, D], F32, kind="ExternalInput")
    R.dbout = nc.dram_tensor("bout", [1, D], F32, kind="ExternalInput")
    R.dwr = nc.dram_tensor("wr_t", [D, D], F32, kind="ExternalInput")
    R.dwk = nc.dram_tensor("wk_t", [D, D], F32, kind="ExternalInput")
    R.dwv2 = nc.dram_tensor("wv2_t", [D, D], F32, kind="ExternalInput")
    R.dmur = nc.dram_tensor("mu_r", [128, 8], F32, kind="ExternalInput")
    R.dmuk = nc.dram_tensor("mu_k", [128, 8], F32, kind="ExternalInput")
    R.dtblc = nc.dram_tensor("tblc", [128, WIN], F32, kind="ExternalInput")
    R.dtbls = nc.dram_tensor("tbls", [128, WIN], F32, kind="ExternalInput")
    R.dperm = nc.dram_tensor("permm", [128, 128], F32, kind="ExternalInput")
    R.didn = nc.dram_tensor("idn", [128, 128], F32, kind="ExternalInput")
    R.dhm = nc.dram_tensor("hmask", [8, 1], F32, kind="ExternalInput")
    R.dy = nc.dram_tensor("y", [WIN, D], F32, kind="ExternalOutput")
    R.A2I2 = nc.dram_tensor("a2i2", [1024, A2C], BF16)
    R.A2O2 = nc.dram_tensor("a2o2", [1024, A2C], BF16)
    R.A2I = nc.dram_tensor("a2i", [1024, WIN], BF16)
    R.A2O = nc.dram_tensor("a2o", [1024, WIN], BF16)
    R.HGI = nc.dram_tensor("hgi", [1, D], BF16)
    R.HGO = nc.dram_tensor("hgo", [8, D], BF16)

    marks = Marks()
    E = Env(nc, marks)

    with ExitStack() as top:
        E.sems(nc, top)
        with ExitStack() as pa:
            e = pa.enter_context
            R.KK = e(nc.sbuf_tensor("KK", [128, 2 * T], BF16))
            R.Vt = e(nc.sbuf_tensor("Vt", [128, T], BF16))
            R.wqk = e(nc.sbuf_tensor("wqk_s", [128, 8 * 2048], BF16))
            R.wv = e(nc.sbuf_tensor("wv_s", [128, 8 * 1024], BF16))
            R.bqk = e(nc.sbuf_tensor("bqk_s", [1, 2048], BF16))
            R.bv = e(nc.sbuf_tensor("bv_s", [1, D], BF16))
            R.Vst = e(nc.sbuf_tensor("Vst", [128, 1024], BF16))
            R.ones = e(nc.sbuf_tensor("ones_s", [1, 512], BF16))
            R.permb = e(nc.sbuf_tensor("permb", [128, 128], BF16))
            R.idnb = e(nc.sbuf_tensor("idnb", [128, 128], BF16))
            R.zb = e(nc.sbuf_tensor("zb", [128, 1], F32))
            R.epsb = e(nc.sbuf_tensor("epsb", [128, 1], F32))
            R.xts = [e(nc.sbuf_tensor(f"xts{i}", [128, 4 * D], BF16)) for i in range(1)]
            R.sqsc = e(nc.sbuf_tensor("sqsc", [128, D], F32))
            R.xnb = [e(nc.sbuf_tensor(f"xnb{i}", [128, 4 * D], BF16)) for i in range(1)]
            R.xnT = e(nc.sbuf_tensor("xnT", [128, 8 * OT], BF16))
            R.sums = e(nc.sbuf_tensor("sums", [128, 4 * NOT1], F32))
            R.sumsq = e(nc.sbuf_tensor("sumsq", [128, 4 * NOT1], F32))
            R.lnt = e(nc.sbuf_tensor("lnt", [128, 24], F32))
            R.es = e(nc.sbuf_tensor("es", [128, 512], BF16))
            R.expm = e(nc.sbuf_tensor("expm", [128, 512], BF16))
            R.stg = [e(nc.sbuf_tensor(f"stg{i}", [128, 1024], BF16)) for i in range(2)]
            R.asb = e(nc.sbuf_tensor("asb", [128, 512], BF16))
            R.bsb = e(nc.sbuf_tensor("bsb", [128, 512], BF16))
            R.tcs = [e(nc.sbuf_tensor("tcs0", [128, 512], BF16))]
            R.tss = [e(nc.sbuf_tensor("tss0", [128, 512], BF16))]
            R.Qt = [e(nc.sbuf_tensor(f"Qt{i}", [128, 1024], BF16)) for i in range(2)]
            R.rp = [e(nc.sbuf_tensor(f"rp{i}", [128, 512], F32)) for i in range(3)]
            R.WT = [e(nc.sbuf_tensor(f"WT{i}", [128, 512], BF16)) for i in range(4)]
            R.Ost = [e(nc.sbuf_tensor(f"Ost{i}", [128, 512], BF16)) for i in range(2)]
            R.S1p = [e(nc.psum_tensor(f"S1p{i}", [128, 512], F32)) for i in range(3)]
            R.S2p = [e(nc.psum_tensor(f"S2p{i}", [128, 512], F32)) for i in range(3)]
            R.TPq = [e(nc.psum_tensor(f"TPq{i}", [128, 1024], BF16)) for i in range(2)]

            E.em = False
            E.reset()
            E.use_engines(t=_DummyEng(), v=_DummyEng(), s=_DummyEng(), g=_DummyEng(), sy=_DummyEng())
            for fn in (prog_sync_a, prog_gpsimd_a, prog_tensor_a, prog_vector_a, prog_scalar_a):
                fn(E, R)
            snapA = [c.n for c in E.ctrs]

            E._snapA = snapA

            Marks.strict = True
            E.em = True
            E.reset()
            with nc.Block() as blk:
                @blk.sync
                def _(sy):
                    E.use_engines(sy=sy)
                    prog_sync_a(E, R)

                @blk.gpsimd
                def _(g):
                    E.use_engines(g=g)
                    prog_gpsimd_a(E, R)

                @blk.tensor
                def _(t):
                    E.use_engines(t=t)
                    prog_tensor_a(E, R)

                @blk.vector
                def _(v):
                    E.use_engines(v=v)
                    prog_vector_a(E, R)

                @blk.scalar
                def _(s):
                    E.use_engines(s=s)
                    prog_scalar_a(E, R)
            assert [c.n for c in E.ctrs] == snapA, "pass A count mismatch"

        if STAGE < 3:
            return nc
        with ExitStack() as pb:
            e = pb.enter_context
            R.OTt = [e(nc.sbuf_tensor(f"OTt{i}", [128, 8 * 128], BF16)) for i in range(2)]
            R.ln2o = e(nc.sbuf_tensor("ln2o", [128, NT3 * D], BF16))
            R.xn3T = e(nc.sbuf_tensor("xn3T", [128, 8 * WIN], BF16))
            R.wouts = e(nc.sbuf_tensor("wouts", [128, 8 * D], BF16))
            R.bouts = e(nc.sbuf_tensor("bouts", [1, D], BF16))
            R.wrs = e(nc.sbuf_tensor("wrs", [128, 8 * D], BF16))
            R.wks = e(nc.sbuf_tensor("wks", [128, 8 * D], BF16))
            R.wv2s = e(nc.sbuf_tensor("wv2s", [128, 8 * D], BF16))
            R.murs = e(nc.sbuf_tensor("murs", [128, 8], F32))
            R.muks = e(nc.sbuf_tensor("muks", [128, 8], F32))
            R.omurs = e(nc.sbuf_tensor("omurs", [128, 8], F32))
            R.omuks = e(nc.sbuf_tensor("omuks", [128, 8], F32))
            R.zb3 = e(nc.sbuf_tensor("zb3", [128, 1], F32))
            R.epsb3 = e(nc.sbuf_tensor("epsb3", [128, 1], F32))
            R.ones3 = e(nc.sbuf_tensor("ones3", [1, 512], BF16))
            R.idn3 = e(nc.sbuf_tensor("idn3", [128, 128], BF16))
            R.hmm = e(nc.sbuf_tensor("hmm", [8, 1], BF16))
            R.hgos = e(nc.sbuf_tensor("hgos", [8, D], BF16))
            R.halo = e(nc.sbuf_tensor("halo", [1, D], BF16))
            R.haloT = e(nc.sbuf_tensor("haloT", [128, 8], BF16))
            R.xw = [e(nc.sbuf_tensor(f"xw{i}", [128, D], F32)) for i in range(2)]
            R.Psb = e(nc.sbuf_tensor("Psb", [128, D], F32))
            R.xr1 = e(nc.sbuf_tensor("xr1", [128, D], F32))
            R.xn3b = e(nc.sbuf_tensor("xn3b", [128, D], BF16))
            R.sq3 = e(nc.sbuf_tensor("sq3", [128, D], F32))
            R.ln4o = e(nc.sbuf_tensor("ln4o", [128, D], F32))
            R.l3 = e(nc.sbuf_tensor("l3", [128, 24], F32))
            R.mixs = e(nc.sbuf_tensor("mixs", [128, 512], BF16))
            R.xmrT = e(nc.sbuf_tensor("xmrT", [128, 8 * 512], BF16))
            R.xmkT = e(nc.sbuf_tensor("xmkT", [128, 8 * 512], BF16))
            R.kactS = e(nc.sbuf_tensor("kactS", [128, 8 * 512], BF16))
            R.kpre = e(nc.sbuf_tensor("kpre", [128, 512], BF16))
            R.rsig = e(nc.sbuf_tensor("rsig", [128, D], BF16))
            R.ffo = e(nc.sbuf_tensor("ffo", [128, D], F32))
            R.ysb = [e(nc.sbuf_tensor(f"ysb{i}", [128, D], F32)) for i in range(2)]
            R.Pp2 = [e(nc.psum_tensor(f"Pp{i}", [128, 1024], F32)) for i in range(2)]
            R.Kp = [e(nc.psum_tensor(f"Kp{i}", [128, 512], F32)) for i in range(2)]
            R.TP3 = [e(nc.psum_tensor(f"TP3{i}", [128, 512], BF16)) for i in range(2)]
            R.HP = R.Kp[0][0:1, 0:512]
            R.HS = R.Kp[1][:, 0:8].bitcast(BF16)

            Marks.strict = False
            E.em = False
            for c, n0 in zip(E.ctrs, E._snapA):
                c.n = n0
            E.use_engines(t=_DummyEng(), v=_DummyEng(), s=_DummyEng(), g=_DummyEng(), sy=_DummyEng())
            for fn in (prog_gpsimd_b, prog_sync_b, prog_tensor_b, prog_vector_b, prog_scalar_b):
                fn(E, R)
            snapB = [c.n for c in E.ctrs]

            Marks.strict = True
            E.em = True
            for c, n0 in zip(E.ctrs, E._snapA):
                c.n = n0
            with nc.Block() as blk3:
                @blk3.gpsimd
                def _(g):
                    E.use_engines(g=g)
                    prog_gpsimd_b(E, R)

                @blk3.sync
                def _(sy):
                    E.use_engines(sy=sy)
                    prog_sync_b(E, R)

                @blk3.tensor
                def _(t):
                    E.use_engines(t=t)
                    prog_tensor_b(E, R)

                @blk3.vector
                def _(v):
                    E.use_engines(v=v)
                    prog_vector_b(E, R)

                @blk3.scalar
                def _(s):
                    E.use_engines(s=s)
                    prog_scalar_b(E, R)
            assert [c.n for c in E.ctrs] == snapB, "pass B count mismatch"

    return nc


def _host_prep(inputs):
    x = np.ascontiguousarray(np.asarray(inputs["L0_attn_x"], dtype=np.float32).reshape(T, D))
    wqkv = np.asarray(inputs["L0_attn_w_qkv"], dtype=np.float32)
    bqkv = np.asarray(inputs["L0_attn_b_qkv"], dtype=np.float32)
    wout = np.ascontiguousarray(np.asarray(inputs["L0_attn_w_out"], dtype=np.float32))
    bout = np.asarray(inputs["L0_attn_b_out"], dtype=np.float32).reshape(1, D)
    Wr = np.asarray(inputs["L0_ffn_W_r"], dtype=np.float32)
    Wk = np.asarray(inputs["L0_ffn_W_k"], dtype=np.float32)
    Wv2 = np.asarray(inputs["L0_ffn_W_v"], dtype=np.float32)
    mur = np.ascontiguousarray(np.asarray(inputs["L0_ffn_mu_r"], dtype=np.float32).reshape(8, 128).T)
    muk = np.ascontiguousarray(np.asarray(inputs["L0_ffn_mu_k"], dtype=np.float32).reshape(8, 128).T)

    perm = np.concatenate([np.arange(0, 64, 2), np.arange(1, 64, 2)])
    p32 = np.arange(32, dtype=np.float64)
    theta = 1.0 / (10000.0 ** (p32 / 32.0))
    pos = np.arange(N, dtype=np.float64)
    ang = pos[:, None] * theta[None, :]
    cosv = np.cos(ang).T
    sinv = np.sin(ang).T
    tblc = np.empty((128, N), dtype=np.float32)
    tbls = np.empty((128, N), dtype=np.float32)
    for g in range(4):
        r0 = g * 32
        tblc[r0:r0 + 32] = cosv
        if g % 2 == 0:
            tbls[r0:r0 + 32] = -sinv
        else:
            tbls[r0:r0 + 32] = sinv
    permm = np.zeros((128, 128), dtype=np.float32)
    for i in range(128):
        j = i + 32 if (i // 32) % 2 == 0 else i - 32
        permm[j, i] = 1.0
    idn = np.eye(128, dtype=np.float32)

    wr_t = np.ascontiguousarray(Wr.T)
    wk_t = np.ascontiguousarray(Wk.T)
    wv2_t = np.ascontiguousarray(Wv2.T)

    qcols = []
    for h in range(16):
        qcols.append(np.concatenate([
            wqkv[:, h * 64 + perm],
            wqkv[:, D + h * 64 + perm],
        ], axis=1))
    wqk = np.ascontiguousarray(np.concatenate(qcols, axis=1))
    bqk = np.ascontiguousarray(np.concatenate([
        np.concatenate([bqkv[h * 64 + perm], bqkv[D + h * 64 + perm]])
        for h in range(16)]).reshape(1, 2048).astype(np.float32))
    wv = np.ascontiguousarray(wqkv[:, 2 * D:3 * D])
    bv = np.ascontiguousarray(bqkv[2 * D:3 * D].reshape(1, D).astype(np.float32))
    in_maps = []
    for c in range(NCORE):
        pos0 = (c * WIN) % N
        hm = np.zeros((8, 1), dtype=np.float32)
        if c not in (0, 4):
            hm[c - 1, 0] = 1.0
        in_maps.append({
            "x_win": np.ascontiguousarray(x[c * WIN:(c + 1) * WIN]),
            "wqk": wqk, "bqk": bqk,
            "wv": wv, "bv": bv,
            "wout": wout, "bout": np.ascontiguousarray(bout),
            "wr_t": wr_t, "wk_t": wk_t, "wv2_t": wv2_t,
            "mu_r": mur, "mu_k": muk,
            "tblc": np.ascontiguousarray(tblc[:, pos0:pos0 + WIN]),
            "tbls": np.ascontiguousarray(tbls[:, pos0:pos0 + WIN]),
            "permm": permm, "idn": idn,
            "hmask": hm,
        })
    return in_maps


def kernel(**inputs):
    in_maps = _host_prep(inputs)
    nc = build_nc()
    res = run_bass_kernel_spmd(nc, in_maps, core_ids=list(range(NCORE)))
    y = np.concatenate([np.asarray(res.results[c]["y"], dtype=np.float32)
                        for c in range(NCORE)], axis=0)
    return y.reshape(B, N, D)


# version 12
# speedup vs baseline: 1.9875x; 1.0178x over previous
import sys
sys.path.insert(0, "/opt/trn_rl_repo")
import numpy as np
import concourse.bass as bass
import concourse.mybir as mybir
from concourse.bass_utils import run_bass_kernel_spmd
from concourse.aot_env import allow_ici_all_to_all
from contextlib import ExitStack

F32 = mybir.dt.float32
BF16 = mybir.dt.bfloat16
AF = mybir.ActivationFunctionType
ALU = mybir.AluOpType
AXX = mybir.AxisListType.X

B = 2
N = 8192
T = 16384
D = 1024
NCORE = 8
WIN = 2048
OT = 512
NOT = T // OT
NI = N // OT
NJ = N // 128
NT3 = WIN // 128
NOT1 = WIN // OT
NFT = 16
A2C = 5 * WIN
EPS = 1e-5
STAGE = 3
P3SUB = 5
P3OPS = 63
LOOSE = False
VB_MODE = 0
SYB_MODE = 0


class Marks(dict):
    strict = False

    def __missing__(self, key):
        if Marks.strict and not LOOSE:
            raise KeyError(key)
        return 0


class _DummyInst:
    def then_inc(self, *a, **k):
        return self


class _DummyEng:
    def __getattr__(self, name):
        return lambda *a, **k: _DummyInst()


_PASSTHRU = {"wait_ge", "dma_start", "collective_compute", "nop", "drain",
             "indirect_dma_start", "sem_inc"}


class EngProxy:

    def __init__(self, real, ctr, em):
        self.raw = real
        self._ctr = ctr
        self._em = em

    def __getattr__(self, name):
        if name in _PASSTHRU:
            return getattr(self.raw, name)
        real_fn = getattr(self.raw, name)
        def call(*a, **k):
            c = self._ctr
            if self._em and c.n - (c.lag - 1) * c.step > 0:
                self.raw.wait_ge(c.sem, c.n - (c.lag - 1) * c.step)
            inst = real_fn(*a, **k)
            c.n += c.step
            if self._em and c.sem is not None:
                inst.then_inc(c.sem, c.step)
            return inst
        return call


class ColView:

    def __init__(self, t, c0, w):
        self.t, self.c0, self.w = t, c0, w

    def __getitem__(self, idx):
        p, c = idx
        a = (c.start or 0) + self.c0
        b = (c.stop if c.stop is not None else self.w) + self.c0
        return self.t[p, a:b]


class Ctr:
    def __init__(self, name, step):
        self.name = name
        self.step = step
        self.n = 0
        self.sem = None
        self.lag = 1

    def inc(self, inst):
        self.n += self.step
        inst.then_inc(self.sem, self.step) if self.sem is not None else None
        return inst


class Env:

    def __init__(self, nc, marks):
        self.nc = nc
        self.M = marks
        self.em = False
        self.t = self.sy = _DummyEng()
        self.v = self.s = self.g = None
        self.tt = Ctr("tt", 1)
        self.vv = Ctr("vv", 1)
        self.ss = Ctr("ss", 1)
        self.gc = Ctr("gc", 1)
        self.cc = Ctr("cc", 1)
        self.syn = Ctr("syn", 16)
        self.gd = Ctr("gd", 16)
        self.st = Ctr("st", 16)
        slot_names = ["gw", "gx0", "gx1", "gt0", "gt1", "gw3", "ghg",
                      "sg0", "sg1", "skk", "sqt0", "sqt1", "sos0", "sos1",
                      "sot0", "sot1", "sxw0", "sxw1", "shgi", "sy0", "sy1", "sv0", "sv1"]
        self.slots = {n: Ctr(n, 16) for n in slot_names}
        self.ctrs = [self.tt, self.vv, self.ss, self.gc, self.cc, self.syn, self.gd, self.st] + \
            [self.slots[n] for n in slot_names]

    def use_engines(self, t=None, v=None, s=None, g=None, sy=None):
        if t is not None:
            self.t = t
        if sy is not None:
            self.sy = sy
        if v is not None:
            self.v = EngProxy(v, self.vv, self.em)
        if s is not None:
            self.s = EngProxy(s, self.ss, self.em)
        if g is not None:
            self.g = EngProxy(g, self.gc, self.em)

    def reset(self):
        for c in self.ctrs:
            c.n = 0

    def sems(self, nc, stack):
        for c in self.ctrs:
            c.sem = stack.enter_context(nc.semaphore("sem_" + c.name))

    def w(self, eng, ctr, val):
        if self.em and val > 0:
            eng.wait_ge(ctr.sem, val)


def act_raw(E, eng, out_ap, in_ap, func, bias_ap, scale=1.0):
    c = eng._ctr
    if not E.em:
        c.n += c.step
        return _DummyInst()
    nc = E.nc
    with nc.discard():
        proto = eng.raw.activation(out_ap, in_ap, AF.Exp, bias=bias_ap, scale=scale).ins
    new = mybir.InstActivation(
        name=nc.get_next_instruction_name(), engine=proto.engine,
        ins=list(proto.ins), outs=list(proto.outs), func=func, debug=proto.debug)
    if c.n - (c.lag - 1) * c.step > 0:
        eng.raw.wait_ge(c.sem, c.n - (c.lag - 1) * c.step)
    inst = eng.raw.add_instruction(new)
    c.n += c.step
    inst.then_inc(c.sem, c.step)
    return inst



def prog_sync_a(E, R):
    sy, M = E.sy, E.M
    for ot in range(NOT1):
        lt = ot * OT
        for h in range(NFT):
            ft = ot * NFT + h
            sp = ft % 2
            d, hl = h // 2, h % 2
            sg = E.slots["sg%d" % sp]
            E.w(sy, E.vv, M[("p1stg", ot, h)])
            sg.inc(sy.dma_start(
                out=R.A2I2.ap()[128 * d:128 * d + 128,
                                2 * WIN + hl * WIN + lt: 2 * WIN + hl * WIN + lt + 512]
                    .rearrange("(g p) t -> p g t", g=2),
                in_=R.stg[sp][0:64, :].rearrange("p (g t) -> p g t", g=2)))
            sg.inc(sy.dma_start(
                out=R.A2I2.ap()[128 * d:128 * d + 128,
                                hl * WIN + lt: hl * WIN + lt + 512]
                    .rearrange("(g p) t -> p g t", g=2),
                in_=R.stg[sp][64:128, :].rearrange("p (g t) -> p g t", g=2)))
            M[("p1syn_stg", ot, h)] = (sg, sg.n)
        sv = E.slots["sv%d" % (ot % 2)]
        for sb in range(4):
            jt = ot * 4 + sb
            E.w(sy, E.ss, M[("p1vst", ot, sb)])
            sv.inc(sy.dma_start(
                out=R.A2I2.ap().rearrange("(d p) c -> p d c", d=8)[:, :,
                    4 * WIN + jt * 128: 4 * WIN + jt * 128 + 128],
                in_=R.Vst[:, :].rearrange("p (d c) -> p d c", d=8)))
            M[("p1vstd", ot, sb)] = (sv, sv.n)
    M["p1sg0_end"] = E.slots["sg0"].n
    M["p1sg1_end"] = E.slots["sg1"].n
    M["p1sv0_end"] = E.slots["sv0"].n
    M["p1sv1_end"] = E.slots["sv1"].n
    if STAGE < 2:
        M["p2syn_end"] = E.syn.n
        return
    def store_O(bi):
        b, I = bi // NI, bi % NI
        gtok = b * N + I * OT
        w_, c0 = gtok // WIN, gtok % WIN
        so = E.slots["sos%d" % (bi % 2)]
        so.inc(sy.dma_start(out=R.A2I.ap()[128 * w_:128 * w_ + 128, c0:c0 + 512],
                            in_=R.Ost[bi % 2][:, :]))
        M[("p2ost", bi)] = (so, so.n)

    E.w(sy, E.cc, M["a2a2"])
    for bi in range(B * NI):
        b, I = bi // NI, bi % NI
        buf = bi % 2
        if bi >= 2:
            E.w(sy, E.tt, M[("p2tte", bi - 2)])
        gt = b * N + I * OT
        s_src = gt // WIN
        off = gt % WIN
        sq = E.slots["sqt%d" % buf]
        sq.inc(sy.dma_start(
            out=R.Qt[buf][:, :].rearrange("p (h t) -> p h t", h=2),
            in_=R.A2O2.ap()[128 * s_src:128 * s_src + 128, 2 * WIN:4 * WIN]
                .rearrange("p (h n) -> p h n", h=2)[:, :, off:off + 512]))
        M[("p2qt", bi)] = (sq, sq.n)
        if bi >= 1:
            E.w(sy, E.vv, M[("p2evac", bi - 1)])
            store_O(bi - 1)
    E.w(sy, E.vv, M[("p2evac", B * NI - 1)])
    store_O(B * NI - 1)
    M["p2sos0_end"] = E.slots["sos0"].n
    M["p2sos1_end"] = E.slots["sos1"].n


def prog_gpsimd_a(E, R):
    g, M = E.g, E.M
    g.memset(R.zb[:, :], 0.0)
    g.memset(R.epsb[:, :], EPS)
    g.memset(R.ones[:, :], 1.0)
    M["p1memset"] = E.gc.n
    gw = E.slots["gw"]
    gw.inc(g.dma_start(out=R.wqk[:, :].rearrange("p (c f) -> p c f", c=8),
                       in_=R.dwqk.ap().rearrange("(c p) f -> p c f", c=8)))
    gw.inc(g.dma_start(out=R.wv[:, :].rearrange("p (c f) -> p c f", c=8),
                       in_=R.dwv.ap().rearrange("(c p) f -> p c f", c=8)))
    gw.inc(g.dma_start(out=R.bqk[:, :], in_=R.dbqk.ap()))
    gw.inc(g.dma_start(out=R.bv[:, :], in_=R.dbv.ap()))
    gw.inc(g.dma_start(out=R.permb[:, :], in_=R.dperm.ap()))
    gw.inc(g.dma_start(out=R.idnb[:, :], in_=R.didn.ap()))
    M["p1w"] = gw.n
    for ot in range(NOT1):
        bsel = ot % 2
        lt = ot * OT
        if ot >= 2:
            E.w(g, E.ss, M[("p1norm", ot - 2)])
            E.w(g, E.vv, M[("p1stat", ot - 2)])
            E.w(g, E.vv, M[("p1stg", ot - 2, NFT - 1)])
        gx = E.slots["gx%d" % bsel]
        gt_s = E.slots["gt%d" % bsel]
        gx.inc(g.dma_start(
            out=R.xts[bsel][:, :].rearrange("p (s c) -> p s c", s=4),
            in_=R.dxw.ap()[lt:lt + 512, :].rearrange("(s p) c -> p s c", s=4)))
        gt_s.inc(g.dma_start(out=R.tcs[bsel][:, :], in_=R.dtblc.ap()[:, lt:lt + 512]))
        gt_s.inc(g.dma_start(out=R.tss[bsel][:, :], in_=R.dtbls.ap()[:, lt:lt + 512]))
        M[("p1x", ot)] = (gx, gx.n)
        M[("p1tb", ot)] = (gt_s, gt_s.n)
    for sl in ("sg0", "sg1", "sv0", "sv1"):
        E.w(g, E.slots[sl], M["p1" + sl + "_end"])
    if E.em:
        with allow_ici_all_to_all():
            E.cc.inc(g.collective_compute(
                "AllToAll", ALU.bypass, replica_groups=[list(range(8))],
                ins=[R.A2I2.ap().opt()], outs=[R.A2O2.ap().opt()]))
    else:
        E.cc.inc(_DummyInst())
    M["a2a2"] = E.cc.n
    E.w(g, E.cc, M["a2a2"])
    gkv = E.slots["skk"]
    for s_src in range(8):
        for hl in range(2):
            gkv.inc(g.dma_start(
                out=R.KK[:, hl * T + s_src * WIN: hl * T + s_src * WIN + WIN],
                in_=R.A2O2.ap()[128 * s_src:128 * s_src + 128, hl * WIN:(hl + 1) * WIN]))
        gkv.inc(g.dma_start(
            out=R.Vt[:, s_src * WIN:(s_src + 1) * WIN],
            in_=R.A2O2.ap()[128 * s_src:128 * s_src + 128, 4 * WIN:5 * WIN]))
    M["p1kv"] = gkv.n
    if STAGE < 2:
        M["a2a"] = E.cc.n
        return
    E.w(g, E.slots["sos0"], M["p2sos0_end"])
    E.w(g, E.slots["sos1"], M["p2sos1_end"])
    if E.em:
        with allow_ici_all_to_all():
            E.cc.inc(g.collective_compute(
                "AllToAll", ALU.bypass, replica_groups=[list(range(8))],
                ins=[R.A2I.ap().opt()], outs=[R.A2O.ap().opt()]))
    else:
        E.cc.inc(_DummyInst())
    M["a2a"] = E.cc.n


def prog_tensor_a(E, R):
    t, M = E.t, E.M
    E.w(t, E.slots["gw"], M["p1w"])
    E.w(t, E.gc, M["p1memset"])
    QKb = [R.S1p[0][:, :], R.S1p[1][:, :]]
    SWp = R.S2p[1][:, :]
    Vpp = R.S2p[0][:, :]
    VB = [Vpp, SWp]
    for ot in range(NOT1):
        E.w(t, E.ss, M[("p1norm", ot)])
        for ccn in range(8):
            tp = R.TPq[ccn % 2]
            gcc = ot * 8 + ccn
            if gcc >= 2:
                ev = M[("p1tpev", gcc - 2)]
                if ev:
                    E.w(t, ev[0], ev[1])
            for sb in range(4):
                inst = t.transpose(tp[:, sb * 128:(sb + 1) * 128],
                                   R.xnb[ot % 2][:, sb * D + ccn * 128: sb * D + ccn * 128 + 128],
                                   R.idnb[:, :])
            E.tt.inc(inst)
            M[("p1tp", ot, ccn)] = E.tt.n
        E.w(t, E.vv, M[("p1trev_v", ot)])
        E.w(t, E.ss, M[("p1trev_s", ot)])

        def swap(h):
            ft = ot * NFT + h
            sp = ft % 2
            E.w(t, E.vv, M[("p1E", ot, h)])
            if ft >= 1:
                E.w(t, E.vv, M[("p1ropeA", (ft - 1) // NFT, (ft - 1) % NFT)])
            E.tt.inc(t.matmul(SWp, R.permb[:, :], R.stg[sp][:, 512:1024],
                              start=True, stop=True))
            M[("p1sw", ot, h)] = E.tt.n

        for h in range(NFT):
            ft = ot * NFT + h
            if ft >= 2:
                E.w(t, E.vv, M[("p1stg", (ft - 2) // NFT, (ft - 2) % NFT)])
            for ccn in range(8):
                t.matmul(QKb[h % 2], R.wqk[:, ccn * 2048 + h * 128: ccn * 2048 + h * 128 + 128],
                         R.xnT[:, ccn * OT: ccn * OT + 512], start=(ccn == 0), stop=False)
            E.tt.inc(t.matmul(QKb[h % 2], R.bqk[:, h * 128:(h + 1) * 128], R.ones[:, 0:512],
                              start=False, stop=True))
            M[("p1qk", ot, h)] = E.tt.n
            if h >= 1:
                swap(h - 1)
        swap(NFT - 1)
        E.w(t, E.vv, M[("p1ropeA", ot, NFT - 1)])
        for sb in range(4):
            for fh in range(2):
                ch = sb * 2 + fh
                if ot * 8 + ch >= 2:
                    ev = M[("p1vch", ot * 8 + ch - 2)]
                    if ev:
                        E.w(t, ev[0], ev[1])
                for ccn in range(8):
                    t.matmul(VB[ch % 2],
                             R.xnT[:, ccn * OT + sb * 128: ccn * OT + sb * 128 + 128],
                             R.wv[:, ccn * 1024 + fh * 512: ccn * 1024 + fh * 512 + 512],
                             start=(ccn == 0), stop=False)
                inst = t.matmul(VB[ch % 2], R.ones[:, 0:128], R.bv[:, fh * 512:(fh + 1) * 512],
                                start=False, stop=True)
                E.tt.inc(inst)
                M[("p1vmm", ot, ch)] = E.tt.n
    if STAGE < 2:
        return
    E.w(t, E.slots["skk"], M["p1kv"])
    E.w(t, E.vv, M["p1vv_end"])
    E.w(t, E.ss, M["p1ss_end"])
    ACCap = [R.TPq[0][:, :].bitcast(F32), R.TPq[1][:, :].bitcast(F32)]
    u = 0
    for bi in range(B * NI):
        b, I = bi // NI, bi % NI
        buf = bi % 2
        qtm = M[("p2qt", bi)]
        if qtm:
            E.w(t, qtm[0], qtm[1])
        for h in range(2):
            hb = h * T + b * N
            for J in range(NJ):
                p = u % 3
                if u >= 3:
                    E.w(t, E.vv, M[("p2w", u - 3)])
                t.matmul(R.S1p[p][:, :], R.KK[0:64, hb + J * 128: hb + J * 128 + 128],
                         R.Qt[buf][0:64, h * 512:(h + 1) * 512],
                         start=True, stop=True, tile_position=(0, 0))
                E.tt.inc(t.matmul(R.S2p[p][:, :], R.KK[64:128, hb + J * 128: hb + J * 128 + 128],
                                  R.Qt[buf][64:128, h * 512:(h + 1) * 512],
                                  start=True, stop=True, tile_position=(64, 0)))
                M[("p2s", u)] = E.tt.n
                if J >= 3:
                    _mm3(E, R, ACCap, b, h, J - 3, u - 3, bi)
                u += 1
            _mm3(E, R, ACCap, b, h, NJ - 3, u - 3, bi)
            _mm3(E, R, ACCap, b, h, NJ - 2, u - 2, bi)
            _mm3(E, R, ACCap, b, h, NJ - 1, u - 1, bi)
        M[("p2tte", bi)] = E.tt.n
    M["p2tt_end"] = E.tt.n


def _mm3(E, R, ACCap, b, h, J, u, bi):
    t, M = E.t, E.M
    if J == 0 and bi >= 1:
        E.w(t, E.vv, M[("p2evac", bi - 1)])
    E.w(t, E.vv, M[("p2w", u)])
    vcol = (b * 64 + J) * 128 + h * 64
    if h == 0:
        inst = t.matmul(ACCap[0][0:64, :], R.Vt[:, vcol:vcol + 64], R.WT[u % 4][:, :],
                        start=(J == 0), stop=(J == NJ - 1))
    else:
        inst = t.matmul(ACCap[1][64:128, :], R.Vt[:, vcol:vcol + 64], R.WT[u % 4][:, :],
                        start=(J == 0), stop=(J == NJ - 1), tile_position=(0, 64))
    if J == NJ - 1:
        E.tt.inc(inst)


def prog_vector_a(E, R):
    v, M = E.v, E.M
    QKb = [R.S1p[0][:, :], R.S1p[1][:, :]]
    SWp = R.S2p[1][:, :]
    Vpp = R.S2p[0][:, :]

    def v_stats(ot):
        xm = M[("p1x", ot)]
        if xm:
            E.w(v, xm[0], xm[1])
        for sb in range(4):
            v.reduce_sum(R.sums[:, ot * 4 + sb: ot * 4 + sb + 1],
                         R.xts[ot % 2][:, sb * D:(sb + 1) * D], axis=AXX)
        M[("p1stat", ot)] = E.vv.n

    def v_var(ot):
        lb = 24 * (ot % 2)
        E.w(v, E.ss, M[("p1sq", ot)])
        c4 = slice(ot * 4, ot * 4 + 4)
        v.tensor_scalar(R.lnt[:, lb + 0:lb + 4], R.sums[:, c4], 1.0 / D, None, ALU.mult)
        v.tensor_scalar(R.lnt[:, lb + 4:lb + 8], R.sumsq[:, c4], 1.0 / D, None, ALU.mult)
        v.tensor_tensor(R.lnt[:, lb + 8:lb + 12], R.lnt[:, lb + 0:lb + 4],
                        R.lnt[:, lb + 0:lb + 4], ALU.mult)
        v.tensor_tensor(R.lnt[:, lb + 12:lb + 16], R.lnt[:, lb + 4:lb + 8],
                        R.lnt[:, lb + 8:lb + 12], ALU.subtract)
        M[("p1var", ot)] = E.vv.n

    def v_negmr(ot):
        lb = 24 * (ot % 2)
        E.w(v, E.ss, M[("p1rstd", ot)])
        v.tensor_scalar(R.lnt[:, lb + 20:lb + 24], R.lnt[:, lb + 0:lb + 4], -1.0, None, ALU.mult)
        v.tensor_tensor(R.lnt[:, lb + 20:lb + 24], R.lnt[:, lb + 20:lb + 24],
                        R.lnt[:, lb + 16:lb + 20], ALU.mult)
        M[("p1negmr", ot)] = E.vv.n

    for ot in range(NOT1):
        if ot == 0:
            v_stats(0)
            v_var(0)
            v_negmr(0)
        for ccn in range(0, 8, 2):
            E.w(v, E.tt, M[("p1tp", ot, ccn)])
            if ot >= 1 and ccn == 0:
                E.w(v, E.tt, M[("p1vmm", ot - 1, 7)])
            v.tensor_copy(R.xnT[:, ccn * OT: ccn * OT + 512], R.TPq[ccn % 2][:, 0:512])
            M[("p1tpev", ot * 8 + ccn)] = (E.vv, E.vv.n)
        M[("p1trev_v", ot)] = E.vv.n
        for h in range(NFT):
            ft = ot * NFT + h
            sp = ft % 2
            E.w(v, E.tt, M[("p1qk", ot, h)])
            if ft >= 1:
                E.w(v, E.ss, M[("p1exp", (ft - 1) // NFT, (ft - 1) % NFT)])
            v.tensor_scalar_min(R.es[:, :], QKb[h % 2], 0.0)
            M[("p1min", ot, h)] = E.vv.n
            E.w(v, E.ss, M[("p1exp", ot, h)])
            if ft >= 2:
                sgm = M[("p1syn_stg", (ft - 2) // NFT, (ft - 2) % NFT)]
                if sgm:
                    E.w(v, sgm[0], sgm[1])
            v.scalar_tensor_tensor(R.stg[sp][:, 512:1024], QKb[h % 2], 0.0,
                                            R.expm[:, :], ALU.max, ALU.add)
            M[("p1E", ot, h)] = E.vv.n
            E.w(v, E.tt, M[("p1sw", ot, h)])
            tbm = M[("p1tb", ot)]
            if tbm:
                E.w(v, tbm[0], tbm[1])
            v.tensor_tensor(R.asb[:, :], SWp, R.tss[ot % 2][:, :], ALU.mult)
            M[("p1ropeA", ot, h)] = E.vv.n
            v.tensor_tensor(R.bsb[:, :], R.stg[sp][:, 512:1024], R.tcs[ot % 2][:, :], ALU.mult)
            v.tensor_tensor(R.stg[sp][:, 0:512], R.asb[:, :], R.bsb[:, :], ALU.add)
            M[("p1stg", ot, h)] = E.vv.n
            if ot + 1 < NOT1:
                if h == 1:
                    v_stats(ot + 1)
                elif h == 5:
                    v_var(ot + 1)
                elif h == 9:
                    v_negmr(ot + 1)
    M["p1vv_end"] = E.vv.n
    if STAGE < 2:
        return
    ACCap = [R.TPq[0][:, :].bitcast(F32), R.TPq[1][:, :].bitcast(F32)]
    u = 0
    E.vv.lag = 3
    for bi in range(B * NI):
        for h in range(2):
            for J in range(NJ):
                p = u % 3
                E.w(v, E.ss, M[("p2r", u)])
                v.tensor_tensor(R.WT[u % 4][:, :], R.S1p[p][:, :], R.rp[p][:, :], ALU.mult)
                M[("p2w", u)] = E.vv.n
                u += 1
        E.w(v, E.tt, M[("p2tte", bi)])
        if bi >= 2:
            osm = M[("p2ost", bi - 2)]
            if osm:
                E.w(v, osm[0], osm[1])
        v.tensor_copy(R.Ost[bi % 2][0:64, :], ACCap[0][0:64, 0:512])
        v.tensor_copy(R.Ost[bi % 2][64:128, :], ACCap[1][64:128, 0:512])
        M[("p2evac", bi)] = E.vv.n
    E.vv.lag = 1


def prog_scalar_a(E, R):
    s, M = E.s, E.M
    VB = [R.S2p[0][:, :], R.S2p[1][:, :]]

    def s_sq(ot):
        xm = M[("p1x", ot)]
        if xm:
            E.w(s, xm[0], xm[1])
        for sb in range(4):
            s.activation(R.sqsc[:, :], R.xts[ot % 2][:, sb * D:(sb + 1) * D],
                         AF.Square, bias=R.zb[:, :],
                         accum_out=R.sumsq[:, ot * 4 + sb: ot * 4 + sb + 1])
        M[("p1sq", ot)] = E.ss.n

    def s_rstd(ot):
        lb = 24 * (ot % 2)
        E.w(s, E.vv, M[("p1var", ot)])
        s.activation(R.lnt[:, lb + 16:lb + 20], R.lnt[:, lb + 12:lb + 16],
                     AF.Sqrt, bias=R.epsb[:, :])
        act_raw(E, s, R.lnt[:, lb + 16:lb + 20], R.lnt[:, lb + 16:lb + 20],
                AF.Reciprocal, R.zb[:, :])
        M[("p1rstd", ot)] = E.ss.n

    def s_norm(ot):
        lb = 24 * (ot % 2)
        E.w(s, E.vv, M[("p1negmr", ot)])
        if ot >= 2:
            E.w(s, E.tt, M[("p1tp", ot - 2, 7)])
        for sb in range(4):
            s.activation(R.xnb[ot % 2][:, sb * D:(sb + 1) * D],
                         R.xts[ot % 2][:, sb * D:(sb + 1) * D], AF.Identity,
                         bias=R.lnt[:, lb + 20 + sb:lb + 21 + sb],
                         scale=R.lnt[:, lb + 16 + sb:lb + 17 + sb])
        M[("p1norm", ot)] = E.ss.n

    for ot in range(NOT1):
        if ot == 0:
            E.w(s, E.gc, M["p1memset"])
            s_sq(0)
            s_rstd(0)
            s_norm(0)
        for ccn in range(1, 8, 2):
            E.w(s, E.tt, M[("p1tp", ot, ccn)])
            if ot >= 1 and ccn == 1:
                E.w(s, E.tt, M[("p1vmm", ot - 1, 7)])
            s.copy(R.xnT[:, ccn * OT: ccn * OT + 512], R.TPq[ccn % 2][:, 0:512])
            M[("p1tpev", ot * 8 + ccn)] = (E.ss, E.ss.n)
        M[("p1trev_s", ot)] = E.ss.n
        for h in range(NFT):
            ft = ot * NFT + h
            E.w(s, E.vv, M[("p1min", ot, h)])
            if ft >= 1:
                E.w(s, E.vv, M[("p1E", (ft - 1) // NFT, (ft - 1) % NFT)])
            s.activation(R.expm[:, :], R.es[:, :], AF.Exp, bias=R.zb[:, :])
            M[("p1exp", ot, h)] = E.ss.n
            if ot + 1 < NOT1:
                if h == 3:
                    s_sq(ot + 1)
                elif h == 7:
                    s_rstd(ot + 1)
                elif h == 11:
                    s_norm(ot + 1)
        for sb in range(4):
            for fh in range(2):
                ch = sb * 2 + fh
                E.w(s, E.tt, M[("p1vmm", ot, ch)])
                if sb >= 1 and fh == 0:
                    vdm = M[("p1vstd", ot, sb - 1)]
                    if vdm:
                        E.w(s, vdm[0], vdm[1])
                if ot >= 1 and sb == 0 and fh == 0:
                    vdm = M[("p1vstd", ot - 1, 3)]
                    if vdm:
                        E.w(s, vdm[0], vdm[1])
                s.copy(R.Vst[:, fh * 512:(fh + 1) * 512], VB[ch % 2])
                M[("p1vch", ot * 8 + ch)] = (E.ss, E.ss.n)
                if fh == 1:
                    M[("p1vst", ot, sb)] = E.ss.n
    M["p1ss_end"] = E.ss.n
    if STAGE < 2:
        return
    u = 0
    E.ss.lag = 3
    for bi in range(B * NI):
        for h in range(2):
            for J in range(NJ):
                p = u % 3
                E.w(s, E.tt, M[("p2s", u)])
                if u >= 3:
                    E.w(s, E.vv, M[("p2w", u - 3)])
                act_raw(E, s, R.rp[p][:, :], R.S2p[p][:, :], AF.Reciprocal, R.zb[:, :])
                M[("p2r", u)] = E.ss.n
                u += 1
    E.ss.lag = 1



def prog_gpsimd_b(E, R):
    g, M = E.g, E.M
    E.w(g, E.cc, M["a2a"])
    g.memset(R.zb3[:, :], 0.0)
    g.memset(R.epsb3[:, :], EPS)
    g.memset(R.ones3[:, :], 1.0)
    M["p3memset"] = E.gc.n
    gw3 = E.slots["gw3"]
    for src, dst in ((R.dwout, R.wouts), (R.dwr, R.wrs), (R.dwk, R.wks), (R.dwv2, R.wv2s)):
        gw3.inc(g.dma_start(out=dst[:, :].rearrange("p (c f) -> p c f", c=8),
                            in_=src.ap().rearrange("(c p) f -> p c f", c=8)))
    gw3.inc(g.dma_start(out=R.bouts[:, :], in_=R.dbout.ap()))
    gw3.inc(g.dma_start(out=R.idn3[:, :], in_=R.didn.ap()))
    gw3.inc(g.dma_start(out=R.hmm[:, :], in_=R.dhm.ap()))
    gw3.inc(g.dma_start(out=R.murs[:, :], in_=R.dmur.ap()))
    gw3.inc(g.dma_start(out=R.muks[:, :], in_=R.dmuk.ap()))
    M["p3w"] = gw3.n
    if P3SUB < 2:
        return
    E.w(g, E.slots["shgi"], M["p3hgi"])
    if E.em:
        E.cc.inc(g.collective_compute(
            "AllGather", ALU.bypass, replica_groups=[list(range(8))],
            ins=[R.HGI.ap().opt()], outs=[R.HGO.ap().opt()]))
    else:
        E.cc.inc(_DummyInst())
    M["hag"] = E.cc.n
    E.w(g, E.cc, M["hag"])
    E.slots["ghg"].inc(g.dma_start(out=R.hgos[:, :], in_=R.HGO.ap()))
    M["p3hgo"] = E.slots["ghg"].n


def prog_sync_b(E, R):
    sy, M = E.sy, E.M
    E.w(sy, E.cc, M["a2a"])
    for it in range(NT3):
        buf = it % 2
        if it >= 2:
            E.w(sy, E.tt, M[("p3op", it - 2)])
        sot = E.slots["sot%d" % buf]
        sot.inc(sy.dma_start(
            out=R.OTt[buf][:, :].rearrange("p (c t) -> p c t", c=8),
            in_=R.A2O.ap().rearrange("(c p) t -> p c t", c=8)[:, :, it * 128:(it + 1) * 128]))
        M[("p3ot", it)] = (sot, sot.n)
        if it >= 2:
            E.w(sy, E.vv, M[("p3xr1", it - 2)])
        sxw = E.slots["sxw%d" % buf]
        sxw.inc(sy.dma_start(out=R.xw[buf][:, :], in_=R.dxw.ap()[it * 128:(it + 1) * 128, :]))
        M[("p3xw", it)] = (sxw, sxw.n)
    if P3SUB < 2:
        E.w(sy, E.vv, M[("p3xn3", NT3 - 1)])
        for half in range(2):
            ev = M[("p3t3ev", (NT3 - 1) * 2 + half)]
            if ev:
                E.w(sy, ev[0], ev[1])
        return
    E.w(sy, E.vv, M[("p3xn3", NT3 - 1)])
    E.slots["shgi"].inc(sy.dma_start(out=R.HGI.ap(), in_=R.xn3b[127:128, :]))
    M["p3hgi"] = E.slots["shgi"].n
    if P3SUB < 5:
        return
    for it in range(NT3):
        buf = it % 2
        if it < 2:
            E.w(sy, E.vv, M[("p3xr1", NT3 - 2 + it)])
        else:
            E.w(sy, E.vv, M[("p3y", it - 2)])
        sxw = E.slots["sxw%d" % buf]
        sxw.inc(sy.dma_start(out=R.xw[buf][:, :], in_=R.dxw.ap()[it * 128:(it + 1) * 128, :]))
        M[("p3xw2", it)] = (sxw, sxw.n)
        E.w(sy, E.vv, M[("p3y", it)])
        syt = E.slots["sy%d" % (it % 2)]
        syt.inc(sy.dma_start(out=R.dy.ap()[it * 128:(it + 1) * 128, :], in_=R.ysb[it % 2][:, :]))
        M[("p3yst", it)] = (syt, syt.n)
    if E.em:
        sy.wait_ge(E.slots["sy0"].sem, E.slots["sy0"].n)
        sy.wait_ge(E.slots["sy1"].sem, E.slots["sy1"].n)


def prog_tensor_b(E, R):
    t, M = E.t, E.M
    E.w(t, E.slots["gw3"], M["p3w"])
    E.w(t, E.gc, M["p3memset"])
    for it in range(NT3):
        buf = it % 2
        if not (P3OPS & 1):
            break
        otm = M[("p3ot", it)]
        if otm:
            E.w(t, otm[0], otm[1])
        if it >= 2:
            E.w(t, E.ss, M[("p3pev", it - 2)])
            E.w(t, E.vv, M[("p3ps", it - 2)])
        for fh in range(2):
            for ccn in range(8):
                t.matmul(R.Pp2[it % 2][:, fh * 512:(fh + 1) * 512],
                         R.OTt[buf][:, ccn * 128:(ccn + 1) * 128],
                         R.wouts[:, ccn * D + fh * 512: ccn * D + fh * 512 + 512],
                         start=(ccn == 0), stop=False)
            inst = t.matmul(R.Pp2[it % 2][:, fh * 512:(fh + 1) * 512], R.ones3[:, 0:128],
                            R.bouts[:, fh * 512:(fh + 1) * 512], start=False, stop=True)
        E.tt.inc(inst)
        M[("p3op", it)] = E.tt.n
        if not (P3OPS & 32):
            continue
        E.w(t, E.vv, M[("p3xn3", it)])
        for half in range(2):
            tp = R.TP3[half]
            ghalf = it * 2 + half
            if ghalf >= 2:
                ev = M[("p3t3ev", ghalf - 2)]
                if ev:
                    E.w(t, ev[0], ev[1])
            for q in range(4):
                ccn = half * 4 + q
                inst = t.transpose(tp[:, q * 128:(q + 1) * 128],
                                   R.xn3b[:, ccn * 128:(ccn + 1) * 128], R.idn3[:, :])
            E.tt.inc(inst)
            M[("p3t3", it, half)] = E.tt.n
    if P3SUB < 2:
        return
    E.w(t, E.slots["ghg"], M["p3hgo"])
    for fh in range(2):
        E.tt.inc(t.matmul(R.HP[0:1, :], R.hmm[:, :], R.hgos[:, fh * 512:(fh + 1) * 512],
                          start=True, stop=True))
        M[("p3hp", fh)] = E.tt.n
        E.w(t, E.ss, M[("p3hev", fh)])
    E.w(t, E.ss, M[("p3hev", 1)])
    for ccn in range(8):
        inst = t.transpose(R.HS[:, 2 * ccn:2 * ccn + 1], R.halo[0:1, ccn * 128:(ccn + 1) * 128],
                           R.idn3[0:1, 0:1])
    E.tt.inc(inst)
    M["p3hst"] = E.tt.n
    if P3SUB < 4:
        return
    for ts4 in range(4):
        for dt_ in range(8):
            u = ts4 * 8 + dt_
            p = u % 2
            if u >= 2:
                E.w(t, E.ss, M[("p3kact", u - 2)])
            E.w(t, E.vv, M[("p3mixk", ts4)])
            for ccn in range(8):
                inst = t.matmul(R.Kp[p][:, :],
                                R.wks[:, ccn * D + dt_ * 128: ccn * D + dt_ * 128 + 128],
                                R.xmkT[:, ccn * 512:(ccn + 1) * 512],
                                start=(ccn == 0), stop=(ccn == 7))
            E.tt.inc(inst)
            M[("p3k", u)] = E.tt.n
        if P3SUB < 5:
            continue
        for q in range(4):
            it = ts4 * 4 + q
            E.w(t, E.vv, M[("p3mixr", ts4)])
            if it >= 2:
                E.w(t, E.ss, M[("p3sig", it - 2)])
                E.w(t, E.vv, M[("p3ff", it - 2)])
            for fh in range(2):
                for ccn in range(8):
                    inst = t.matmul(R.Pp2[it % 2][:, fh * 512:(fh + 1) * 512],
                                    R.xmrT[:, ccn * 512 + q * 128: ccn * 512 + q * 128 + 128],
                                    R.wrs[:, ccn * D + fh * 512: ccn * D + fh * 512 + 512],
                                    start=(ccn == 0), stop=(ccn == 7))
            E.tt.inc(inst)
            M[("p3r", it)] = E.tt.n
            E.w(t, E.ss, M[("p3sig", it)])
            E.w(t, E.ss, M[("p3kact", ts4 * 8 + 7)])
            for fh in range(2):
                for ccn in range(8):
                    inst = t.matmul(R.Pp2[it % 2][:, fh * 512:(fh + 1) * 512],
                                    R.kactS[:, ccn * 512 + q * 128: ccn * 512 + q * 128 + 128],
                                    R.wv2s[:, ccn * D + fh * 512: ccn * D + fh * 512 + 512],
                                    start=(ccn == 0), stop=(ccn == 7))
            E.tt.inc(inst)
            M[("p3vo", it)] = E.tt.n


def prog_vector_b(E, R):
    v, M = E.v, E.M
    E.w(v, E.slots["gw3"], M["p3w"])
    v.tensor_scalar(R.omurs[:, :], R.murs[:, :], -1.0, 1.0, ALU.mult, ALU.add)
    v.tensor_scalar(R.omuks[:, :], R.muks[:, :], -1.0, 1.0, ALU.mult, ALU.add)
    for it in range(NT3):
        if not (P3OPS & 2):
            break
        E.w(v, E.tt, M[("p3op", it)])
        if VB_MODE == 0:
            v.reduce_sum(R.l3[:, 0:1], R.Pp2[it % 2][:, :], axis=AXX)
        elif VB_MODE == 1:
            v.reduce_sum(R.l3[:, 0:1], R.Pp2[it % 2][:, 0:512], axis=AXX)
        elif VB_MODE == 2:
            v.tensor_copy(R.l3[:, 0:8], R.Pp2[it % 2][:, 0:8])
        else:
            v.tensor_copy(R.l3[:, 0:8], R.xw[0][:, 0:8])
        M[("p3ps", it)] = E.vv.n
        if not (P3OPS & 8):
            continue
        E.w(v, E.ss, M[("p3pev", it)])
        v.tensor_scalar(R.l3[:, 2:3], R.l3[:, 0:1], 1.0 / D, None, ALU.mult)
        v.tensor_scalar(R.l3[:, 3:4], R.l3[:, 1:2], 1.0 / D, None, ALU.mult)
        v.tensor_tensor(R.l3[:, 4:5], R.l3[:, 2:3], R.l3[:, 2:3], ALU.mult)
        v.tensor_tensor(R.l3[:, 5:6], R.l3[:, 3:4], R.l3[:, 4:5], ALU.subtract)
        M[("p3var2", it)] = E.vv.n
        E.w(v, E.ss, M[("p3rs2", it)])
        v.tensor_scalar(R.ln2o[:, it * D:(it + 1) * D], R.Psb[:, :],
                                 R.l3[:, 2:3], R.l3[:, 6:7], ALU.subtract, ALU.mult)
        M[("p3ln2", it)] = E.vv.n
        if not (P3OPS & 16):
            continue
        xwm = M[("p3xw", it)]
        if xwm:
            E.w(v, xwm[0], xwm[1])
        v.tensor_tensor(R.xr1[:, :], R.xw[it % 2][:, :],
                                 R.ln2o[:, it * D:(it + 1) * D], ALU.add)
        M[("p3xr1", it)] = E.vv.n
        v.reduce_sum(R.l3[:, 8:9], R.xr1[:, :], axis=AXX)
        E.w(v, E.ss, M[("p3sq3", it)])
        v.tensor_scalar(R.l3[:, 10:11], R.l3[:, 8:9], 1.0 / D, None, ALU.mult)
        v.tensor_scalar(R.l3[:, 11:12], R.l3[:, 9:10], 1.0 / D, None, ALU.mult)
        v.tensor_tensor(R.l3[:, 12:13], R.l3[:, 10:11], R.l3[:, 10:11], ALU.mult)
        v.tensor_tensor(R.l3[:, 13:14], R.l3[:, 11:12], R.l3[:, 12:13], ALU.subtract)
        M[("p3var3", it)] = E.vv.n
        E.w(v, E.ss, M[("p3rs3", it)])
        if it >= 1:
            E.w(v, E.tt, M[("p3t3", it - 1, 1)])
        v.tensor_scalar(R.xn3b[:, :], R.xr1[:, :],
                                 R.l3[:, 10:11], R.l3[:, 14:15], ALU.subtract, ALU.mult)
        M[("p3xn3", it)] = E.vv.n
        for half in range(2):
            E.w(v, E.tt, M[("p3t3", it, half)])
            dst = R.xn3T[:, :].rearrange("p (c w) -> p c w", c=8)[:, half * 4:half * 4 + 4,
                                                                 it * 128:it * 128 + 128]
            v.tensor_copy(dst, R.TP3[half][:, :].rearrange("p (c w) -> p c w", c=4))
            M[("p3t3ev", it * 2 + half)] = (E.vv, E.vv.n)
    if P3SUB < 2:
        return
    E.w(v, E.tt, M["p3hst"])
    v.tensor_copy(R.haloT[:, :], R.HS[:, 0:16:2])
    M["p3halot"] = E.vv.n
    if P3SUB < 3:
        return
    for ts4 in range(4):
        t0 = ts4 * 512
        for mu, omu, dst, mk in ((R.murs, R.omurs, R.xmrT, "p3mixr"), (R.muks, R.omuks, R.xmkT, "p3mixk")):
            if ts4 == 0:
                E.w(v, E.vv, M["p3halot"])
            if ts4 >= 1:
                E.w(v, E.tt, M[("p3r", ts4 * 4 - 1)] if mk == "p3mixr" else M[("p3k", (ts4 - 1) * 8 + 7)])
            for ccn in range(8):
                cbase = ccn * WIN
                if t0 == 0:
                    v.tensor_scalar(R.mixs[:, 1:512], R.xn3T[:, cbase: cbase + 511],
                                    omu[:, ccn:ccn + 1], None, ALU.mult)
                    v.tensor_scalar(R.mixs[:, 0:1], R.haloT[:, ccn:ccn + 1],
                                    omu[:, ccn:ccn + 1], None, ALU.mult)
                else:
                    v.tensor_scalar(R.mixs[:, 0:512], R.xn3T[:, cbase + t0 - 1: cbase + t0 + 511],
                                    omu[:, ccn:ccn + 1], None, ALU.mult)
                inst = v.scalar_tensor_tensor(dst[:, ccn * 512:(ccn + 1) * 512],
                                              R.xn3T[:, cbase + t0: cbase + t0 + 512],
                                              mu[:, ccn:ccn + 1], R.mixs[:, 0:512],
                                              ALU.mult, ALU.add)
            M[(mk, ts4)] = E.vv.n
        if P3SUB < 4:
            continue
        for dt_ in range(8):
            u = ts4 * 8 + dt_
            p = u % 2
            E.w(v, E.tt, M[("p3k", u)])
            if u >= 1:
                E.w(v, E.ss, M[("p3kact", u - 1)])
            v.tensor_scalar_max(R.kpre[:, :], R.Kp[p][:, :], 0.0)
            M[("p3relu", u)] = E.vv.n
        if P3SUB < 5:
            continue
        for q in range(4):
            it = ts4 * 4 + q
            E.w(v, E.tt, M[("p3vo", it)])
            if it >= 1:
                E.w(v, E.ss, M[("p3sq4", it - 1)])
            v.tensor_tensor(R.ffo[:, :], R.Pp2[it % 2][:, :], R.rsig[:, :], ALU.mult)
            M[("p3ff", it)] = E.vv.n
            v.reduce_sum(R.l3[:, 16:17], R.ffo[:, :], axis=AXX)
            E.w(v, E.ss, M[("p3sq4", it)])
            v.tensor_scalar(R.l3[:, 18:19], R.l3[:, 16:17], 1.0 / D, None, ALU.mult)
            v.tensor_scalar(R.l3[:, 19:20], R.l3[:, 17:18], 1.0 / D, None, ALU.mult)
            v.tensor_tensor(R.l3[:, 20:21], R.l3[:, 18:19], R.l3[:, 18:19], ALU.mult)
            v.tensor_tensor(R.l3[:, 21:22], R.l3[:, 19:20], R.l3[:, 20:21], ALU.subtract)
            M[("p3var4", it)] = E.vv.n
            E.w(v, E.ss, M[("p3rs4", it)])
            v.tensor_scalar(R.ln4o[:, :], R.ffo[:, :], R.l3[:, 18:19], R.l3[:, 22:23],
                            ALU.subtract, ALU.mult)
            xwm = M[("p3xw2", it)]
            if xwm:
                E.w(v, xwm[0], xwm[1])
            if it >= 2:
                ym = M[("p3yst", it - 2)]
                if ym:
                    E.w(v, ym[0], ym[1])
            v.tensor_tensor(R.ysb[it % 2][:, :], R.xw[it % 2][:, :],
                            R.ln2o[:, it * D:(it + 1) * D], ALU.add)
            v.tensor_tensor(R.ysb[it % 2][:, :], R.ysb[it % 2][:, :], R.ln4o[:, :], ALU.add)
            M[("p3y", it)] = E.vv.n


def prog_scalar_b(E, R):
    s, M = E.s, E.M
    for it in range(NT3):
        if not (P3OPS & 4):
            break
        E.w(s, E.tt, M[("p3op", it)])
        E.w(s, E.vv, M[("p3ps", it)])
        if it >= 1:
            E.w(s, E.vv, M[("p3xn3", it - 1)])
        s.copy(R.Psb[:, :], R.Pp2[it % 2][:, :])
        s.activation(R.sq3[:, :], R.Pp2[it % 2][:, :], AF.Square, bias=R.zb3[:, :],
                              accum_out=R.l3[:, 1:2])
        M[("p3pev", it)] = E.ss.n
        if not (P3OPS & 8):
            continue
        E.w(s, E.vv, M[("p3var2", it)])
        s.activation(R.l3[:, 6:7], R.l3[:, 5:6], AF.Sqrt, bias=R.epsb3[:, :])
        act_raw(E, s, R.l3[:, 6:7], R.l3[:, 6:7], AF.Reciprocal, R.zb3[:, :])
        M[("p3rs2", it)] = E.ss.n
        E.w(s, E.vv, M[("p3xr1", it)])
        s.activation(R.sq3[:, :], R.xr1[:, :], AF.Square, bias=R.zb3[:, :],
                              accum_out=R.l3[:, 9:10])
        M[("p3sq3", it)] = E.ss.n
        E.w(s, E.vv, M[("p3var3", it)])
        s.activation(R.l3[:, 14:15], R.l3[:, 13:14], AF.Sqrt, bias=R.epsb3[:, :])
        act_raw(E, s, R.l3[:, 14:15], R.l3[:, 14:15], AF.Reciprocal, R.zb3[:, :])
        M[("p3rs3", it)] = E.ss.n

    if P3SUB < 2:
        return
    for fh in range(2):
        E.w(s, E.tt, M[("p3hp", fh)])
        s.copy(R.halo[0:1, fh * 512:(fh + 1) * 512], R.HP[0:1, :])
        M[("p3hev", fh)] = E.ss.n
    if P3SUB < 4:
        return
    for ts4 in range(4):
        for dt_ in range(8):
            u = ts4 * 8 + dt_
            E.w(s, E.vv, M[("p3relu", u)])
            if dt_ == 0 and ts4 >= 1:
                E.w(s, E.tt, M[("p3vo", ts4 * 4 - 1)])
            s.activation(R.kactS[:, dt_ * 512:(dt_ + 1) * 512], R.kpre[:, :],
                                  AF.Square, bias=R.zb3[:, :])
            M[("p3kact", u)] = E.ss.n
        if P3SUB < 5:
            continue
        for q in range(4):
            it = ts4 * 4 + q
            E.w(s, E.tt, M[("p3r", it)])
            if it >= 1:
                E.w(s, E.vv, M[("p3ff", it - 1)])
            s.activation(R.rsig[:, :], R.Pp2[it % 2][:, :], AF.Sigmoid, bias=R.zb3[:, :])
            M[("p3sig", it)] = E.ss.n
            E.w(s, E.vv, M[("p3ff", it)])
            s.activation(R.sq3[:, :], R.ffo[:, :], AF.Square, bias=R.zb3[:, :],
                                  accum_out=R.l3[:, 17:18])
            M[("p3sq4", it)] = E.ss.n
            E.w(s, E.vv, M[("p3var4", it)])
            s.activation(R.l3[:, 22:23], R.l3[:, 21:22], AF.Sqrt, bias=R.epsb3[:, :])
            act_raw(E, s, R.l3[:, 22:23], R.l3[:, 22:23], AF.Reciprocal, R.zb3[:, :])
            M[("p3rs4", it)] = E.ss.n



class _R:
    pass


def build_nc():
    nc = bass.Bass("TRN2", target_bir_lowering=False)
    R = _R()
    R.dxw = nc.dram_tensor("x_win", [WIN, D], F32, kind="ExternalInput")
    R.dwqk = nc.dram_tensor("wqk", [D, 2048], F32, kind="ExternalInput")
    R.dbqk = nc.dram_tensor("bqk", [1, 2048], F32, kind="ExternalInput")
    R.dwv = nc.dram_tensor("wv", [D, D], F32, kind="ExternalInput")
    R.dbv = nc.dram_tensor("bv", [1, D], F32, kind="ExternalInput")
    R.dwout = nc.dram_tensor("wout", [D, D], F32, kind="ExternalInput")
    R.dbout = nc.dram_tensor("bout", [1, D], F32, kind="ExternalInput")
    R.dwr = nc.dram_tensor("wr_t", [D, D], F32, kind="ExternalInput")
    R.dwk = nc.dram_tensor("wk_t", [D, D], F32, kind="ExternalInput")
    R.dwv2 = nc.dram_tensor("wv2_t", [D, D], F32, kind="ExternalInput")
    R.dmur = nc.dram_tensor("mu_r", [128, 8], F32, kind="ExternalInput")
    R.dmuk = nc.dram_tensor("mu_k", [128, 8], F32, kind="ExternalInput")
    R.dtblc = nc.dram_tensor("tblc", [128, WIN], F32, kind="ExternalInput")
    R.dtbls = nc.dram_tensor("tbls", [128, WIN], F32, kind="ExternalInput")
    R.dperm = nc.dram_tensor("permm", [128, 128], F32, kind="ExternalInput")
    R.didn = nc.dram_tensor("idn", [128, 128], F32, kind="ExternalInput")
    R.dhm = nc.dram_tensor("hmask", [8, 1], F32, kind="ExternalInput")
    R.dy = nc.dram_tensor("y", [WIN, D], F32, kind="ExternalOutput")
    R.A2I2 = nc.dram_tensor("a2i2", [1024, A2C], BF16)
    R.A2O2 = nc.dram_tensor("a2o2", [1024, A2C], BF16)
    R.A2I = nc.dram_tensor("a2i", [1024, WIN], BF16)
    R.A2O = nc.dram_tensor("a2o", [1024, WIN], BF16)
    R.HGI = nc.dram_tensor("hgi", [1, D], BF16)
    R.HGO = nc.dram_tensor("hgo", [8, D], BF16)

    marks = Marks()
    E = Env(nc, marks)

    with ExitStack() as top:
        E.sems(nc, top)
        with ExitStack() as pa:
            e = pa.enter_context
            R.KK = e(nc.sbuf_tensor("KK", [128, 2 * T], BF16))
            R.Vt = e(nc.sbuf_tensor("Vt", [128, T], BF16))
            R.wqk = e(nc.sbuf_tensor("wqk_s", [128, 8 * 2048], BF16))
            R.wv = e(nc.sbuf_tensor("wv_s", [128, 8 * 1024], BF16))
            R.bqk = e(nc.sbuf_tensor("bqk_s", [1, 2048], BF16))
            R.bv = e(nc.sbuf_tensor("bv_s", [1, D], BF16))
            R.Vst = e(nc.sbuf_tensor("Vst", [128, 1024], BF16))
            R.ones = e(nc.sbuf_tensor("ones_s", [1, 512], BF16))
            R.permb = e(nc.sbuf_tensor("permb", [128, 128], BF16))
            R.idnb = e(nc.sbuf_tensor("idnb", [128, 128], BF16))
            R.zb = e(nc.sbuf_tensor("zb", [128, 1], F32))
            R.epsb = e(nc.sbuf_tensor("epsb", [128, 1], F32))
            R.xts = [e(nc.sbuf_tensor("xts0", [128, 4 * D], BF16)),
                     ColView(R.KK, 0, 4 * D)]
            R.sqsc = e(nc.sbuf_tensor("sqsc", [128, D], F32))
            R.xnb = [e(nc.sbuf_tensor("xnb0", [128, 4 * D], BF16)),
                     ColView(R.KK, 4 * D, 4 * D)]
            R.xnT = e(nc.sbuf_tensor("xnT", [128, 8 * OT], BF16))
            R.sums = e(nc.sbuf_tensor("sums", [128, 4 * NOT1], F32))
            R.sumsq = e(nc.sbuf_tensor("sumsq", [128, 4 * NOT1], F32))
            R.lnt = e(nc.sbuf_tensor("lnt", [128, 48], F32))
            R.es = ColView(R.KK, 9 * D + 512, 512)
            R.expm = e(nc.sbuf_tensor("expm", [128, 512], BF16))
            R.stg = [e(nc.sbuf_tensor(f"stg{i}", [128, 1024], BF16)) for i in range(2)]
            R.asb = e(nc.sbuf_tensor("asb", [128, 512], BF16))
            R.bsb = e(nc.sbuf_tensor("bsb", [128, 512], BF16))
            R.tcs = [e(nc.sbuf_tensor("tcs0", [128, 512], BF16)),
                     ColView(R.KK, 8 * D, 512)]
            R.tss = [e(nc.sbuf_tensor("tss0", [128, 512], BF16)),
                     ColView(R.KK, 8 * D + 512, 512)]
            R.Qt = [e(nc.sbuf_tensor(f"Qt{i}", [128, 1024], BF16)) for i in range(2)]
            R.rp = [e(nc.sbuf_tensor(f"rp{i}", [128, 512], F32)) for i in range(3)]
            R.WT = [e(nc.sbuf_tensor(f"WT{i}", [128, 512], BF16)) for i in range(4)]
            R.Ost = [e(nc.sbuf_tensor(f"Ost{i}", [128, 512], BF16)) for i in range(2)]
            R.S1p = [e(nc.psum_tensor(f"S1p{i}", [128, 512], F32)) for i in range(3)]
            R.S2p = [e(nc.psum_tensor(f"S2p{i}", [128, 512], F32)) for i in range(3)]
            R.TPq = [e(nc.psum_tensor(f"TPq{i}", [128, 1024], BF16)) for i in range(2)]

            E.em = False
            E.reset()
            E.use_engines(t=_DummyEng(), v=_DummyEng(), s=_DummyEng(), g=_DummyEng(), sy=_DummyEng())
            for fn in (prog_sync_a, prog_gpsimd_a, prog_tensor_a, prog_vector_a, prog_scalar_a):
                fn(E, R)
            snapA = [c.n for c in E.ctrs]

            E._snapA = snapA

            Marks.strict = True
            E.em = True
            E.reset()
            with nc.Block() as blk:
                @blk.sync
                def _(sy):
                    E.use_engines(sy=sy)
                    prog_sync_a(E, R)

                @blk.gpsimd
                def _(g):
                    E.use_engines(g=g)
                    prog_gpsimd_a(E, R)

                @blk.tensor
                def _(t):
                    E.use_engines(t=t)
                    prog_tensor_a(E, R)

                @blk.vector
                def _(v):
                    E.use_engines(v=v)
                    prog_vector_a(E, R)

                @blk.scalar
                def _(s):
                    E.use_engines(s=s)
                    prog_scalar_a(E, R)
            assert [c.n for c in E.ctrs] == snapA, "pass A count mismatch"

        if STAGE < 3:
            return nc
        with ExitStack() as pb:
            e = pb.enter_context
            R.OTt = [e(nc.sbuf_tensor(f"OTt{i}", [128, 8 * 128], BF16)) for i in range(2)]
            R.ln2o = e(nc.sbuf_tensor("ln2o", [128, NT3 * D], BF16))
            R.xn3T = e(nc.sbuf_tensor("xn3T", [128, 8 * WIN], BF16))
            R.wouts = e(nc.sbuf_tensor("wouts", [128, 8 * D], BF16))
            R.bouts = e(nc.sbuf_tensor("bouts", [1, D], BF16))
            R.wrs = e(nc.sbuf_tensor("wrs", [128, 8 * D], BF16))
            R.wks = e(nc.sbuf_tensor("wks", [128, 8 * D], BF16))
            R.wv2s = e(nc.sbuf_tensor("wv2s", [128, 8 * D], BF16))
            R.murs = e(nc.sbuf_tensor("murs", [128, 8], F32))
            R.muks = e(nc.sbuf_tensor("muks", [128, 8], F32))
            R.omurs = e(nc.sbuf_tensor("omurs", [128, 8], F32))
            R.omuks = e(nc.sbuf_tensor("omuks", [128, 8], F32))
            R.zb3 = e(nc.sbuf_tensor("zb3", [128, 1], F32))
            R.epsb3 = e(nc.sbuf_tensor("epsb3", [128, 1], F32))
            R.ones3 = e(nc.sbuf_tensor("ones3", [1, 512], BF16))
            R.idn3 = e(nc.sbuf_tensor("idn3", [128, 128], BF16))
            R.hmm = e(nc.sbuf_tensor("hmm", [8, 1], BF16))
            R.hgos = e(nc.sbuf_tensor("hgos", [8, D], BF16))
            R.halo = e(nc.sbuf_tensor("halo", [1, D], BF16))
            R.haloT = e(nc.sbuf_tensor("haloT", [128, 8], BF16))
            R.xw = [e(nc.sbuf_tensor(f"xw{i}", [128, D], F32)) for i in range(2)]
            R.Psb = e(nc.sbuf_tensor("Psb", [128, D], F32))
            R.xr1 = e(nc.sbuf_tensor("xr1", [128, D], F32))
            R.xn3b = e(nc.sbuf_tensor("xn3b", [128, D], BF16))
            R.sq3 = e(nc.sbuf_tensor("sq3", [128, D], F32))
            R.ln4o = e(nc.sbuf_tensor("ln4o", [128, D], F32))
            R.l3 = e(nc.sbuf_tensor("l3", [128, 24], F32))
            R.mixs = e(nc.sbuf_tensor("mixs", [128, 512], BF16))
            R.xmrT = e(nc.sbuf_tensor("xmrT", [128, 8 * 512], BF16))
            R.xmkT = e(nc.sbuf_tensor("xmkT", [128, 8 * 512], BF16))
            R.kactS = e(nc.sbuf_tensor("kactS", [128, 8 * 512], BF16))
            R.kpre = e(nc.sbuf_tensor("kpre", [128, 512], BF16))
            R.rsig = e(nc.sbuf_tensor("rsig", [128, D], BF16))
            R.ffo = e(nc.sbuf_tensor("ffo", [128, D], F32))
            R.ysb = [e(nc.sbuf_tensor(f"ysb{i}", [128, D], F32)) for i in range(2)]
            R.Pp2 = [e(nc.psum_tensor(f"Pp{i}", [128, 1024], F32)) for i in range(2)]
            R.Kp = [e(nc.psum_tensor(f"Kp{i}", [128, 512], F32)) for i in range(2)]
            R.TP3 = [e(nc.psum_tensor(f"TP3{i}", [128, 512], BF16)) for i in range(2)]
            R.HP = R.Kp[0][0:1, 0:512]
            R.HS = R.Kp[1][:, 0:8].bitcast(BF16)

            Marks.strict = False
            E.em = False
            for c, n0 in zip(E.ctrs, E._snapA):
                c.n = n0
            E.use_engines(t=_DummyEng(), v=_DummyEng(), s=_DummyEng(), g=_DummyEng(), sy=_DummyEng())
            for fn in (prog_gpsimd_b, prog_sync_b, prog_tensor_b, prog_vector_b, prog_scalar_b):
                fn(E, R)
            snapB = [c.n for c in E.ctrs]

            Marks.strict = True
            E.em = True
            for c, n0 in zip(E.ctrs, E._snapA):
                c.n = n0
            with nc.Block() as blk3:
                @blk3.gpsimd
                def _(g):
                    E.use_engines(g=g)
                    prog_gpsimd_b(E, R)

                @blk3.sync
                def _(sy):
                    E.use_engines(sy=sy)
                    prog_sync_b(E, R)

                @blk3.tensor
                def _(t):
                    E.use_engines(t=t)
                    prog_tensor_b(E, R)

                @blk3.vector
                def _(v):
                    E.use_engines(v=v)
                    prog_vector_b(E, R)

                @blk3.scalar
                def _(s):
                    E.use_engines(s=s)
                    prog_scalar_b(E, R)
            assert [c.n for c in E.ctrs] == snapB, "pass B count mismatch"

    return nc


def _host_prep(inputs):
    x = np.ascontiguousarray(np.asarray(inputs["L0_attn_x"], dtype=np.float32).reshape(T, D))
    wqkv = np.asarray(inputs["L0_attn_w_qkv"], dtype=np.float32)
    bqkv = np.asarray(inputs["L0_attn_b_qkv"], dtype=np.float32)
    wout = np.ascontiguousarray(np.asarray(inputs["L0_attn_w_out"], dtype=np.float32))
    bout = np.asarray(inputs["L0_attn_b_out"], dtype=np.float32).reshape(1, D)
    Wr = np.asarray(inputs["L0_ffn_W_r"], dtype=np.float32)
    Wk = np.asarray(inputs["L0_ffn_W_k"], dtype=np.float32)
    Wv2 = np.asarray(inputs["L0_ffn_W_v"], dtype=np.float32)
    mur = np.ascontiguousarray(np.asarray(inputs["L0_ffn_mu_r"], dtype=np.float32).reshape(8, 128).T)
    muk = np.ascontiguousarray(np.asarray(inputs["L0_ffn_mu_k"], dtype=np.float32).reshape(8, 128).T)

    perm = np.concatenate([np.arange(0, 64, 2), np.arange(1, 64, 2)])
    p32 = np.arange(32, dtype=np.float64)
    theta = 1.0 / (10000.0 ** (p32 / 32.0))
    pos = np.arange(N, dtype=np.float64)
    ang = pos[:, None] * theta[None, :]
    cosv = np.cos(ang).T
    sinv = np.sin(ang).T
    tblc = np.empty((128, N), dtype=np.float32)
    tbls = np.empty((128, N), dtype=np.float32)
    for g in range(4):
        r0 = g * 32
        tblc[r0:r0 + 32] = cosv
        if g % 2 == 0:
            tbls[r0:r0 + 32] = -sinv
        else:
            tbls[r0:r0 + 32] = sinv
    permm = np.zeros((128, 128), dtype=np.float32)
    for i in range(128):
        j = i + 32 if (i // 32) % 2 == 0 else i - 32
        permm[j, i] = 1.0
    idn = np.eye(128, dtype=np.float32)

    wr_t = np.ascontiguousarray(Wr.T)
    wk_t = np.ascontiguousarray(Wk.T)
    wv2_t = np.ascontiguousarray(Wv2.T)

    qcols = []
    for h in range(16):
        qcols.append(np.concatenate([
            wqkv[:, h * 64 + perm],
            wqkv[:, D + h * 64 + perm],
        ], axis=1))
    wqk = np.ascontiguousarray(np.concatenate(qcols, axis=1))
    bqk = np.ascontiguousarray(np.concatenate([
        np.concatenate([bqkv[h * 64 + perm], bqkv[D + h * 64 + perm]])
        for h in range(16)]).reshape(1, 2048).astype(np.float32))
    wv = np.ascontiguousarray(wqkv[:, 2 * D:3 * D])
    bv = np.ascontiguousarray(bqkv[2 * D:3 * D].reshape(1, D).astype(np.float32))
    in_maps = []
    for c in range(NCORE):
        pos0 = (c * WIN) % N
        hm = np.zeros((8, 1), dtype=np.float32)
        if c not in (0, 4):
            hm[c - 1, 0] = 1.0
        in_maps.append({
            "x_win": np.ascontiguousarray(x[c * WIN:(c + 1) * WIN]),
            "wqk": wqk, "bqk": bqk,
            "wv": wv, "bv": bv,
            "wout": wout, "bout": np.ascontiguousarray(bout),
            "wr_t": wr_t, "wk_t": wk_t, "wv2_t": wv2_t,
            "mu_r": mur, "mu_k": muk,
            "tblc": np.ascontiguousarray(tblc[:, pos0:pos0 + WIN]),
            "tbls": np.ascontiguousarray(tbls[:, pos0:pos0 + WIN]),
            "permm": permm, "idn": idn,
            "hmask": hm,
        })
    return in_maps


def kernel(**inputs):
    in_maps = _host_prep(inputs)
    nc = build_nc()
    res = run_bass_kernel_spmd(nc, in_maps, core_ids=list(range(NCORE)))
    y = np.concatenate([np.asarray(res.results[c]["y"], dtype=np.float32)
                        for c in range(NCORE)], axis=0)
    return y.reshape(B, N, D)


# version 13 (final)
# speedup vs baseline: 2.0135x; 1.0131x over previous
import sys
sys.path.insert(0, "/opt/trn_rl_repo")
import numpy as np
import concourse.bass as bass
import concourse.mybir as mybir
from concourse.bass_utils import run_bass_kernel_spmd
from concourse.aot_env import allow_ici_all_to_all
from contextlib import ExitStack

F32 = mybir.dt.float32
BF16 = mybir.dt.bfloat16
AF = mybir.ActivationFunctionType
ALU = mybir.AluOpType
AXX = mybir.AxisListType.X

B = 2
N = 8192
T = 16384
D = 1024
NCORE = 8
WIN = 2048
OT = 512
NOT = T // OT
NI = N // OT
NJ = N // 128
NT3 = WIN // 128
NOT1 = WIN // OT
NFT = 16
A2C = 5 * WIN
EPS = 1e-5
STAGE = 3
P3SUB = 5
P3OPS = 63
LOOSE = False
VB_MODE = 0
SYB_MODE = 0


class Marks(dict):
    strict = False

    def __missing__(self, key):
        if Marks.strict and not LOOSE:
            raise KeyError(key)
        return 0


class _DummyInst:
    def then_inc(self, *a, **k):
        return self


class _DummyEng:
    def __getattr__(self, name):
        return lambda *a, **k: _DummyInst()


_PASSTHRU = {"wait_ge", "dma_start", "collective_compute", "nop", "drain",
             "indirect_dma_start", "sem_inc"}


class EngProxy:

    def __init__(self, real, ctr, em):
        self.raw = real
        self._ctr = ctr
        self._em = em

    def __getattr__(self, name):
        if name in _PASSTHRU:
            return getattr(self.raw, name)
        real_fn = getattr(self.raw, name)
        def call(*a, **k):
            c = self._ctr
            if self._em and c.n - (c.lag - 1) * c.step > 0:
                self.raw.wait_ge(c.sem, c.n - (c.lag - 1) * c.step)
            inst = real_fn(*a, **k)
            c.n += c.step
            if self._em and c.sem is not None:
                inst.then_inc(c.sem, c.step)
            return inst
        return call


class ColView:

    def __init__(self, t, c0, w):
        self.t, self.c0, self.w = t, c0, w

    def __getitem__(self, idx):
        p, c = idx
        a = (c.start or 0) + self.c0
        b = (c.stop if c.stop is not None else self.w) + self.c0
        return self.t[p, a:b]


class Ctr:
    def __init__(self, name, step):
        self.name = name
        self.step = step
        self.n = 0
        self.sem = None
        self.lag = 1

    def inc(self, inst):
        self.n += self.step
        inst.then_inc(self.sem, self.step) if self.sem is not None else None
        return inst


class Env:

    def __init__(self, nc, marks):
        self.nc = nc
        self.M = marks
        self.em = False
        self.t = self.sy = _DummyEng()
        self.v = self.s = self.g = None
        self.tt = Ctr("tt", 1)
        self.vv = Ctr("vv", 1)
        self.ss = Ctr("ss", 1)
        self.gc = Ctr("gc", 1)
        self.cc = Ctr("cc", 1)
        self.syn = Ctr("syn", 16)
        self.gd = Ctr("gd", 16)
        self.st = Ctr("st", 16)
        slot_names = ["gw", "gx0", "gx1", "gt0", "gt1", "gw3", "ghg",
                      "sg0", "sg1", "skk", "sqt0", "sqt1", "sos0", "sos1",
                      "sot0", "sot1", "sxw0", "sxw1", "shgi", "sy0", "sy1", "sv0", "sv1"]
        self.slots = {n: Ctr(n, 16) for n in slot_names}
        self.ctrs = [self.tt, self.vv, self.ss, self.gc, self.cc, self.syn, self.gd, self.st] + \
            [self.slots[n] for n in slot_names]

    def use_engines(self, t=None, v=None, s=None, g=None, sy=None):
        if t is not None:
            self.t = t
        if sy is not None:
            self.sy = sy
        if v is not None:
            self.v = EngProxy(v, self.vv, self.em)
        if s is not None:
            self.s = EngProxy(s, self.ss, self.em)
        if g is not None:
            self.g = EngProxy(g, self.gc, self.em)

    def reset(self):
        for c in self.ctrs:
            c.n = 0

    def sems(self, nc, stack):
        for c in self.ctrs:
            c.sem = stack.enter_context(nc.semaphore("sem_" + c.name))

    def w(self, eng, ctr, val):
        if self.em and val > 0:
            eng.wait_ge(ctr.sem, val)


def act_raw(E, eng, out_ap, in_ap, func, bias_ap, scale=1.0):
    c = eng._ctr
    if not E.em:
        c.n += c.step
        return _DummyInst()
    nc = E.nc
    with nc.discard():
        proto = eng.raw.activation(out_ap, in_ap, AF.Exp, bias=bias_ap, scale=scale).ins
    new = mybir.InstActivation(
        name=nc.get_next_instruction_name(), engine=proto.engine,
        ins=list(proto.ins), outs=list(proto.outs), func=func, debug=proto.debug)
    if c.n - (c.lag - 1) * c.step > 0:
        eng.raw.wait_ge(c.sem, c.n - (c.lag - 1) * c.step)
    inst = eng.raw.add_instruction(new)
    c.n += c.step
    inst.then_inc(c.sem, c.step)
    return inst



def prog_sync_a(E, R):
    sy, M = E.sy, E.M
    for ot in range(NOT1):
        lt = ot * OT
        for h in range(NFT):
            ft = ot * NFT + h
            sp = ft % 2
            d, hl = h // 2, h % 2
            sg = E.slots["sg%d" % sp]
            E.w(sy, E.vv, M[("p1stg", ot, h)])
            sg.inc(sy.dma_start(
                out=R.A2I2c[ot].ap()[128 * d:128 * d + 128,
                                     1024 + hl * 512: 1024 + hl * 512 + 512]
                    .rearrange("(g p) t -> p g t", g=2),
                in_=R.stg[sp][0:64, :].rearrange("p (g t) -> p g t", g=2)))
            sg.inc(sy.dma_start(
                out=R.A2I2c[ot].ap()[128 * d:128 * d + 128,
                                     hl * 512: hl * 512 + 512]
                    .rearrange("(g p) t -> p g t", g=2),
                in_=R.stg[sp][64:128, :].rearrange("p (g t) -> p g t", g=2)))
            M[("p1syn_stg", ot, h)] = (sg, sg.n)
        sv = E.slots["sv%d" % (ot % 2)]
        for sb in range(4):
            jt = ot * 4 + sb
            E.w(sy, E.ss, M[("p1vst", ot, sb)])
            sv.inc(sy.dma_start(
                out=R.A2I2c[ot].ap().rearrange("(d p) c -> p d c", d=8)[:, :,
                    2048 + sb * 128: 2048 + sb * 128 + 128],
                in_=R.Vst[:, :].rearrange("p (d c) -> p d c", d=8)))
            M[("p1vstd", ot, sb)] = (sv, sv.n)
    M["p1sg0_end"] = E.slots["sg0"].n
    M["p1sg1_end"] = E.slots["sg1"].n
    M["p1sv0_end"] = E.slots["sv0"].n
    M["p1sv1_end"] = E.slots["sv1"].n
    if STAGE < 2:
        M["p2syn_end"] = E.syn.n
        return
    def store_O(bi):
        b, I = bi // NI, bi % NI
        gtok = b * N + I * OT
        w_, c0 = gtok // WIN, gtok % WIN
        so = E.slots["sos%d" % (bi % 2)]
        so.inc(sy.dma_start(out=R.A2I.ap()[128 * w_:128 * w_ + 128, c0:c0 + 512],
                            in_=R.Ost[bi % 2][:, :]))
        M[("p2ost", bi)] = (so, so.n)

    E.w(sy, E.cc, M["a2a2"])
    for bi in range(B * NI):
        b, I = bi // NI, bi % NI
        buf = bi % 2
        if bi >= 2:
            E.w(sy, E.tt, M[("p2tte", bi - 2)])
        gt = b * N + I * OT
        s_src = gt // WIN
        off = gt % WIN
        sq = E.slots["sqt%d" % buf]
        sq.inc(sy.dma_start(
            out=R.Qt[buf][:, :].rearrange("p (h t) -> p h t", h=2),
            in_=R.A2O2c[off // 512].ap()[128 * s_src:128 * s_src + 128, 1024:2048]
                .rearrange("p (h n) -> p h n", h=2)))
        M[("p2qt", bi)] = (sq, sq.n)
        if bi >= 1:
            E.w(sy, E.vv, M[("p2evac", bi - 1)])
            store_O(bi - 1)
    E.w(sy, E.vv, M[("p2evac", B * NI - 1)])
    store_O(B * NI - 1)
    M["p2sos0_end"] = E.slots["sos0"].n
    M["p2sos1_end"] = E.slots["sos1"].n


def prog_gpsimd_a(E, R):
    g, M = E.g, E.M
    g.memset(R.zb[:, :], 0.0)
    g.memset(R.epsb[:, :], EPS)
    g.memset(R.ones[:, :], 1.0)
    M["p1memset"] = E.gc.n
    gw = E.slots["gw"]
    gw.inc(g.dma_start(out=R.wqk[:, :].rearrange("p (c f) -> p c f", c=8),
                       in_=R.dwqk.ap().rearrange("(c p) f -> p c f", c=8)))
    gw.inc(g.dma_start(out=R.wv[:, :].rearrange("p (c f) -> p c f", c=8),
                       in_=R.dwv.ap().rearrange("(c p) f -> p c f", c=8)))
    gw.inc(g.dma_start(out=R.bqk[:, :], in_=R.dbqk.ap()))
    gw.inc(g.dma_start(out=R.bv[:, :], in_=R.dbv.ap()))
    gw.inc(g.dma_start(out=R.permb[:, :], in_=R.dperm.ap()))
    gw.inc(g.dma_start(out=R.idnb[:, :], in_=R.didn.ap()))
    M["p1w"] = gw.n
    for ot in range(NOT1):
        bsel = ot % 2
        lt = ot * OT
        if ot >= 2:
            E.w(g, E.ss, M[("p1norm", ot - 2)])
            E.w(g, E.vv, M[("p1stat", ot - 2)])
            E.w(g, E.vv, M[("p1stg", ot - 2, NFT - 1)])
        gx = E.slots["gx%d" % bsel]
        gt_s = E.slots["gt%d" % bsel]
        gx.inc(g.dma_start(
            out=R.xts[bsel][:, :].rearrange("p (s c) -> p s c", s=4),
            in_=R.dxw.ap()[lt:lt + 512, :].rearrange("(s p) c -> p s c", s=4)))
        gt_s.inc(g.dma_start(out=R.tcs[bsel][:, :], in_=R.dtblc.ap()[:, lt:lt + 512]))
        gt_s.inc(g.dma_start(out=R.tss[bsel][:, :], in_=R.dtbls.ap()[:, lt:lt + 512]))
        M[("p1x", ot)] = (gx, gx.n)
        M[("p1tb", ot)] = (gt_s, gt_s.n)
    for k in range(NOT1):
        for hh in (NFT - 2, NFT - 1):
            m = M[("p1syn_stg", k, hh)]
            if m:
                E.w(g, m[0], m[1])
        m = M[("p1vstd", k, 3)]
        if m:
            E.w(g, m[0], m[1])
        if E.em:
            with allow_ici_all_to_all():
                E.cc.inc(g.collective_compute(
                    "AllToAll", ALU.bypass, replica_groups=[list(range(8))],
                    ins=[R.A2I2c[k].ap().opt()], outs=[R.A2O2c[k].ap().opt()]))
        else:
            E.cc.inc(_DummyInst())
        M[("a2a2k", k)] = E.cc.n
    M["a2a2"] = E.cc.n
    gkv = E.slots["skk"]
    for k in range(NOT1):
        E.w(g, E.cc, M[("a2a2k", k)])
        for s_src in range(8):
            for hl in range(2):
                gkv.inc(g.dma_start(
                    out=R.KK[:, hl * T + s_src * WIN + k * 512:
                             hl * T + s_src * WIN + k * 512 + 512],
                    in_=R.A2O2c[k].ap()[128 * s_src:128 * s_src + 128,
                                        hl * 512:(hl + 1) * 512]))
            gkv.inc(g.dma_start(
                out=R.Vt[:, s_src * WIN + k * 512: s_src * WIN + k * 512 + 512],
                in_=R.A2O2c[k].ap()[128 * s_src:128 * s_src + 128, 2048:2560]))
    M["p1kv"] = gkv.n
    if STAGE < 2:
        M["a2a"] = E.cc.n
        return
    E.w(g, E.slots["sos0"], M["p2sos0_end"])
    E.w(g, E.slots["sos1"], M["p2sos1_end"])
    if E.em:
        with allow_ici_all_to_all():
            E.cc.inc(g.collective_compute(
                "AllToAll", ALU.bypass, replica_groups=[list(range(8))],
                ins=[R.A2I.ap().opt()], outs=[R.A2O.ap().opt()]))
    else:
        E.cc.inc(_DummyInst())
    M["a2a"] = E.cc.n


def prog_tensor_a(E, R):
    t, M = E.t, E.M
    E.w(t, E.slots["gw"], M["p1w"])
    E.w(t, E.gc, M["p1memset"])
    QKb = [R.S1p[0][:, :], R.S1p[1][:, :]]
    SWp = R.S2p[1][:, :]
    Vpp = R.S2p[0][:, :]
    VB = [Vpp, SWp]
    for ot in range(NOT1):
        E.w(t, E.ss, M[("p1norm", ot)])
        for ccn in range(8):
            tp = R.TPq[ccn % 2]
            gcc = ot * 8 + ccn
            if gcc >= 2:
                ev = M[("p1tpev", gcc - 2)]
                if ev:
                    E.w(t, ev[0], ev[1])
            for sb in range(4):
                inst = t.transpose(tp[:, sb * 128:(sb + 1) * 128],
                                   R.xnb[ot % 2][:, sb * D + ccn * 128: sb * D + ccn * 128 + 128],
                                   R.idnb[:, :])
            E.tt.inc(inst)
            M[("p1tp", ot, ccn)] = E.tt.n
        E.w(t, E.vv, M[("p1trev_v", ot)])
        E.w(t, E.ss, M[("p1trev_s", ot)])

        def swap(h):
            ft = ot * NFT + h
            sp = ft % 2
            E.w(t, E.vv, M[("p1E", ot, h)])
            if ft >= 1:
                E.w(t, E.vv, M[("p1ropeA", (ft - 1) // NFT, (ft - 1) % NFT)])
            E.tt.inc(t.matmul(SWp, R.permb[:, :], R.stg[sp][:, 512:1024],
                              start=True, stop=True))
            M[("p1sw", ot, h)] = E.tt.n

        for h in range(NFT):
            ft = ot * NFT + h
            if ft >= 2:
                E.w(t, E.vv, M[("p1stg", (ft - 2) // NFT, (ft - 2) % NFT)])
            for ccn in range(8):
                t.matmul(QKb[h % 2], R.wqk[:, ccn * 2048 + h * 128: ccn * 2048 + h * 128 + 128],
                         R.xnT[:, ccn * OT: ccn * OT + 512], start=(ccn == 0), stop=False)
            E.tt.inc(t.matmul(QKb[h % 2], R.bqk[:, h * 128:(h + 1) * 128], R.ones[:, 0:512],
                              start=False, stop=True))
            M[("p1qk", ot, h)] = E.tt.n
            if h >= 1:
                swap(h - 1)
        swap(NFT - 1)
        E.w(t, E.vv, M[("p1ropeA", ot, NFT - 1)])
        for sb in range(4):
            for fh in range(2):
                ch = sb * 2 + fh
                if ot * 8 + ch >= 2:
                    ev = M[("p1vch", ot * 8 + ch - 2)]
                    if ev:
                        E.w(t, ev[0], ev[1])
                for ccn in range(8):
                    t.matmul(VB[ch % 2],
                             R.xnT[:, ccn * OT + sb * 128: ccn * OT + sb * 128 + 128],
                             R.wv[:, ccn * 1024 + fh * 512: ccn * 1024 + fh * 512 + 512],
                             start=(ccn == 0), stop=False)
                inst = t.matmul(VB[ch % 2], R.ones[:, 0:128], R.bv[:, fh * 512:(fh + 1) * 512],
                                start=False, stop=True)
                E.tt.inc(inst)
                M[("p1vmm", ot, ch)] = E.tt.n
    if STAGE < 2:
        return
    E.w(t, E.slots["skk"], M["p1kv"])
    E.w(t, E.vv, M["p1vv_end"])
    E.w(t, E.ss, M["p1ss_end"])
    ACCap = [R.TPq[0][:, :].bitcast(F32), R.TPq[1][:, :].bitcast(F32)]
    u = 0
    for bi in range(B * NI):
        b, I = bi // NI, bi % NI
        buf = bi % 2
        qtm = M[("p2qt", bi)]
        if qtm:
            E.w(t, qtm[0], qtm[1])
        for h in range(2):
            hb = h * T + b * N
            for J in range(NJ):
                p = u % 3
                if u >= 3:
                    E.w(t, E.vv, M[("p2w", u - 3)])
                t.matmul(R.S1p[p][:, :], R.KK[0:64, hb + J * 128: hb + J * 128 + 128],
                         R.Qt[buf][0:64, h * 512:(h + 1) * 512],
                         start=True, stop=True, tile_position=(0, 0))
                E.tt.inc(t.matmul(R.S2p[p][:, :], R.KK[64:128, hb + J * 128: hb + J * 128 + 128],
                                  R.Qt[buf][64:128, h * 512:(h + 1) * 512],
                                  start=True, stop=True, tile_position=(64, 0)))
                M[("p2s", u)] = E.tt.n
                if J >= 3:
                    _mm3(E, R, ACCap, b, h, J - 3, u - 3, bi)
                u += 1
            _mm3(E, R, ACCap, b, h, NJ - 3, u - 3, bi)
            _mm3(E, R, ACCap, b, h, NJ - 2, u - 2, bi)
            _mm3(E, R, ACCap, b, h, NJ - 1, u - 1, bi)
        M[("p2tte", bi)] = E.tt.n
    M["p2tt_end"] = E.tt.n


def _mm3(E, R, ACCap, b, h, J, u, bi):
    t, M = E.t, E.M
    if J == 0 and bi >= 1:
        E.w(t, E.vv, M[("p2evac", bi - 1)])
    E.w(t, E.vv, M[("p2w", u)])
    vcol = (b * 64 + J) * 128 + h * 64
    if h == 0:
        inst = t.matmul(ACCap[0][0:64, :], R.Vt[:, vcol:vcol + 64], R.WT[u % 4][:, :],
                        start=(J == 0), stop=(J == NJ - 1))
    else:
        inst = t.matmul(ACCap[1][64:128, :], R.Vt[:, vcol:vcol + 64], R.WT[u % 4][:, :],
                        start=(J == 0), stop=(J == NJ - 1), tile_position=(0, 64))
    if J == NJ - 1:
        E.tt.inc(inst)


def prog_vector_a(E, R):
    v, M = E.v, E.M
    QKb = [R.S1p[0][:, :], R.S1p[1][:, :]]
    SWp = R.S2p[1][:, :]
    Vpp = R.S2p[0][:, :]

    def v_stats(ot):
        xm = M[("p1x", ot)]
        if xm:
            E.w(v, xm[0], xm[1])
        for sb in range(4):
            v.reduce_sum(R.sums[:, ot * 4 + sb: ot * 4 + sb + 1],
                         R.xts[ot % 2][:, sb * D:(sb + 1) * D], axis=AXX)
        M[("p1stat", ot)] = E.vv.n

    def v_var(ot):
        lb = 24 * (ot % 2)
        E.w(v, E.ss, M[("p1sq", ot)])
        c4 = slice(ot * 4, ot * 4 + 4)
        v.tensor_scalar(R.lnt[:, lb + 0:lb + 4], R.sums[:, c4], 1.0 / D, None, ALU.mult)
        v.tensor_scalar(R.lnt[:, lb + 4:lb + 8], R.sumsq[:, c4], 1.0 / D, None, ALU.mult)
        v.tensor_tensor(R.lnt[:, lb + 8:lb + 12], R.lnt[:, lb + 0:lb + 4],
                        R.lnt[:, lb + 0:lb + 4], ALU.mult)
        v.tensor_tensor(R.lnt[:, lb + 12:lb + 16], R.lnt[:, lb + 4:lb + 8],
                        R.lnt[:, lb + 8:lb + 12], ALU.subtract)
        M[("p1var", ot)] = E.vv.n

    def v_negmr(ot):
        lb = 24 * (ot % 2)
        E.w(v, E.ss, M[("p1rstd", ot)])
        v.tensor_scalar(R.lnt[:, lb + 20:lb + 24], R.lnt[:, lb + 0:lb + 4], -1.0, None, ALU.mult)
        v.tensor_tensor(R.lnt[:, lb + 20:lb + 24], R.lnt[:, lb + 20:lb + 24],
                        R.lnt[:, lb + 16:lb + 20], ALU.mult)
        M[("p1negmr", ot)] = E.vv.n

    for ot in range(NOT1):
        if ot == 0:
            v_stats(0)
            v_var(0)
            v_negmr(0)
        for ccn in range(0, 8, 2):
            E.w(v, E.tt, M[("p1tp", ot, ccn)])
            if ot >= 1 and ccn == 0:
                E.w(v, E.tt, M[("p1vmm", ot - 1, 7)])
            v.tensor_copy(R.xnT[:, ccn * OT: ccn * OT + 512], R.TPq[ccn % 2][:, 0:512])
            M[("p1tpev", ot * 8 + ccn)] = (E.vv, E.vv.n)
        M[("p1trev_v", ot)] = E.vv.n
        for h in range(NFT):
            ft = ot * NFT + h
            sp = ft % 2
            E.w(v, E.tt, M[("p1qk", ot, h)])
            if ft >= 1:
                E.w(v, E.ss, M[("p1exp", (ft - 1) // NFT, (ft - 1) % NFT)])
            v.tensor_scalar_min(R.es[:, :], QKb[h % 2], 0.0)
            M[("p1min", ot, h)] = E.vv.n
            E.w(v, E.ss, M[("p1exp", ot, h)])
            if ft >= 2:
                sgm = M[("p1syn_stg", (ft - 2) // NFT, (ft - 2) % NFT)]
                if sgm:
                    E.w(v, sgm[0], sgm[1])
            v.scalar_tensor_tensor(R.stg[sp][:, 512:1024], QKb[h % 2], 0.0,
                                            R.expm[:, :], ALU.max, ALU.add)
            M[("p1E", ot, h)] = E.vv.n
            E.w(v, E.tt, M[("p1sw", ot, h)])
            tbm = M[("p1tb", ot)]
            if tbm:
                E.w(v, tbm[0], tbm[1])
            v.tensor_tensor(R.asb[:, :], SWp, R.tss[ot % 2][:, :], ALU.mult)
            M[("p1ropeA", ot, h)] = E.vv.n
            v.tensor_tensor(R.bsb[:, :], R.stg[sp][:, 512:1024], R.tcs[ot % 2][:, :], ALU.mult)
            v.tensor_tensor(R.stg[sp][:, 0:512], R.asb[:, :], R.bsb[:, :], ALU.add)
            M[("p1stg", ot, h)] = E.vv.n
            if ot + 1 < NOT1:
                if h == 1:
                    v_stats(ot + 1)
                elif h == 5:
                    v_var(ot + 1)
                elif h == 9:
                    v_negmr(ot + 1)
    M["p1vv_end"] = E.vv.n
    if STAGE < 2:
        return
    ACCap = [R.TPq[0][:, :].bitcast(F32), R.TPq[1][:, :].bitcast(F32)]
    u = 0
    E.vv.lag = 3
    for bi in range(B * NI):
        for h in range(2):
            for J in range(NJ):
                p = u % 3
                E.w(v, E.ss, M[("p2r", u)])
                v.tensor_tensor(R.WT[u % 4][:, :], R.S1p[p][:, :], R.rp[p][:, :], ALU.mult)
                M[("p2w", u)] = E.vv.n
                u += 1
        E.w(v, E.tt, M[("p2tte", bi)])
        if bi >= 2:
            osm = M[("p2ost", bi - 2)]
            if osm:
                E.w(v, osm[0], osm[1])
        v.tensor_copy(R.Ost[bi % 2][0:64, :], ACCap[0][0:64, 0:512])
        v.tensor_copy(R.Ost[bi % 2][64:128, :], ACCap[1][64:128, 0:512])
        M[("p2evac", bi)] = E.vv.n
    E.vv.lag = 1


def prog_scalar_a(E, R):
    s, M = E.s, E.M
    VB = [R.S2p[0][:, :], R.S2p[1][:, :]]

    def s_sq(ot):
        xm = M[("p1x", ot)]
        if xm:
            E.w(s, xm[0], xm[1])
        for sb in range(4):
            s.activation(R.sqsc[:, :], R.xts[ot % 2][:, sb * D:(sb + 1) * D],
                         AF.Square, bias=R.zb[:, :],
                         accum_out=R.sumsq[:, ot * 4 + sb: ot * 4 + sb + 1])
        M[("p1sq", ot)] = E.ss.n

    def s_rstd(ot):
        lb = 24 * (ot % 2)
        E.w(s, E.vv, M[("p1var", ot)])
        s.activation(R.lnt[:, lb + 16:lb + 20], R.lnt[:, lb + 12:lb + 16],
                     AF.Sqrt, bias=R.epsb[:, :])
        act_raw(E, s, R.lnt[:, lb + 16:lb + 20], R.lnt[:, lb + 16:lb + 20],
                AF.Reciprocal, R.zb[:, :])
        M[("p1rstd", ot)] = E.ss.n

    def s_norm(ot):
        lb = 24 * (ot % 2)
        E.w(s, E.vv, M[("p1negmr", ot)])
        if ot >= 2:
            E.w(s, E.tt, M[("p1tp", ot - 2, 7)])
        for sb in range(4):
            s.activation(R.xnb[ot % 2][:, sb * D:(sb + 1) * D],
                         R.xts[ot % 2][:, sb * D:(sb + 1) * D], AF.Identity,
                         bias=R.lnt[:, lb + 20 + sb:lb + 21 + sb],
                         scale=R.lnt[:, lb + 16 + sb:lb + 17 + sb])
        M[("p1norm", ot)] = E.ss.n

    for ot in range(NOT1):
        if ot == 0:
            E.w(s, E.gc, M["p1memset"])
            s_sq(0)
            s_rstd(0)
            s_norm(0)
        for ccn in range(1, 8, 2):
            E.w(s, E.tt, M[("p1tp", ot, ccn)])
            if ot >= 1 and ccn == 1:
                E.w(s, E.tt, M[("p1vmm", ot - 1, 7)])
            s.copy(R.xnT[:, ccn * OT: ccn * OT + 512], R.TPq[ccn % 2][:, 0:512])
            M[("p1tpev", ot * 8 + ccn)] = (E.ss, E.ss.n)
        M[("p1trev_s", ot)] = E.ss.n
        for h in range(NFT):
            ft = ot * NFT + h
            E.w(s, E.vv, M[("p1min", ot, h)])
            if ft >= 1:
                E.w(s, E.vv, M[("p1E", (ft - 1) // NFT, (ft - 1) % NFT)])
            s.activation(R.expm[:, :], R.es[:, :], AF.Exp, bias=R.zb[:, :])
            M[("p1exp", ot, h)] = E.ss.n
            if ot + 1 < NOT1:
                if h == 3:
                    s_sq(ot + 1)
                elif h == 7:
                    s_rstd(ot + 1)
                elif h == 11:
                    s_norm(ot + 1)
        for sb in range(4):
            for fh in range(2):
                ch = sb * 2 + fh
                E.w(s, E.tt, M[("p1vmm", ot, ch)])
                if sb >= 1 and fh == 0:
                    vdm = M[("p1vstd", ot, sb - 1)]
                    if vdm:
                        E.w(s, vdm[0], vdm[1])
                if ot >= 1 and sb == 0 and fh == 0:
                    vdm = M[("p1vstd", ot - 1, 3)]
                    if vdm:
                        E.w(s, vdm[0], vdm[1])
                s.copy(R.Vst[:, fh * 512:(fh + 1) * 512], VB[ch % 2])
                M[("p1vch", ot * 8 + ch)] = (E.ss, E.ss.n)
                if fh == 1:
                    M[("p1vst", ot, sb)] = E.ss.n
    M["p1ss_end"] = E.ss.n
    if STAGE < 2:
        return
    u = 0
    E.ss.lag = 3
    for bi in range(B * NI):
        for h in range(2):
            for J in range(NJ):
                p = u % 3
                E.w(s, E.tt, M[("p2s", u)])
                if u >= 3:
                    E.w(s, E.vv, M[("p2w", u - 3)])
                act_raw(E, s, R.rp[p][:, :], R.S2p[p][:, :], AF.Reciprocal, R.zb[:, :])
                M[("p2r", u)] = E.ss.n
                u += 1
    E.ss.lag = 1



def prog_gpsimd_b(E, R):
    g, M = E.g, E.M
    E.w(g, E.cc, M["a2a"])
    g.memset(R.zb3[:, :], 0.0)
    g.memset(R.epsb3[:, :], EPS)
    g.memset(R.ones3[:, :], 1.0)
    M["p3memset"] = E.gc.n
    gw3 = E.slots["gw3"]
    for src, dst in ((R.dwout, R.wouts), (R.dwr, R.wrs), (R.dwk, R.wks), (R.dwv2, R.wv2s)):
        gw3.inc(g.dma_start(out=dst[:, :].rearrange("p (c f) -> p c f", c=8),
                            in_=src.ap().rearrange("(c p) f -> p c f", c=8)))
    gw3.inc(g.dma_start(out=R.bouts[:, :], in_=R.dbout.ap()))
    gw3.inc(g.dma_start(out=R.idn3[:, :], in_=R.didn.ap()))
    gw3.inc(g.dma_start(out=R.hmm[:, :], in_=R.dhm.ap()))
    gw3.inc(g.dma_start(out=R.murs[:, :], in_=R.dmur.ap()))
    gw3.inc(g.dma_start(out=R.muks[:, :], in_=R.dmuk.ap()))
    M["p3w"] = gw3.n
    if P3SUB < 2:
        return
    E.w(g, E.slots["shgi"], M["p3hgi"])
    if E.em:
        E.cc.inc(g.collective_compute(
            "AllGather", ALU.bypass, replica_groups=[list(range(8))],
            ins=[R.HGI.ap().opt()], outs=[R.HGO.ap().opt()]))
    else:
        E.cc.inc(_DummyInst())
    M["hag"] = E.cc.n
    E.w(g, E.cc, M["hag"])
    E.slots["ghg"].inc(g.dma_start(out=R.hgos[:, :], in_=R.HGO.ap()))
    M["p3hgo"] = E.slots["ghg"].n


def prog_sync_b(E, R):
    sy, M = E.sy, E.M
    E.w(sy, E.cc, M["a2a"])
    for it in range(NT3):
        buf = it % 2
        if it >= 2:
            E.w(sy, E.tt, M[("p3op", it - 2)])
        sot = E.slots["sot%d" % buf]
        sot.inc(sy.dma_start(
            out=R.OTt[buf][:, :].rearrange("p (c t) -> p c t", c=8),
            in_=R.A2O.ap().rearrange("(c p) t -> p c t", c=8)[:, :, it * 128:(it + 1) * 128]))
        M[("p3ot", it)] = (sot, sot.n)
        if it >= 2:
            E.w(sy, E.vv, M[("p3xr1", it - 2)])
        sxw = E.slots["sxw%d" % buf]
        sxw.inc(sy.dma_start(out=R.xw[buf][:, :], in_=R.dxw.ap()[it * 128:(it + 1) * 128, :]))
        M[("p3xw", it)] = (sxw, sxw.n)
    if P3SUB < 2:
        E.w(sy, E.vv, M[("p3xn3", NT3 - 1)])
        for half in range(2):
            ev = M[("p3t3ev", (NT3 - 1) * 2 + half)]
            if ev:
                E.w(sy, ev[0], ev[1])
        return
    E.w(sy, E.vv, M[("p3xn3", NT3 - 1)])
    E.slots["shgi"].inc(sy.dma_start(out=R.HGI.ap(), in_=R.xn3b[127:128, :]))
    M["p3hgi"] = E.slots["shgi"].n
    if P3SUB < 5:
        return
    for it in range(NT3):
        buf = it % 2
        if it < 2:
            E.w(sy, E.vv, M[("p3xr1", NT3 - 2 + it)])
        else:
            E.w(sy, E.vv, M[("p3y", it - 2)])
        sxw = E.slots["sxw%d" % buf]
        sxw.inc(sy.dma_start(out=R.xw[buf][:, :], in_=R.dxw.ap()[it * 128:(it + 1) * 128, :]))
        M[("p3xw2", it)] = (sxw, sxw.n)
        E.w(sy, E.vv, M[("p3y", it)])
        syt = E.slots["sy%d" % (it % 2)]
        syt.inc(sy.dma_start(out=R.dy.ap()[it * 128:(it + 1) * 128, :], in_=R.ysb[it % 2][:, :]))
        M[("p3yst", it)] = (syt, syt.n)
    if E.em:
        sy.wait_ge(E.slots["sy0"].sem, E.slots["sy0"].n)
        sy.wait_ge(E.slots["sy1"].sem, E.slots["sy1"].n)


def prog_tensor_b(E, R):
    t, M = E.t, E.M
    E.w(t, E.slots["gw3"], M["p3w"])
    E.w(t, E.gc, M["p3memset"])
    for it in range(NT3):
        buf = it % 2
        if not (P3OPS & 1):
            break
        otm = M[("p3ot", it)]
        if otm:
            E.w(t, otm[0], otm[1])
        if it >= 2:
            E.w(t, E.ss, M[("p3pev", it - 2)])
            E.w(t, E.vv, M[("p3ps", it - 2)])
        for fh in range(2):
            for ccn in range(8):
                t.matmul(R.Pp2[it % 2][:, fh * 512:(fh + 1) * 512],
                         R.OTt[buf][:, ccn * 128:(ccn + 1) * 128],
                         R.wouts[:, ccn * D + fh * 512: ccn * D + fh * 512 + 512],
                         start=(ccn == 0), stop=False)
            inst = t.matmul(R.Pp2[it % 2][:, fh * 512:(fh + 1) * 512], R.ones3[:, 0:128],
                            R.bouts[:, fh * 512:(fh + 1) * 512], start=False, stop=True)
        E.tt.inc(inst)
        M[("p3op", it)] = E.tt.n
        if not (P3OPS & 32):
            continue
        E.w(t, E.vv, M[("p3xn3", it)])
        for half in range(2):
            tp = R.TP3[half]
            ghalf = it * 2 + half
            if ghalf >= 2:
                ev = M[("p3t3ev", ghalf - 2)]
                if ev:
                    E.w(t, ev[0], ev[1])
            for q in range(4):
                ccn = half * 4 + q
                inst = t.transpose(tp[:, q * 128:(q + 1) * 128],
                                   R.xn3b[:, ccn * 128:(ccn + 1) * 128], R.idn3[:, :])
            E.tt.inc(inst)
            M[("p3t3", it, half)] = E.tt.n
    if P3SUB < 2:
        return
    E.w(t, E.slots["ghg"], M["p3hgo"])
    for fh in range(2):
        E.tt.inc(t.matmul(R.HP[0:1, :], R.hmm[:, :], R.hgos[:, fh * 512:(fh + 1) * 512],
                          start=True, stop=True))
        M[("p3hp", fh)] = E.tt.n
        E.w(t, E.ss, M[("p3hev", fh)])
    E.w(t, E.ss, M[("p3hev", 1)])
    for ccn in range(8):
        inst = t.transpose(R.HS[:, 2 * ccn:2 * ccn + 1], R.halo[0:1, ccn * 128:(ccn + 1) * 128],
                           R.idn3[0:1, 0:1])
    E.tt.inc(inst)
    M["p3hst"] = E.tt.n
    if P3SUB < 4:
        return
    for ts4 in range(4):
        for dt_ in range(8):
            u = ts4 * 8 + dt_
            p = u % 2
            if u >= 2:
                E.w(t, E.ss, M[("p3kact", u - 2)])
            E.w(t, E.vv, M[("p3mixk", ts4)])
            for ccn in range(8):
                inst = t.matmul(R.Kp[p][:, :],
                                R.wks[:, ccn * D + dt_ * 128: ccn * D + dt_ * 128 + 128],
                                R.xmkT[:, ccn * 512:(ccn + 1) * 512],
                                start=(ccn == 0), stop=(ccn == 7))
            E.tt.inc(inst)
            M[("p3k", u)] = E.tt.n
        if P3SUB < 5:
            continue
        for q in range(4):
            it = ts4 * 4 + q
            E.w(t, E.vv, M[("p3mixr", ts4)])
            if it >= 2:
                E.w(t, E.ss, M[("p3sig", it - 2)])
                E.w(t, E.vv, M[("p3ff", it - 2)])
            for fh in range(2):
                for ccn in range(8):
                    inst = t.matmul(R.Pp2[it % 2][:, fh * 512:(fh + 1) * 512],
                                    R.xmrT[:, ccn * 512 + q * 128: ccn * 512 + q * 128 + 128],
                                    R.wrs[:, ccn * D + fh * 512: ccn * D + fh * 512 + 512],
                                    start=(ccn == 0), stop=(ccn == 7))
            E.tt.inc(inst)
            M[("p3r", it)] = E.tt.n
            E.w(t, E.ss, M[("p3sig", it)])
            E.w(t, E.ss, M[("p3kact", ts4 * 8 + 7)])
            for fh in range(2):
                for ccn in range(8):
                    inst = t.matmul(R.Pp2[it % 2][:, fh * 512:(fh + 1) * 512],
                                    R.kactS[:, ccn * 512 + q * 128: ccn * 512 + q * 128 + 128],
                                    R.wv2s[:, ccn * D + fh * 512: ccn * D + fh * 512 + 512],
                                    start=(ccn == 0), stop=(ccn == 7))
            E.tt.inc(inst)
            M[("p3vo", it)] = E.tt.n


def prog_vector_b(E, R):
    v, M = E.v, E.M
    E.w(v, E.slots["gw3"], M["p3w"])
    v.tensor_scalar(R.omurs[:, :], R.murs[:, :], -1.0, 1.0, ALU.mult, ALU.add)
    v.tensor_scalar(R.omuks[:, :], R.muks[:, :], -1.0, 1.0, ALU.mult, ALU.add)
    for it in range(NT3):
        if not (P3OPS & 2):
            break
        E.w(v, E.tt, M[("p3op", it)])
        if VB_MODE == 0:
            v.reduce_sum(R.l3[:, 0:1], R.Pp2[it % 2][:, :], axis=AXX)
        elif VB_MODE == 1:
            v.reduce_sum(R.l3[:, 0:1], R.Pp2[it % 2][:, 0:512], axis=AXX)
        elif VB_MODE == 2:
            v.tensor_copy(R.l3[:, 0:8], R.Pp2[it % 2][:, 0:8])
        else:
            v.tensor_copy(R.l3[:, 0:8], R.xw[0][:, 0:8])
        M[("p3ps", it)] = E.vv.n
        if not (P3OPS & 8):
            continue
        E.w(v, E.ss, M[("p3pev", it)])
        v.tensor_scalar(R.l3[:, 2:3], R.l3[:, 0:1], 1.0 / D, None, ALU.mult)
        v.tensor_scalar(R.l3[:, 3:4], R.l3[:, 1:2], 1.0 / D, None, ALU.mult)
        v.tensor_tensor(R.l3[:, 4:5], R.l3[:, 2:3], R.l3[:, 2:3], ALU.mult)
        v.tensor_tensor(R.l3[:, 5:6], R.l3[:, 3:4], R.l3[:, 4:5], ALU.subtract)
        M[("p3var2", it)] = E.vv.n
        E.w(v, E.ss, M[("p3rs2", it)])
        v.tensor_scalar(R.ln2o[:, it * D:(it + 1) * D], R.Psb[:, :],
                                 R.l3[:, 2:3], R.l3[:, 6:7], ALU.subtract, ALU.mult)
        M[("p3ln2", it)] = E.vv.n
        if not (P3OPS & 16):
            continue
        xwm = M[("p3xw", it)]
        if xwm:
            E.w(v, xwm[0], xwm[1])
        v.tensor_tensor(R.xr1[:, :], R.xw[it % 2][:, :],
                                 R.ln2o[:, it * D:(it + 1) * D], ALU.add)
        M[("p3xr1", it)] = E.vv.n
        v.reduce_sum(R.l3[:, 8:9], R.xr1[:, :], axis=AXX)
        E.w(v, E.ss, M[("p3sq3", it)])
        v.tensor_scalar(R.l3[:, 10:11], R.l3[:, 8:9], 1.0 / D, None, ALU.mult)
        v.tensor_scalar(R.l3[:, 11:12], R.l3[:, 9:10], 1.0 / D, None, ALU.mult)
        v.tensor_tensor(R.l3[:, 12:13], R.l3[:, 10:11], R.l3[:, 10:11], ALU.mult)
        v.tensor_tensor(R.l3[:, 13:14], R.l3[:, 11:12], R.l3[:, 12:13], ALU.subtract)
        M[("p3var3", it)] = E.vv.n
        E.w(v, E.ss, M[("p3rs3", it)])
        if it >= 1:
            E.w(v, E.tt, M[("p3t3", it - 1, 1)])
        v.tensor_scalar(R.xn3b[:, :], R.xr1[:, :],
                                 R.l3[:, 10:11], R.l3[:, 14:15], ALU.subtract, ALU.mult)
        M[("p3xn3", it)] = E.vv.n
        for half in range(2):
            E.w(v, E.tt, M[("p3t3", it, half)])
            dst = R.xn3T[:, :].rearrange("p (c w) -> p c w", c=8)[:, half * 4:half * 4 + 4,
                                                                 it * 128:it * 128 + 128]
            v.tensor_copy(dst, R.TP3[half][:, :].rearrange("p (c w) -> p c w", c=4))
            M[("p3t3ev", it * 2 + half)] = (E.vv, E.vv.n)
    if P3SUB < 2:
        return
    E.w(v, E.tt, M["p3hst"])
    v.tensor_copy(R.haloT[:, :], R.HS[:, 0:16:2])
    M["p3halot"] = E.vv.n
    if P3SUB < 3:
        return
    for ts4 in range(4):
        t0 = ts4 * 512
        for mu, omu, dst, mk in ((R.murs, R.omurs, R.xmrT, "p3mixr"), (R.muks, R.omuks, R.xmkT, "p3mixk")):
            if ts4 == 0:
                E.w(v, E.vv, M["p3halot"])
            if ts4 >= 1:
                E.w(v, E.tt, M[("p3r", ts4 * 4 - 1)] if mk == "p3mixr" else M[("p3k", (ts4 - 1) * 8 + 7)])
            for ccn in range(8):
                cbase = ccn * WIN
                if t0 == 0:
                    v.tensor_scalar(R.mixs[:, 1:512], R.xn3T[:, cbase: cbase + 511],
                                    omu[:, ccn:ccn + 1], None, ALU.mult)
                    v.tensor_scalar(R.mixs[:, 0:1], R.haloT[:, ccn:ccn + 1],
                                    omu[:, ccn:ccn + 1], None, ALU.mult)
                else:
                    v.tensor_scalar(R.mixs[:, 0:512], R.xn3T[:, cbase + t0 - 1: cbase + t0 + 511],
                                    omu[:, ccn:ccn + 1], None, ALU.mult)
                inst = v.scalar_tensor_tensor(dst[:, ccn * 512:(ccn + 1) * 512],
                                              R.xn3T[:, cbase + t0: cbase + t0 + 512],
                                              mu[:, ccn:ccn + 1], R.mixs[:, 0:512],
                                              ALU.mult, ALU.add)
            M[(mk, ts4)] = E.vv.n
        if P3SUB < 4:
            continue
        for dt_ in range(8):
            u = ts4 * 8 + dt_
            p = u % 2
            E.w(v, E.tt, M[("p3k", u)])
            if u >= 1:
                E.w(v, E.ss, M[("p3kact", u - 1)])
            v.tensor_scalar_max(R.kpre[:, :], R.Kp[p][:, :], 0.0)
            M[("p3relu", u)] = E.vv.n
        if P3SUB < 5:
            continue
        for q in range(4):
            it = ts4 * 4 + q
            E.w(v, E.tt, M[("p3vo", it)])
            if it >= 1:
                E.w(v, E.ss, M[("p3sq4", it - 1)])
            v.tensor_tensor(R.ffo[:, :], R.Pp2[it % 2][:, :], R.rsig[:, :], ALU.mult)
            M[("p3ff", it)] = E.vv.n
            v.reduce_sum(R.l3[:, 16:17], R.ffo[:, :], axis=AXX)
            E.w(v, E.ss, M[("p3sq4", it)])
            v.tensor_scalar(R.l3[:, 18:19], R.l3[:, 16:17], 1.0 / D, None, ALU.mult)
            v.tensor_scalar(R.l3[:, 19:20], R.l3[:, 17:18], 1.0 / D, None, ALU.mult)
            v.tensor_tensor(R.l3[:, 20:21], R.l3[:, 18:19], R.l3[:, 18:19], ALU.mult)
            v.tensor_tensor(R.l3[:, 21:22], R.l3[:, 19:20], R.l3[:, 20:21], ALU.subtract)
            M[("p3var4", it)] = E.vv.n
            E.w(v, E.ss, M[("p3rs4", it)])
            v.tensor_scalar(R.ln4o[:, :], R.ffo[:, :], R.l3[:, 18:19], R.l3[:, 22:23],
                            ALU.subtract, ALU.mult)
            xwm = M[("p3xw2", it)]
            if xwm:
                E.w(v, xwm[0], xwm[1])
            if it >= 2:
                ym = M[("p3yst", it - 2)]
                if ym:
                    E.w(v, ym[0], ym[1])
            v.tensor_tensor(R.ysb[it % 2][:, :], R.xw[it % 2][:, :],
                            R.ln2o[:, it * D:(it + 1) * D], ALU.add)
            v.tensor_tensor(R.ysb[it % 2][:, :], R.ysb[it % 2][:, :], R.ln4o[:, :], ALU.add)
            M[("p3y", it)] = E.vv.n


def prog_scalar_b(E, R):
    s, M = E.s, E.M
    for it in range(NT3):
        if not (P3OPS & 4):
            break
        E.w(s, E.tt, M[("p3op", it)])
        E.w(s, E.vv, M[("p3ps", it)])
        if it >= 1:
            E.w(s, E.vv, M[("p3xn3", it - 1)])
        s.copy(R.Psb[:, :], R.Pp2[it % 2][:, :])
        s.activation(R.sq3[:, :], R.Pp2[it % 2][:, :], AF.Square, bias=R.zb3[:, :],
                              accum_out=R.l3[:, 1:2])
        M[("p3pev", it)] = E.ss.n
        if not (P3OPS & 8):
            continue
        E.w(s, E.vv, M[("p3var2", it)])
        s.activation(R.l3[:, 6:7], R.l3[:, 5:6], AF.Sqrt, bias=R.epsb3[:, :])
        act_raw(E, s, R.l3[:, 6:7], R.l3[:, 6:7], AF.Reciprocal, R.zb3[:, :])
        M[("p3rs2", it)] = E.ss.n
        E.w(s, E.vv, M[("p3xr1", it)])
        s.activation(R.sq3[:, :], R.xr1[:, :], AF.Square, bias=R.zb3[:, :],
                              accum_out=R.l3[:, 9:10])
        M[("p3sq3", it)] = E.ss.n
        E.w(s, E.vv, M[("p3var3", it)])
        s.activation(R.l3[:, 14:15], R.l3[:, 13:14], AF.Sqrt, bias=R.epsb3[:, :])
        act_raw(E, s, R.l3[:, 14:15], R.l3[:, 14:15], AF.Reciprocal, R.zb3[:, :])
        M[("p3rs3", it)] = E.ss.n

    if P3SUB < 2:
        return
    for fh in range(2):
        E.w(s, E.tt, M[("p3hp", fh)])
        s.copy(R.halo[0:1, fh * 512:(fh + 1) * 512], R.HP[0:1, :])
        M[("p3hev", fh)] = E.ss.n
    if P3SUB < 4:
        return
    for ts4 in range(4):
        for dt_ in range(8):
            u = ts4 * 8 + dt_
            E.w(s, E.vv, M[("p3relu", u)])
            if dt_ == 0 and ts4 >= 1:
                E.w(s, E.tt, M[("p3vo", ts4 * 4 - 1)])
            s.activation(R.kactS[:, dt_ * 512:(dt_ + 1) * 512], R.kpre[:, :],
                                  AF.Square, bias=R.zb3[:, :])
            M[("p3kact", u)] = E.ss.n
        if P3SUB < 5:
            continue
        for q in range(4):
            it = ts4 * 4 + q
            E.w(s, E.tt, M[("p3r", it)])
            if it >= 1:
                E.w(s, E.vv, M[("p3ff", it - 1)])
            s.activation(R.rsig[:, :], R.Pp2[it % 2][:, :], AF.Sigmoid, bias=R.zb3[:, :])
            M[("p3sig", it)] = E.ss.n
            E.w(s, E.vv, M[("p3ff", it)])
            s.activation(R.sq3[:, :], R.ffo[:, :], AF.Square, bias=R.zb3[:, :],
                                  accum_out=R.l3[:, 17:18])
            M[("p3sq4", it)] = E.ss.n
            E.w(s, E.vv, M[("p3var4", it)])
            s.activation(R.l3[:, 22:23], R.l3[:, 21:22], AF.Sqrt, bias=R.epsb3[:, :])
            act_raw(E, s, R.l3[:, 22:23], R.l3[:, 22:23], AF.Reciprocal, R.zb3[:, :])
            M[("p3rs4", it)] = E.ss.n



class _R:
    pass


def build_nc():
    nc = bass.Bass("TRN2", target_bir_lowering=False)
    R = _R()
    R.dxw = nc.dram_tensor("x_win", [WIN, D], F32, kind="ExternalInput")
    R.dwqk = nc.dram_tensor("wqk", [D, 2048], F32, kind="ExternalInput")
    R.dbqk = nc.dram_tensor("bqk", [1, 2048], F32, kind="ExternalInput")
    R.dwv = nc.dram_tensor("wv", [D, D], F32, kind="ExternalInput")
    R.dbv = nc.dram_tensor("bv", [1, D], F32, kind="ExternalInput")
    R.dwout = nc.dram_tensor("wout", [D, D], F32, kind="ExternalInput")
    R.dbout = nc.dram_tensor("bout", [1, D], F32, kind="ExternalInput")
    R.dwr = nc.dram_tensor("wr_t", [D, D], F32, kind="ExternalInput")
    R.dwk = nc.dram_tensor("wk_t", [D, D], F32, kind="ExternalInput")
    R.dwv2 = nc.dram_tensor("wv2_t", [D, D], F32, kind="ExternalInput")
    R.dmur = nc.dram_tensor("mu_r", [128, 8], F32, kind="ExternalInput")
    R.dmuk = nc.dram_tensor("mu_k", [128, 8], F32, kind="ExternalInput")
    R.dtblc = nc.dram_tensor("tblc", [128, WIN], F32, kind="ExternalInput")
    R.dtbls = nc.dram_tensor("tbls", [128, WIN], F32, kind="ExternalInput")
    R.dperm = nc.dram_tensor("permm", [128, 128], F32, kind="ExternalInput")
    R.didn = nc.dram_tensor("idn", [128, 128], F32, kind="ExternalInput")
    R.dhm = nc.dram_tensor("hmask", [8, 1], F32, kind="ExternalInput")
    R.dy = nc.dram_tensor("y", [WIN, D], F32, kind="ExternalOutput")
    R.A2I2c = [nc.dram_tensor(f"a2i2_{k}", [1024, 2560], BF16) for k in range(4)]
    R.A2O2c = [nc.dram_tensor(f"a2o2_{k}", [1024, 2560], BF16) for k in range(4)]
    R.A2I = nc.dram_tensor("a2i", [1024, WIN], BF16)
    R.A2O = nc.dram_tensor("a2o", [1024, WIN], BF16)
    R.HGI = nc.dram_tensor("hgi", [1, D], BF16)
    R.HGO = nc.dram_tensor("hgo", [8, D], BF16)

    marks = Marks()
    E = Env(nc, marks)

    with ExitStack() as top:
        E.sems(nc, top)
        with ExitStack() as pa:
            e = pa.enter_context
            R.KK = e(nc.sbuf_tensor("KK", [128, 2 * T], BF16))
            R.Vt = e(nc.sbuf_tensor("Vt", [128, T], BF16))
            R.wqk = e(nc.sbuf_tensor("wqk_s", [128, 8 * 2048], BF16))
            R.wv = e(nc.sbuf_tensor("wv_s", [128, 8 * 1024], BF16))
            R.bqk = e(nc.sbuf_tensor("bqk_s", [1, 2048], BF16))
            R.bv = e(nc.sbuf_tensor("bv_s", [1, D], BF16))
            R.Vst = e(nc.sbuf_tensor("Vst", [128, 1024], BF16))
            R.ones = e(nc.sbuf_tensor("ones_s", [1, 512], BF16))
            R.permb = e(nc.sbuf_tensor("permb", [128, 128], BF16))
            R.idnb = e(nc.sbuf_tensor("idnb", [128, 128], BF16))
            R.zb = e(nc.sbuf_tensor("zb", [128, 1], F32))
            R.epsb = e(nc.sbuf_tensor("epsb", [128, 1], F32))
            R.xts = [e(nc.sbuf_tensor("xts0", [128, 4 * D], BF16)),
                     ColView(R.KK, 0, 4 * D)]
            R.sqsc = e(nc.sbuf_tensor("sqsc", [128, D], F32))
            R.xnb = [e(nc.sbuf_tensor("xnb0", [128, 4 * D], BF16)),
                     ColView(R.KK, 4 * D, 4 * D)]
            R.xnT = e(nc.sbuf_tensor("xnT", [128, 8 * OT], BF16))
            R.sums = e(nc.sbuf_tensor("sums", [128, 4 * NOT1], F32))
            R.sumsq = e(nc.sbuf_tensor("sumsq", [128, 4 * NOT1], F32))
            R.lnt = e(nc.sbuf_tensor("lnt", [128, 48], F32))
            R.es = ColView(R.KK, 9 * D + 512, 512)
            R.expm = e(nc.sbuf_tensor("expm", [128, 512], BF16))
            R.stg = [e(nc.sbuf_tensor(f"stg{i}", [128, 1024], BF16)) for i in range(2)]
            R.asb = e(nc.sbuf_tensor("asb", [128, 512], BF16))
            R.bsb = e(nc.sbuf_tensor("bsb", [128, 512], BF16))
            R.tcs = [e(nc.sbuf_tensor("tcs0", [128, 512], BF16)),
                     ColView(R.KK, 8 * D, 512)]
            R.tss = [e(nc.sbuf_tensor("tss0", [128, 512], BF16)),
                     ColView(R.KK, 8 * D + 512, 512)]
            R.Qt = [e(nc.sbuf_tensor(f"Qt{i}", [128, 1024], BF16)) for i in range(2)]
            R.rp = [e(nc.sbuf_tensor(f"rp{i}", [128, 512], F32)) for i in range(3)]
            R.WT = [e(nc.sbuf_tensor(f"WT{i}", [128, 512], BF16)) for i in range(4)]
            R.Ost = [e(nc.sbuf_tensor(f"Ost{i}", [128, 512], BF16)) for i in range(2)]
            R.S1p = [e(nc.psum_tensor(f"S1p{i}", [128, 512], F32)) for i in range(3)]
            R.S2p = [e(nc.psum_tensor(f"S2p{i}", [128, 512], F32)) for i in range(3)]
            R.TPq = [e(nc.psum_tensor(f"TPq{i}", [128, 1024], BF16)) for i in range(2)]

            E.em = False
            E.reset()
            E.use_engines(t=_DummyEng(), v=_DummyEng(), s=_DummyEng(), g=_DummyEng(), sy=_DummyEng())
            for fn in (prog_sync_a, prog_gpsimd_a, prog_tensor_a, prog_vector_a, prog_scalar_a):
                fn(E, R)
            snapA = [c.n for c in E.ctrs]

            E._snapA = snapA

            Marks.strict = True
            E.em = True
            E.reset()
            with nc.Block() as blk:
                @blk.sync
                def _(sy):
                    E.use_engines(sy=sy)
                    prog_sync_a(E, R)

                @blk.gpsimd
                def _(g):
                    E.use_engines(g=g)
                    prog_gpsimd_a(E, R)

                @blk.tensor
                def _(t):
                    E.use_engines(t=t)
                    prog_tensor_a(E, R)

                @blk.vector
                def _(v):
                    E.use_engines(v=v)
                    prog_vector_a(E, R)

                @blk.scalar
                def _(s):
                    E.use_engines(s=s)
                    prog_scalar_a(E, R)
            assert [c.n for c in E.ctrs] == snapA, "pass A count mismatch"

        if STAGE < 3:
            return nc
        with ExitStack() as pb:
            e = pb.enter_context
            R.OTt = [e(nc.sbuf_tensor(f"OTt{i}", [128, 8 * 128], BF16)) for i in range(2)]
            R.ln2o = e(nc.sbuf_tensor("ln2o", [128, NT3 * D], BF16))
            R.xn3T = e(nc.sbuf_tensor("xn3T", [128, 8 * WIN], BF16))
            R.wouts = e(nc.sbuf_tensor("wouts", [128, 8 * D], BF16))
            R.bouts = e(nc.sbuf_tensor("bouts", [1, D], BF16))
            R.wrs = e(nc.sbuf_tensor("wrs", [128, 8 * D], BF16))
            R.wks = e(nc.sbuf_tensor("wks", [128, 8 * D], BF16))
            R.wv2s = e(nc.sbuf_tensor("wv2s", [128, 8 * D], BF16))
            R.murs = e(nc.sbuf_tensor("murs", [128, 8], F32))
            R.muks = e(nc.sbuf_tensor("muks", [128, 8], F32))
            R.omurs = e(nc.sbuf_tensor("omurs", [128, 8], F32))
            R.omuks = e(nc.sbuf_tensor("omuks", [128, 8], F32))
            R.zb3 = e(nc.sbuf_tensor("zb3", [128, 1], F32))
            R.epsb3 = e(nc.sbuf_tensor("epsb3", [128, 1], F32))
            R.ones3 = e(nc.sbuf_tensor("ones3", [1, 512], BF16))
            R.idn3 = e(nc.sbuf_tensor("idn3", [128, 128], BF16))
            R.hmm = e(nc.sbuf_tensor("hmm", [8, 1], BF16))
            R.hgos = e(nc.sbuf_tensor("hgos", [8, D], BF16))
            R.halo = e(nc.sbuf_tensor("halo", [1, D], BF16))
            R.haloT = e(nc.sbuf_tensor("haloT", [128, 8], BF16))
            R.xw = [e(nc.sbuf_tensor(f"xw{i}", [128, D], F32)) for i in range(2)]
            R.Psb = e(nc.sbuf_tensor("Psb", [128, D], F32))
            R.xr1 = e(nc.sbuf_tensor("xr1", [128, D], F32))
            R.xn3b = e(nc.sbuf_tensor("xn3b", [128, D], BF16))
            R.sq3 = e(nc.sbuf_tensor("sq3", [128, D], F32))
            R.ln4o = e(nc.sbuf_tensor("ln4o", [128, D], F32))
            R.l3 = e(nc.sbuf_tensor("l3", [128, 24], F32))
            R.mixs = e(nc.sbuf_tensor("mixs", [128, 512], BF16))
            R.xmrT = e(nc.sbuf_tensor("xmrT", [128, 8 * 512], BF16))
            R.xmkT = e(nc.sbuf_tensor("xmkT", [128, 8 * 512], BF16))
            R.kactS = e(nc.sbuf_tensor("kactS", [128, 8 * 512], BF16))
            R.kpre = e(nc.sbuf_tensor("kpre", [128, 512], BF16))
            R.rsig = e(nc.sbuf_tensor("rsig", [128, D], BF16))
            R.ffo = e(nc.sbuf_tensor("ffo", [128, D], F32))
            R.ysb = [e(nc.sbuf_tensor(f"ysb{i}", [128, D], F32)) for i in range(2)]
            R.Pp2 = [e(nc.psum_tensor(f"Pp{i}", [128, 1024], F32)) for i in range(2)]
            R.Kp = [e(nc.psum_tensor(f"Kp{i}", [128, 512], F32)) for i in range(2)]
            R.TP3 = [e(nc.psum_tensor(f"TP3{i}", [128, 512], BF16)) for i in range(2)]
            R.HP = R.Kp[0][0:1, 0:512]
            R.HS = R.Kp[1][:, 0:8].bitcast(BF16)

            Marks.strict = False
            E.em = False
            for c, n0 in zip(E.ctrs, E._snapA):
                c.n = n0
            E.use_engines(t=_DummyEng(), v=_DummyEng(), s=_DummyEng(), g=_DummyEng(), sy=_DummyEng())
            for fn in (prog_gpsimd_b, prog_sync_b, prog_tensor_b, prog_vector_b, prog_scalar_b):
                fn(E, R)
            snapB = [c.n for c in E.ctrs]

            Marks.strict = True
            E.em = True
            for c, n0 in zip(E.ctrs, E._snapA):
                c.n = n0
            with nc.Block() as blk3:
                @blk3.gpsimd
                def _(g):
                    E.use_engines(g=g)
                    prog_gpsimd_b(E, R)

                @blk3.sync
                def _(sy):
                    E.use_engines(sy=sy)
                    prog_sync_b(E, R)

                @blk3.tensor
                def _(t):
                    E.use_engines(t=t)
                    prog_tensor_b(E, R)

                @blk3.vector
                def _(v):
                    E.use_engines(v=v)
                    prog_vector_b(E, R)

                @blk3.scalar
                def _(s):
                    E.use_engines(s=s)
                    prog_scalar_b(E, R)
            assert [c.n for c in E.ctrs] == snapB, "pass B count mismatch"

    return nc


def _host_prep(inputs):
    x = np.ascontiguousarray(np.asarray(inputs["L0_attn_x"], dtype=np.float32).reshape(T, D))
    wqkv = np.asarray(inputs["L0_attn_w_qkv"], dtype=np.float32)
    bqkv = np.asarray(inputs["L0_attn_b_qkv"], dtype=np.float32)
    wout = np.ascontiguousarray(np.asarray(inputs["L0_attn_w_out"], dtype=np.float32))
    bout = np.asarray(inputs["L0_attn_b_out"], dtype=np.float32).reshape(1, D)
    Wr = np.asarray(inputs["L0_ffn_W_r"], dtype=np.float32)
    Wk = np.asarray(inputs["L0_ffn_W_k"], dtype=np.float32)
    Wv2 = np.asarray(inputs["L0_ffn_W_v"], dtype=np.float32)
    mur = np.ascontiguousarray(np.asarray(inputs["L0_ffn_mu_r"], dtype=np.float32).reshape(8, 128).T)
    muk = np.ascontiguousarray(np.asarray(inputs["L0_ffn_mu_k"], dtype=np.float32).reshape(8, 128).T)

    perm = np.concatenate([np.arange(0, 64, 2), np.arange(1, 64, 2)])
    p32 = np.arange(32, dtype=np.float64)
    theta = 1.0 / (10000.0 ** (p32 / 32.0))
    pos = np.arange(N, dtype=np.float64)
    ang = pos[:, None] * theta[None, :]
    cosv = np.cos(ang).T
    sinv = np.sin(ang).T
    tblc = np.empty((128, N), dtype=np.float32)
    tbls = np.empty((128, N), dtype=np.float32)
    for g in range(4):
        r0 = g * 32
        tblc[r0:r0 + 32] = cosv
        if g % 2 == 0:
            tbls[r0:r0 + 32] = -sinv
        else:
            tbls[r0:r0 + 32] = sinv
    permm = np.zeros((128, 128), dtype=np.float32)
    for i in range(128):
        j = i + 32 if (i // 32) % 2 == 0 else i - 32
        permm[j, i] = 1.0
    idn = np.eye(128, dtype=np.float32)

    wr_t = np.ascontiguousarray(Wr.T)
    wk_t = np.ascontiguousarray(Wk.T)
    wv2_t = np.ascontiguousarray(Wv2.T)

    qcols = []
    for h in range(16):
        qcols.append(np.concatenate([
            wqkv[:, h * 64 + perm],
            wqkv[:, D + h * 64 + perm],
        ], axis=1))
    wqk = np.ascontiguousarray(np.concatenate(qcols, axis=1))
    bqk = np.ascontiguousarray(np.concatenate([
        np.concatenate([bqkv[h * 64 + perm], bqkv[D + h * 64 + perm]])
        for h in range(16)]).reshape(1, 2048).astype(np.float32))
    wv = np.ascontiguousarray(wqkv[:, 2 * D:3 * D])
    bv = np.ascontiguousarray(bqkv[2 * D:3 * D].reshape(1, D).astype(np.float32))
    in_maps = []
    for c in range(NCORE):
        pos0 = (c * WIN) % N
        hm = np.zeros((8, 1), dtype=np.float32)
        if c not in (0, 4):
            hm[c - 1, 0] = 1.0
        in_maps.append({
            "x_win": np.ascontiguousarray(x[c * WIN:(c + 1) * WIN]),
            "wqk": wqk, "bqk": bqk,
            "wv": wv, "bv": bv,
            "wout": wout, "bout": np.ascontiguousarray(bout),
            "wr_t": wr_t, "wk_t": wk_t, "wv2_t": wv2_t,
            "mu_r": mur, "mu_k": muk,
            "tblc": np.ascontiguousarray(tblc[:, pos0:pos0 + WIN]),
            "tbls": np.ascontiguousarray(tbls[:, pos0:pos0 + WIN]),
            "permm": permm, "idn": idn,
            "hmask": hm,
        })
    return in_maps


def kernel(**inputs):
    in_maps = _host_prep(inputs)
    nc = build_nc()
    res = run_bass_kernel_spmd(nc, in_maps, core_ids=list(range(NCORE)))
    y = np.concatenate([np.asarray(res.results[c]["y"], dtype=np.float32)
                        for c in range(NCORE)], axis=0)
    return y.reshape(B, N, D)
